# version 05 (final)
# baseline (speedup 1.0000x reference)
import numpy as np
import ml_dtypes
import concourse.bass as bass
import concourse.mybir as mybir
from concourse.bass_utils import run_bass_kernel_spmd
from concourse.bass_core_rust import IndirectOffsetOnAxis

F32 = mybir.dt.float32
BF16 = mybir.dt.bfloat16
AF = mybir.ActivationFunctionType
ALU = mybir.AluOpType
AX = mybir.AxisListType.X
I32 = mybir.dt.int32

NCORES = 8
B = 2
SEQ = 8192
D = 1024
T = 2048
HALO = 513
NEG = -1.0e30
BF = ml_dtypes.bfloat16

ENGINES = ("sync", "gpsimd", "tensor", "vector", "scalar")



def _host_consts():
    c = {}
    c["ones128"] = np.ones((128, 1), np.float32)
    c["ones1x128"] = np.ones((1, 128), np.float32)
    h2 = np.zeros((128, 2), np.float32)
    h2[0:64, 0] = 1.0
    h2[64:128, 1] = 1.0
    c["head2"] = h2
    h2e = np.zeros((2, 128), np.float32)
    h2e[0, 0:64] = 1.0
    h2e[1, 64:128] = 1.0
    c["head2exp"] = h2e
    c["ident"] = np.eye(128, dtype=np.float32)
    c["onesrow"] = np.ones((1, SEQ), np.float32)
    kp = np.zeros((4, SEQ), np.float32)
    kp[0, :] = 1.0
    c["kpre"] = kp
    qp = np.zeros((3, SEQ), np.float32)
    qp[0:2, :] = 1.0
    c["qpre"] = qp
    c["jrow"] = np.arange(SEQ, dtype=np.float32)[None, :]
    c["j16"] = np.tile(np.arange(SEQ, dtype=np.float32)[None, :], (16, 1))
    ht_ = np.arange(16) % 4
    c["ebits"] = (ht_ & 1).astype(np.float32)[:, None]

    mL = np.zeros((8, 128, 512), np.float32)
    for m in range(8):
        r = np.arange(128)[:, None]
        cc = np.arange(512)[None, :]
        diff = cc + 512 - 128 * m - r
        mL[m] = np.where((diff >= 0) & (diff <= 512), 0.0, NEG)
    c["maskL"] = np.ascontiguousarray(mL.transpose(1, 0, 2).reshape(128, 8 * 512))
    c["maskL0"] = np.ascontiguousarray(
        np.full((128, 4 * 512), NEG, np.float32))

    mC = np.zeros((5, 128, 512), np.float32)
    for rel in range(4):
        r = np.arange(128)[:, None]
        cc = np.arange(512)[None, :]
        mC[rel] = np.where(rel * 128 + r <= cc, 0.0, NEG)
    mC[4] = NEG
    c["maskC"] = np.ascontiguousarray(mC.transpose(1, 0, 2).reshape(128, 5 * 512))

    pos = np.arange(SEQ, dtype=np.float64)
    inv = 10000.0 ** (-(np.arange(0, 64, 2, dtype=np.float64) / 64))
    th = pos[None, :] * inv[:, None]
    cosA = np.cos(th).astype(np.float32)
    sinA = np.sin(th).astype(np.float32)
    c["att_t1"] = np.concatenate([cosA, cosA], 0)
    c["att_t2"] = np.concatenate([-sinA, sinA], 0)
    c["att_t1q"] = c["att_t1"] * 0.125
    c["att_t2q"] = c["att_t2"] * 0.125

    DH = 64
    WIN = 512
    half = np.arange(0, DH, 2, dtype=np.float64)
    invf = 1.0 / 10000.0 ** (half / DH)
    base = (half + 0.4 * DH) / (1.4 * DH)

    def tabs(tglob, expo):
        fr = tglob[None, :] * invf[:, None]
        sc = base[:, None] ** (expo[None, :] / 256.0)
        cosF = np.cos(fr) * sc
        sinF = np.sin(fr) * sc
        t1 = np.concatenate([cosF, cosF], 0)
        t2 = np.concatenate([-sinF, sinF], 0)
        return t1.astype(np.float32), t2.astype(np.float32)

    tmod = np.arange(SEQ, dtype=np.float64) % WIN
    tg = np.arange(SEQ, dtype=np.float64)
    c["ffq_t1"], c["ffq_t2"] = tabs(tg, tmod)
    c["ffka_t1"], c["ffka_t2"] = tabs(tg, -tmod)
    c["ffkb_t1"], c["ffkb_t2"] = tabs(tg, (WIN - tmod))
    c["ffq_t1"] = c["ffq_t1"] * 8.0
    c["ffq_t2"] = c["ffq_t2"] * 8.0
    return c


_CONSTS = None


def consts():
    global _CONSTS
    if _CONSTS is None:
        _CONSTS = _host_consts()
    return _CONSTS


def core_pos(c):
    return c // 4, (c % 4) * T


def _pp(vec, nchunk=8):
    v = np.asarray(vec, np.float32).reshape(nchunk, 128)
    return np.ascontiguousarray(v.T)


def _pp_pad(vec, nchunk, size):
    v = np.zeros(nchunk * 128, np.float32)
    v[:size] = np.asarray(vec, np.float32).ravel()[:size]
    return _pp(v, nchunk)


def _fm(x_btd):
    return np.ascontiguousarray(np.transpose(np.asarray(x_btd, np.float32), (0, 2, 1)))


def _slice_own(x_fm, c):
    b, t0 = core_pos(c)
    return np.ascontiguousarray(x_fm[b, :, t0:t0 + T])


def _slice_halo(x_fm, c, n=HALO):
    b, t0 = core_pos(c)
    out = np.zeros((D, n), np.float32)
    lo = t0 - n
    if lo >= 0:
        out[:] = x_fm[b, :, lo:t0]
    elif t0 > 0:
        out[:, n - t0:] = x_fm[b, :, 0:t0]
    return out


def _assemble_fm(outs):
    full = np.empty((B, D, SEQ), np.float32)
    for c in range(NCORES):
        b, t0 = core_pos(c)
        full[b, :, t0:t0 + T] = outs[c]
    return full



class Rec:
    def __init__(self, nc):
        self.nc = nc
        self.ops = {e: [] for e in ENGINES}
        self.s = {}
        self.c = {}

    salt = ""

    def sem(self, n):
        n = self.salt + n
        if n not in self.s:
            self.s[n] = self.nc.alloc_semaphore(name="sm_" + n)
            self.c[n] = 0
        return self.s[n]

    def snap(self):
        return dict(self.c)

    def v(self, n):
        self.sem(n)
        return self.c[self.salt + n]

    def on(self, engine, fn, inc=None, inc_by=1, waits=None):
        if waits is None:
            waits = self.snap()
        waits = {n: v for n, v in waits.items() if v > 0}
        if inc is not None:
            self.sem(inc)
            self.c[self.salt + inc] += inc_by
            inc = self.salt + inc
        self.ops[engine].append((fn, waits, inc, inc_by))

    def dma(self, queue, out, in_, inc=None, slow=False):
        if inc is None:
            inc = {"sync": "dma", "gpsimd": "wdma", "scalar": "adma"}[queue]
        self.on(queue, lambda e, o=out, i=in_: e.dma_start(
            out=o, in_=i, allow_slow_non_contiguous=slow), inc=inc, inc_by=16)

    def finalize(self, block):
        for eng_name in ENGINES:
            ops = self.ops[eng_name]
            if not ops:
                continue

            def body(eng, ops=ops):
                last_waits = {}
                for fn, waits, inc, inc_by in ops:
                    for n, v in waits.items():
                        if last_waits.get(n, -1) < v:
                            eng.wait_ge(self.s[n], v)
                            last_waits[n] = v
                    li = fn(eng)
                    if inc is not None:
                        assert li is not None, f"op for inc={inc} returned None"
                        li.then_inc(self.s[inc], inc_by)
            getattr(block, eng_name)(body)


def _new_nc():
    return bass.Bass("TRN2", target_bir_lowering=False, debug=False)


def _inp(ctx, nc, name, shape, dtype=F32):
    if ctx is not None:
        return ctx["aps"][name]
    return _din(nc, name, shape, dtype)


def _outp(ctx, nc, name, shape, dtype=F32):
    if ctx is not None:
        return ctx["aps"][name]
    return _dout(nc, name, shape, dtype)


def _ctx_open(ctx, pfx):
    if ctx is None:
        nc = _new_nc()
        return nc, Rec(nc), Scope(nc, pfx), True
    return ctx["nc"], ctx["r"], Scope(ctx["nc"], pfx), False


class Scope:

    def __init__(self, nc, pfx=""):
        self.nc = nc
        self.pfx = pfx
        self.gs = []

    def sb(self, name, shape, dtype=F32):
        g = self.nc.sbuf_tensor(self.pfx + name, list(shape), dtype)
        t = g.__enter__()
        self.gs.append(g)
        return t

    def ps(self, name, shape, dtype=F32):
        g = self.nc.psum_tensor(self.pfx + name, list(shape), dtype)
        t = g.__enter__()
        self.gs.append(g)
        return t

    def close(self):
        for g in reversed(self.gs):
            g.__exit__(None, None, None)


def _din(nc, name, shape, dtype=F32):
    return nc.dram_tensor(name, list(shape), dtype, kind="ExternalInput").ap()


def _dout(nc, name, shape, dtype=F32):
    return nc.dram_tensor(name, list(shape), dtype, kind="ExternalOutput").ap()


def _sb(nc, name, shape, dtype=F32):
    return nc.alloc_sbuf_tensor(name, list(shape), dtype)


def _psum(nc, name, shape, dtype=F32):
    return nc.alloc_psum_tensor(name, list(shape), dtype)


def load_consts(r, nc, names):
    cst = consts()
    tiles = {}
    aps = {}
    for n in names:
        arr = cst[n]
        aps[n] = _din(nc, "c_" + n, arr.shape, F32)
        tiles[n] = _sb(nc, "t_" + n, arr.shape, F32)
        r.dma("sync", tiles[n][:], aps[n][:])
    return tiles, aps


def load_consts_bf(r, nc, names):
    cst = consts()
    tiles = {}
    aps = {}
    for n in names:
        arr = cst[n]
        aps[n] = _din(nc, "c_" + n, arr.shape, F32)
        tiles[n] = _sb(nc, "b_" + n, arr.shape, BF16)
        r.dma("gpsimd", tiles[n][:], aps[n][:])
    return tiles, aps


def const_in_map(names):
    cst = consts()
    return {"c_" + n: cst[n] for n in names}



class LNKit:
    def __init__(self, nc):
        self.sq = _sb(nc, "ln_sq", [128, 3 * 512], F32)
        self.st = _sb(nc, "ln_st", [1, 4 * 512], F32)
        self.nsc = _sb(nc, "ln_nsc", [2, 2 * 512], F32)
        self.ps_s0 = _psum(nc, "ln_ps0", [2, 512], F32)
        self.ps_s1 = _psum(nc, "ln_ps1", [1, 512], F32)
        self.ps_b0 = _psum(nc, "ln_pb0", [128, 512], F32)
        self.ps_b1 = _psum(nc, "ln_pb1", [128, 512], F32)


def rec_ln(r, kit, tiles, sl_in, sl_out, width, gb=None):
    nt = (width + 511) // 512
    for it in range(nt):
        j0 = it * 512
        w = min(512, width - j0)

        def sums_fn(t, j0=j0, w=w):
            for c in range(8):
                li = t.matmul(kit.ps_s0[0:1, 0:w], tiles["ones128"][:],
                              sl_in(c, j0, w), start=(c == 0), stop=(c == 7))
            return li
        r.on("tensor", sums_fn, inc="pe")
        for c in range(8):
            r.on("scalar", lambda s, c=c, j0=j0, w=w: s.activation(
                kit.sq[:, 0:w], sl_in(c, j0, w), AF.Square), inc="act")
            r.on("tensor", lambda t, c=c, w=w: t.matmul(
                kit.ps_s1[0:1, 0:w], tiles["ones128"][:], kit.sq[:, 0:w],
                start=(c == 0), stop=(c == 7)), inc="pe")

        MU, WRK, RSTD, NMR = (kit.st[0:1, i * 512:i * 512 + 512] for i in range(4))

        def st1_fn(v, w=w):
            v.tensor_scalar_mul(MU[:, 0:w], kit.ps_s0[0:1, 0:w], 1.0 / D)
            v.tensor_scalar_mul(WRK[:, 0:w], kit.ps_s1[0:1, 0:w], 1.0 / D)
            v.tensor_tensor(RSTD[:, 0:w], MU[:, 0:w], MU[:, 0:w], ALU.mult)
            v.tensor_tensor(WRK[:, 0:w], WRK[:, 0:w], RSTD[:, 0:w], ALU.subtract)
            return v.tensor_scalar_add(WRK[:, 0:w], WRK[:, 0:w], 1e-5)
        r.on("vector", st1_fn, inc="dve")

        r.on("scalar", lambda s, w=w: s.activation(NMR[:, 0:w], WRK[:, 0:w],
                                                   AF.Sqrt), inc="act")

        def st2_fn(v, w=w):
            v.reciprocal(RSTD[:, 0:w], NMR[:, 0:w])
            return v.scalar_tensor_tensor(NMR[:, 0:w], MU[:, 0:w], -1.0,
                                          RSTD[:, 0:w], ALU.mult, ALU.mult)
        r.on("vector", st2_fn, inc="dve")

        def bc_fn(t, w=w):
            t.matmul(kit.ps_b0[:, 0:w], tiles["ones1x128"][:], RSTD[:, 0:w],
                     start=True, stop=True)
            return t.matmul(kit.ps_b1[:, 0:w], tiles["ones1x128"][:], NMR[:, 0:w],
                            start=True, stop=True)
        r.on("tensor", bc_fn, inc="pe")

        def ap_fn(v, j0=j0, w=w):
            for c in range(8):
                src = sl_in(c, j0, w)
                dst = sl_out(c, j0, w)
                v.tensor_tensor(kit.sq[:, 0:w], src, kit.ps_b0[:, 0:w], ALU.mult)
                if gb is None:
                    li = v.tensor_tensor(dst, kit.sq[:, 0:w],
                                         kit.ps_b1[:, 0:w], ALU.add)
                else:
                    v.tensor_tensor(kit.sq[:, 0:w], kit.sq[:, 0:w],
                                    kit.ps_b1[:, 0:w], ALU.add)
                    li = v.tensor_scalar(dst, kit.sq[:, 0:w],
                                         gb[0][:, c:c + 1], gb[1][:, c:c + 1],
                                         ALU.mult, ALU.add)
            return li
        r.on("vector", ap_fn, inc="dve")


def rec_ln_stream(r, kit, tiles, stream, chnk, width, out=None, out_chnk=None, gb=None):
    def sl_in(c, j0, w):
        return stream[:, c * chnk + j0:c * chnk + j0 + w]
    if out is None:
        sl_out = sl_in
    else:
        def sl_out(c, j0, w):
            return out[:, c * out_chnk + j0:c * out_chnk + j0 + w]
    rec_ln(r, kit, tiles, sl_in, sl_out, width, gb=gb)



def rec_norm_rope(r, tiles, kit, ps_in, ps_n, ps_bc, scale_pp, t1, t2, tj0, w, dsts):
    sc1 = kit.sq[:, 0:512]
    sc2 = kit.sq[:, 512:1024]
    sc3 = kit.sq[:, 1024:1536]
    nrm = kit.nsc[0:2, 0:512]
    nri = kit.nsc[0:2, 512:1024]

    r.on("scalar", lambda s: s.activation(sc1[:, 0:w], ps_in[:, 0:w], AF.Square),
         inc="act")
    r.on("tensor", lambda t: t.matmul(ps_n[0:2, 0:w], tiles["head2"][:], sc1[:, 0:w],
                                      start=True, stop=True), inc="pe")
    r.on("scalar", lambda s: s.activation(nrm[:, 0:w], ps_n[0:2, 0:w], AF.Sqrt),
         inc="act")

    def clip_fn(v):
        v.tensor_scalar_max(nrm[:, 0:w], nrm[:, 0:w], 1e-12)
        return v.reciprocal(nri[:, 0:w], nrm[:, 0:w])
    r.on("vector", clip_fn, inc="dve")

    r.on("tensor", lambda t: t.matmul(ps_bc[:, 0:w], tiles["head2exp"][:],
                                      nri[:, 0:w], start=True, stop=True), inc="pe")
    r.on("scalar", lambda s: s.copy(sc1[:, 0:w], ps_in[:, 0:w]), inc="act")

    def fin_fn(v):
        v.scalar_tensor_tensor(sc2[:, 0:w], sc1[:, 0:w], scale_pp[:, 0:1],
                               ps_bc[:, 0:w], ALU.mult, ALU.mult)
        li = None
        for sub in range(2):
            b0 = 64 * sub
            dst = dsts[sub]
            v.tensor_scalar_mul(sc1[0:64, 0:w], sc2[b0:b0 + 64, 0:w], 1.0)
            v.tensor_scalar_mul(sc3[0:32, 0:w], sc2[b0 + 32:b0 + 64, 0:w], 1.0)
            v.tensor_scalar_mul(sc3[32:64, 0:w], sc2[b0:b0 + 32, 0:w], 1.0)
            v.tensor_tensor(sc1[0:64, 0:w], sc1[0:64, 0:w],
                            t1[0:64, tj0:tj0 + w], ALU.mult)
            v.tensor_tensor(sc3[0:64, 0:w], sc3[0:64, 0:w],
                            t2[0:64, tj0:tj0 + w], ALU.mult)
            li = v.tensor_tensor(dst[0:64, 0:w], sc1[0:64, 0:w], sc3[0:64, 0:w], ALU.add)
        return li
    r.on("vector", fin_fn, inc="dve")


def rec_rope_plain(r, kit, ps_in, t1, t2, tj0, w, dsts, do_rope, scl):
    sc1 = kit.sq[:, 0:512]
    sc2 = kit.sq[:, 512:1024]
    sc3 = kit.sq[:, 1024:1536]
    if not do_rope:
        def cp_fn(s):
            s.activation(dsts[0], ps_in[0:64, 0:w], AF.Copy, scale=scl)
            return s.activation(dsts[1], ps_in[64:128, 0:w], AF.Copy, scale=scl)
        r.on("scalar", cp_fn, inc="act")
        return
    r.on("scalar", lambda s: s.copy(sc1[:, 0:w], ps_in[:, 0:w]), inc="act")

    def fin_fn(v):
        li = None
        for sub in range(2):
            b0 = 64 * sub
            dst = dsts[sub]
            v.tensor_scalar_mul(sc2[0:64, 0:w], sc1[b0:b0 + 64, 0:w], 1.0)
            v.tensor_scalar_mul(sc3[0:32, 0:w], sc1[b0 + 32:b0 + 64, 0:w], 1.0)
            v.tensor_scalar_mul(sc3[32:64, 0:w], sc1[b0:b0 + 32, 0:w], 1.0)
            v.tensor_tensor(sc2[0:64, 0:w], sc2[0:64, 0:w],
                            t1[0:64, tj0:tj0 + w], ALU.mult)
            v.tensor_tensor(sc3[0:64, 0:w], sc3[0:64, 0:w],
                            t2[0:64, tj0:tj0 + w], ALU.mult)
            li = v.tensor_tensor(dst[0:64, 0:w], sc2[0:64, 0:w], sc3[0:64, 0:w], ALU.add)
        return li
    r.on("vector", fin_fn, inc="dve")



def rec_attn_blocks(r, kt, kdim, qt, vt, pt, ps_S, ps_O, qj0, qw, kblocks,
                    add_masks, vt_stride=65):
    n = len(kblocks)
    for i, kb in enumerate(kblocks):
        r.on("tensor", lambda t, kb=kb: t.matmul(
            ps_S[:, 0:qw], kt[0:kdim, kb * 128:(kb + 1) * 128],
            qt[0:kdim, qj0:qj0 + qw], start=True, stop=True), inc="pe")
        if add_masks is not None and add_masks[i] is not None:
            r.on("vector", lambda v, m=add_masks[i]: v.tensor_tensor(
                ps_S[:, 0:qw], ps_S[:, 0:qw], m, ALU.add), inc="dve")
        r.on("scalar", lambda s: s.activation(pt[:, 0:qw], ps_S[:, 0:qw], AF.Exp),
             inc="act")
        r.on("tensor", lambda t, kb=kb, i=i: t.matmul(
            ps_O[:, 0:qw], vt[:, kb * vt_stride:kb * vt_stride + 65],
            pt[:, 0:qw], start=(i == 0), stop=(i == n - 1)), inc="pe")


def rec_o_norm(r, tiles, kit, ps_O, ps_bc, osb, rcp, dst, w):
    r.on("vector", lambda v: v.reciprocal(rcp[0:1, 0:w], ps_O[64:65, 0:w]), inc="dve")
    r.on("scalar", lambda s: s.copy(osb[0:64, 0:w], ps_O[0:64, 0:w]), inc="act")
    r.on("tensor", lambda t: t.matmul(ps_bc[0:64, 0:w], tiles["ones1x128"][0:1, 0:64],
                                      rcp[0:1, 0:w], start=True, stop=True), inc="pe")

    def div_fn(v):
        v.tensor_tensor(kit.sq[0:64, 0:w], osb[0:64, 0:w], ps_bc[0:64, 0:w], ALU.mult)
        return v.tensor_scalar_mul(dst, kit.sq[0:64, 0:w], 1.0)
    r.on("vector", div_fn, inc="dve")



def rec_attn_win(r, kt, kdim, qt, vt, pt, ps_S, ps_O, qw, blocks):
    n = len(blocks)
    for i, (kbk, kbv, msk) in enumerate(blocks):
        r.on("tensor", lambda t, kbk=kbk: t.matmul(
            ps_S[:, 0:qw], kt[0:kdim, kbk * 128:(kbk + 1) * 128],
            qt[0:kdim, 0:qw], start=True, stop=True), inc="pe")
        if msk is not None:
            r.on("vector", lambda v, m=msk: v.tensor_tensor(
                ps_S[:, 0:qw], ps_S[:, 0:qw], m, ALU.add), inc="dve")
        r.on("scalar", lambda s: s.activation(pt[:, 0:qw], ps_S[:, 0:qw], AF.Exp),
             inc="act")
        r.on("tensor", lambda t, kbv=kbv, i=i: t.matmul(
            ps_O[:, 0:qw], vt[:, kbv * 65:kbv * 65 + 65],
            pt[:, 0:qw], start=(i == 0), stop=(i == n - 1)), inc="pe")



def build_ffnA(pre_ln, ctx=None, pfx="", halo_fill=None):
    nc, r, sc, ext = _ctx_open(ctx, pfx)
    KW = 2560
    HBW = 2561

    src = _inp(ctx, nc, "src", (D, T))
    srch = _inp(ctx, nc, "srch", (D, HALO)) if halo_fill is None else None
    wqkv_i = _inp(ctx, nc, "wqkv", (D, 1536))
    wo_i = _inp(ctx, nc, "wo", (512, D))
    qs_i = _inp(ctx, nc, "qs", (128, 1))
    ks_i = _inp(ctx, nc, "ks", (128, 1))
    mW0_i = _inp(ctx, nc, "mW0", (128, 4 * 512))
    ffq1_i = _inp(ctx, nc, "ffq1", (64, T))
    ffq2_i = _inp(ctx, nc, "ffq2", (64, T))
    ffka1_i = _inp(ctx, nc, "ffka1", (64, KW))
    ffka2_i = _inp(ctx, nc, "ffka2", (64, KW))
    ffkb1_i = _inp(ctx, nc, "ffkb1", (64, KW))
    ffkb2_i = _inp(ctx, nc, "ffkb2", (64, KW))
    umid_o = _outp(ctx, nc, "umid", (D, T))

    U = sc.sb("U", [128, 8 * T], F32)
    HB = sc.sb("HBs", [128, 8 * HBW], BF16)
    WQ = sc.sb("WQs", [128, 8 * 384], BF16)
    WO = sc.sb("WOs", [128, 4 * 128], BF16)
    QS = sc.sb("QSs", [128, 1], F32)
    KS = sc.sb("KSs", [128, 1], F32)
    MW0 = sc.sb("MW0", [128, 4 * 512], BF16)
    MKL = sc.sb("MKL", [128, 8 * 512], BF16)
    FQ1 = sc.sb("FQ1", [64, T], BF16)
    FQ2 = sc.sb("FQ2", [64, T], BF16)
    FKA1 = sc.sb("FKA1", [64, KW], BF16)
    FKA2 = sc.sb("FKA2", [64, KW], BF16)
    FKB1 = sc.sb("FKB1", [64, KW], BF16)
    FKB2 = sc.sb("FKB2", [64, KW], BF16)
    kit = ctx["kit"] if ctx else LNKit(nc)
    tiles, caps = (ctx["tiles"], ctx["caps"]) if ctx else load_consts(r, nc, ["ones128", "ones1x128", "head2",
                                      "head2exp"])
    cst = consts()
    kpre_ap = _inp(ctx, nc, "c_kpre", cst["kpre"].shape, F32)
    ident_ap = _inp(ctx, nc, "c_ident", cst["ident"].shape, F32)
    maskL_ap = _inp(ctx, nc, "c_maskL", cst["maskL"].shape, F32)

    uh_guard = nc.sbuf_tensor(pfx + "UH", [128, 8 * HALO], F32)
    UH = uh_guard.__enter__()

    r.dma("sync", U[:].rearrange("p (c w) -> p c w", w=T),
          src[:].rearrange("(c p) w -> p c w", p=128))
    if halo_fill is None:
        r.dma("sync", UH[:].rearrange("p (c w) -> p c w", w=HALO),
              srch[:].rearrange("(c p) w -> p c w", p=128))
    else:
        halo_fill(UH)
    r.dma("sync", QS[:], qs_i[:])
    r.dma("sync", KS[:], ks_i[:])
    r.dma("gpsimd", MW0[:], mW0_i[:])
    r.dma("gpsimd", MKL[:], maskL_ap[:])
    r.dma("gpsimd", FQ1[:], ffq1_i[:])
    r.dma("gpsimd", FQ2[:], ffq2_i[:])
    r.dma("gpsimd", FKA1[:], ffka1_i[:])
    r.dma("gpsimd", FKA2[:], ffka2_i[:])
    r.dma("gpsimd", FKB1[:], ffkb1_i[:])
    r.dma("gpsimd", FKB2[:], ffkb2_i[:])

    if pre_ln:
        rec_ln_stream(r, kit, tiles, U, T, T)
        rec_ln_stream(r, kit, tiles, UH, HALO, HALO)

    def hb_fn(v):
        for c in range(8):
            v.tensor_copy(HB[:, c * HBW:c * HBW + HALO], UH[:, c * HALO:(c + 1) * HALO])
            li = v.tensor_copy(HB[:, c * HBW + HALO:(c + 1) * HBW], U[:, c * T:(c + 1) * T])
        return li
    r.on("vector", hb_fn, inc="dve")

    uh_guard.__exit__(None, None, None)
    IDB = sc.sb("IDB", [128, 128], BF16)
    KTw = [sc.sb(f"KTw{i}", [65, 1024], BF16) for i in range(2)]
    QTw = [sc.sb(f"QTw{i}", [65, 512], BF16) for i in range(2)]
    VT = [sc.sb(f"VT{i}", [128, 20 * 65], BF16) for i in range(2)]
    OALL = [sc.sb(f"OALL{i}", [128, T], BF16) for i in range(4)]
    PT = sc.sb("PTs", [128, 512], BF16)
    MXC = sc.sb("MXC", [128, 4], F32)
    MXB = sc.sb("MXB", [128, 4], BF16)
    MT = sc.sb("MTs", [4, 128], BF16)
    OSB = sc.sb("OSB", [64, 512], F32)
    RCP = sc.sb("RCPs", [1, 512], F32)

    ps_qkv = kit.ps_b0
    ps_n = kit.ps_s0
    ps_bc = kit.ps_b1
    ps_S = sc.ps("ps_S", [128, 640], F32)
    ps_O = sc.ps("ps_O", [65, 512], F32)
    ps_T = sc.ps("ps_T", [4, 128], BF16)

    r.dma("gpsimd", IDB[:], ident_ap[:])
    for i in range(2):
        r.dma("gpsimd", KTw[i][64:65, :], kpre_ap[0:1, 0:1024])
        r.on("gpsimd", lambda g, i=i: g.memset(
            VT[i][:].rearrange("p (b c) -> p b c", c=65)[:, :, 64:65], 1.0),
            inc="pool")

    def hslice(c, j0, w):
        off = 1 if c < 4 else 0
        return HB[:, c * HBW + j0 + off: c * HBW + j0 + off + w]

    for hp in range(4):
        r.dma("gpsimd", WQ[:].rearrange("p (c w) -> p c w", w=384)[:, :, 0:128],
              wqkv_i[:, hp * 128:(hp + 1) * 128].rearrange("(c p) w -> p c w", p=128))
        r.dma("gpsimd", WQ[:].rearrange("p (c w) -> p c w", w=384)[:, :, 128:256],
              wqkv_i[:, 512 + hp * 128:512 + (hp + 1) * 128].rearrange("(c p) w -> p c w", p=128))
        r.dma("gpsimd", WQ[:].rearrange("p (c w) -> p c w", w=384)[:, :, 256:384],
              wqkv_i[:, 1024 + hp * 128:1024 + (hp + 1) * 128].rearrange("(c p) w -> p c w", p=128))

        for blk in range(20):
            j0 = blk * 128

            def vmm_fn(t, j0=j0):
                for sub in range(2):
                    for c in range(8):
                        li = t.matmul(ps_qkv[0:128, sub * 64:sub * 64 + 64],
                                      hslice(c, j0, 128),
                                      WQ[:, c * 384 + 256 + sub * 64:c * 384 + 256 + sub * 64 + 64],
                                      start=(c == 0), stop=(c == 7))
                return li
            r.on("tensor", vmm_fn, inc="pe")

            def vev_fn(s, blk=blk):
                s.copy(VT[0][:, blk * 65:blk * 65 + 64], ps_qkv[0:128, 0:64])
                return s.copy(VT[1][:, blk * 65:blk * 65 + 64], ps_qkv[0:128, 64:128])
            r.on("scalar", vev_fn, inc="act")

        for w in range(4):
            w0 = w * 512
            for half in range(2):
                kj0 = w0 + half * 512

                def kmm_fn(t, kj0=kj0):
                    for c in range(8):
                        li = t.matmul(ps_qkv[:], WQ[:, c * 384 + 128:c * 384 + 256],
                                      hslice(c, kj0, 512), start=(c == 0), stop=(c == 7))
                    return li
                r.on("tensor", kmm_fn, inc="pe")
                t1 = FKA1 if half == 1 else FKB1
                t2 = FKA2 if half == 1 else FKB2
                rec_norm_rope(r, tiles, kit, ps_qkv, ps_n, ps_bc, KS,
                              t1, t2, kj0, 512,
                              (KTw[0][0:64, half * 512:(half + 1) * 512],
                               KTw[1][0:64, half * 512:(half + 1) * 512]))

            def qmm_fn(t, w0=w0):
                for c in range(8):
                    li = t.matmul(ps_qkv[:], WQ[:, c * 384:c * 384 + 128],
                                  hslice(c, 512 + w0, 512), start=(c == 0), stop=(c == 7))
                return li
            r.on("tensor", qmm_fn, inc="pe")
            rec_norm_rope(r, tiles, kit, ps_qkv, ps_n, ps_bc, QS,
                          FQ1, FQ2, w0, 512,
                          (QTw[0][0:64, :], QTw[1][0:64, :]))

            for sub in range(2):
                Kt, Qt, Vt = KTw[sub], QTw[sub], VT[sub]
                for qs in range(4):
                    def mx_fn(t, qs=qs, Kt=Kt, Qt=Qt):
                        t.matmul(ps_S[:, 0:512], Qt[0:64, qs * 128:(qs + 1) * 128],
                                 Kt[0:64, qs * 128:qs * 128 + 512], start=True, stop=True)
                        return t.matmul(ps_S[:, 512:640],
                                        Qt[0:64, qs * 128:(qs + 1) * 128],
                                        Kt[0:64, qs * 128 + 512:qs * 128 + 640],
                                        start=True, stop=True)
                    r.on("tensor", mx_fn, inc="pe")
                    r.on("vector", lambda v, qs=qs: v.reduce_max(
                        MXC[:, qs:qs + 1], ps_S[:, 0:640], AX), inc="dve")
                r.on("vector", lambda v: v.tensor_scalar_mul(MXB[:], MXC[:], -1.0),
                     inc="dve")
                r.on("tensor", lambda t: t.transpose(ps_T[:], MXB[:], IDB[:]),
                     inc="pe")
                r.on("scalar", lambda s: s.copy(MT[:], ps_T[:]), inc="act")
                for qs in range(4):
                    r.dma("scalar", Qt[64:65, qs * 128:(qs + 1) * 128],
                          MT[qs:qs + 1, :])
                blocks = []
                for m in range(8):
                    if w == 0 and m < 4:
                        msk = MW0[:, m * 512:(m + 1) * 512]
                    else:
                        msk = MKL[:, m * 512:(m + 1) * 512]
                    blocks.append((m, 4 * w + m, msk))
                rec_attn_win(r, Kt, 65, Qt, Vt, PT, ps_S, ps_O, 512, blocks)
                hh = 2 * hp + sub
                dst = OALL[hh // 2][(hh % 2) * 64:(hh % 2) * 64 + 64, w0:w0 + 512]
                rec_o_norm(r, tiles, kit, ps_O, ps_bc, OSB, RCP, dst, 512)

    for f in range(8):
        r.dma("gpsimd", WO[:].rearrange("p (c w) -> p c w", w=128),
              wo_i[:, f * 128:(f + 1) * 128].rearrange("(c p) w -> p c w", p=128))
        for tt_ in range(4):
            j0 = tt_ * 512

            def wo_fn(t, f=f, j0=j0):
                for c in range(4):
                    li = t.matmul(ps_qkv[:], WO[:, c * 128:(c + 1) * 128],
                                  OALL[c][:, j0:j0 + 512], start=(c == 0), stop=(c == 3))
                return li
            r.on("tensor", wo_fn, inc="pe")
            r.on("vector", lambda v, f=f, j0=j0: v.tensor_tensor(
                U[:, f * T + j0:f * T + j0 + 512], U[:, f * T + j0:f * T + j0 + 512],
                ps_qkv[:], ALU.add), inc="dve")

    r.dma("sync", umid_o[:].rearrange("(c p) w -> p c w", p=128),
          U[:].rearrange("p (c w) -> p c w", w=T), inc="st")
    r.on("sync", lambda e: e.nop(), waits=r.snap())
    if not ext:
        sc.close()
    if ext:
        with nc.Block() as block:
            r.finalize(block)
    return nc


def ffnA_inputs(stream_fm, c, layer, W, pre_ln):
    cst = consts()
    b, t0 = core_pos(c)

    def ksl(name):
        out = np.zeros((64, 2560), np.float32)
        lo = t0 - 512
        s = max(lo, 0)
        out[:, s - lo:] = cst[name][:, s:t0 + T]
        return out

    m = {
        "src": _slice_own(stream_fm, c),
        "srch": _slice_halo(stream_fm, c),
        "wqkv": np.ascontiguousarray(W["L0_ffn_wqkv"][layer]),
        "wo": np.ascontiguousarray(W["L0_ffn_wo"][layer]),
        "qs": np.tile(np.asarray(W["L0_ffn_q_scale"][layer], np.float32).reshape(64, 1), (2, 1)),
        "ks": np.tile(np.asarray(W["L0_ffn_k_scale"][layer], np.float32).reshape(64, 1), (2, 1)),
        "mW0": cst["maskL0"] if t0 == 0 else np.ascontiguousarray(cst["maskL"][:, 0:4 * 512]),
        "ffq1": np.ascontiguousarray(cst["ffq_t1"][:, t0:t0 + T]),
        "ffq2": np.ascontiguousarray(cst["ffq_t2"][:, t0:t0 + T]),
        "ffka1": ksl("ffka_t1"),
        "ffka2": ksl("ffka_t2"),
        "ffkb1": ksl("ffkb_t1"),
        "ffkb2": ksl("ffkb_t2"),
    }
    m.update(const_in_map(["ones128", "ones1x128", "head2", "head2exp",
                           "kpre", "ident", "maskL"]))
    return m



def build_ffnB(final_x, ctx=None, pfx="", prev_fill=None):
    nc, r, sc, ext = _ctx_open(ctx, pfx)
    UW = T + 1

    u_i = _inp(ctx, nc, "u", (D, T))
    up_i = _inp(ctx, nc, "uprev", (D, 1)) if prev_fill is None else None
    lng_i = _inp(ctx, nc, "lng", (128, 8))
    lnb_i = _inp(ctx, nc, "lnb", (128, 8))
    w1_i = _inp(ctx, nc, "w1", (D, 5460))
    b1a_i = _inp(ctx, nc, "b1a", (128, 22))
    b1g_i = _inp(ctx, nc, "b1g", (128, 22))
    w2_i = _inp(ctx, nc, "w2", (2730, D))
    b2_i = _inp(ctx, nc, "b2", (128, 8))
    if final_x:
        x_i = _inp(ctx, nc, "x", (D, T))
        out_o = _outp(ctx, nc, "xout", (D, T))
    else:
        out_o = _outp(ctx, nc, "uout", (D, T))

    U = sc.sb("U", [128, 8 * UW], F32)
    Z = sc.sb("Z", [128, 8 * 512], BF16)
    HT = sc.sb("HT", [128, 22 * 512], BF16)
    W1C = sc.sb("W1C", [128, 8 * 256], BF16)
    W2C = sc.sb("W2C", [128, 22 * 128], BF16)
    GEL = sc.sb("GEL", [128, 512], BF16)
    LNG = sc.sb("LNG", [128, 8], F32)
    LNB = sc.sb("LNB", [128, 8], F32)
    B1A = sc.sb("B1A", [128, 22], F32)
    B1G = sc.sb("B1G", [128, 22], F32)
    B2 = sc.sb("B2s", [128, 8], F32)
    if final_x:
        X = sc.sb("X", [128, 8 * T], F32)

    ps_a = sc.ps("ps_a", [128, 512], F32)
    ps_g = sc.ps("ps_g", [128, 512], F32)
    ps_w2 = sc.ps("ps_w2", [128, 512], F32)

    kit = ctx["kit"] if ctx else LNKit(nc)
    tiles, caps = (ctx["tiles"], ctx["caps"]) if ctx else load_consts(r, nc, ["ones128", "ones1x128"])

    r.dma("sync", U[:].rearrange("p (c w) -> p c w", w=UW)[:, :, 1:UW],
          u_i[:].rearrange("(c p) w -> p c w", p=128))
    if prev_fill is None:
        r.dma("sync", U[:].rearrange("p (c w) -> p c w", w=UW)[:, :, 0:1],
              up_i[:].rearrange("(c p) w -> p c w", p=128), slow=True)
    else:
        prev_fill(U)
    r.dma("sync", LNG[:], lng_i[:])
    r.dma("sync", LNB[:], lnb_i[:])
    r.dma("sync", B1A[:], b1a_i[:])
    r.dma("sync", B1G[:], b1g_i[:])
    r.dma("sync", B2[:], b2_i[:])
    if final_x:
        r.dma("sync", X[:].rearrange("p (c w) -> p c w", w=T),
              x_i[:].rearrange("(c p) w -> p c w", p=128))

    for tt_ in reversed(range(4)):
        tb = tt_ * 512

        def sl_in(c, j0, w, tb=tb):
            off = 1 if c < 4 else 0
            return U[:, c * UW + off + tb + j0: c * UW + off + tb + j0 + w]

        def sl_out(c, j0, w):
            return Z[:, c * 512 + j0: c * 512 + j0 + w]

        rec_ln(r, kit, tiles, sl_in, sl_out, 512, gb=(LNG, LNB))

        for jc in range(22):
            acw = 128 if jc < 21 else 42
            r.dma("gpsimd", W1C[:].rearrange("p (c w) -> p c w", w=256)[:, :, 0:acw],
                  w1_i[:, jc * 128:jc * 128 + acw].rearrange("(c p) w -> p c w", p=128))
            r.dma("gpsimd", W1C[:].rearrange("p (c w) -> p c w", w=256)[:, :, 128:128 + acw],
                  w1_i[:, 2730 + jc * 128:2730 + jc * 128 + acw].rearrange("(c p) w -> p c w", p=128))

            def ag_fn(t, jc=jc, acw=acw):
                for c in range(8):
                    t.matmul(ps_a[0:acw, :], W1C[:, c * 256:c * 256 + acw],
                             Z[:, c * 512:(c + 1) * 512],
                             start=(c == 0), stop=(c == 7))
                for c in range(8):
                    li = t.matmul(ps_g[0:acw, :], W1C[:, c * 256 + 128:c * 256 + 128 + acw],
                                  Z[:, c * 512:(c + 1) * 512],
                                  start=(c == 0), stop=(c == 7))
                return li
            r.on("tensor", ag_fn, inc="pe")
            r.on("scalar", lambda s, jc=jc, acw=acw: s.activation(
                GEL[0:acw, :], ps_g[0:acw, :], AF.Gelu, bias=B1G[0:acw, jc:jc + 1]),
                inc="act")
            r.on("vector", lambda v, jc=jc, acw=acw: v.scalar_tensor_tensor(
                HT[0:acw, jc * 512:(jc + 1) * 512], ps_a[0:acw, :],
                B1A[0:acw, jc:jc + 1], GEL[0:acw, :], ALU.add, ALU.mult),
                inc="dve")

        for f in range(8):
            r.dma("gpsimd", W2C[:, 0:21 * 128],
                  w2_i[0:21 * 128, f * 128:(f + 1) * 128].rearrange("(c p) w -> p c w", p=128))
            r.dma("gpsimd", W2C[0:42, 21 * 128:22 * 128],
                  w2_i[21 * 128:2730, f * 128:(f + 1) * 128])

            def w2_fn(t, f=f):
                for c in range(22):
                    kc = 128 if c < 21 else 42
                    li = t.matmul(ps_w2[:], W2C[0:kc, c * 128:(c + 1) * 128],
                                  HT[0:kc, c * 512:(c + 1) * 512],
                                  start=(c == 0), stop=(c == 21))
                return li
            r.on("tensor", w2_fn, inc="pe")
            r.on("vector", lambda v, f=f, tb=tb: v.scalar_tensor_tensor(
                U[:, f * UW + 1 + tb:f * UW + 1 + tb + 512], ps_w2[:],
                B2[:, f:f + 1], U[:, f * UW + 1 + tb:f * UW + 1 + tb + 512],
                ALU.add, ALU.add), inc="dve")

    if final_x:
        def xres_fn(v):
            for f in range(8):
                li = v.scalar_tensor_tensor(
                    X[:, f * T:(f + 1) * T], U[:, f * UW + 1:f * UW + 1 + T],
                    0.5, X[:, f * T:(f + 1) * T], ALU.mult, ALU.add)
            return li
        r.on("vector", xres_fn, inc="dve")
        r.dma("sync", out_o[:].rearrange("(c p) w -> p c w", p=128),
              X[:].rearrange("p (c w) -> p c w", w=T), inc="st")
    else:
        r.dma("sync", out_o[:].rearrange("(c p) w -> p c w", p=128),
              U[:].rearrange("p (c w) -> p c w", w=UW)[:, :, 1:UW], inc="st")
    r.on("sync", lambda e: e.nop(), waits=r.snap())
    if not ext:
        sc.close()
    if ext:
        with nc.Block() as block:
            r.finalize(block)
    return nc


def ffnB_inputs(u_fm, c, layer, W, x_fm=None):
    m = {
        "u": _slice_own(u_fm, c),
        "uprev": _slice_halo(u_fm, c, 1),
        "lng": _pp(W["L0_ffn_ln_g"][layer]),
        "lnb": _pp(W["L0_ffn_ln_b"][layer]),
        "w1": np.ascontiguousarray(W["L0_ffn_w1"][layer]),
        "b1a": _pp_pad(W["L0_ffn_b1"][layer][:2730], 22, 2730),
        "b1g": _pp_pad(W["L0_ffn_b1"][layer][2730:], 22, 2730),
        "w2": np.ascontiguousarray(W["L0_ffn_w2"][layer]),
        "b2": _pp(W["L0_ffn_b2"][layer]),
    }
    if x_fm is not None:
        m["x"] = _slice_own(x_fm, c)
    m.update(const_in_map(["ones128", "ones1x128"]))
    return m



def prep_weights(inputs):
    W = {}
    for k, v in inputs.items():
        W[k] = np.asarray(v, np.float32)
    wq = W["L0_attn_w_qkv"].copy()
    ht = np.arange(16) % 4
    perm64 = np.concatenate([np.arange(0, 64, 2), np.arange(1, 64, 2)])
    for h in range(16):
        if (ht[h] >> 1) & 1:
            for base in (0, 1024):
                cols = base + h * 64 + np.arange(64)
                wq[:, cols] = wq[:, cols[perm64]]
    W["att_wqkv_perm"] = np.ascontiguousarray(wq)
    for k in ["L1_attn_q_w", "L1_attn_k_w", "L1_attn_v_w", "L1_attn_o_w",
              "L1_attn_fc_w", "L1_attn_proj_w", "L1_ffn_w1_w", "L1_ffn_w2_w",
              "L1_ffn_h1_w", "L1_ffn_h2_w", "L1_ffn_w3_w"]:
        W[k + "_t"] = np.ascontiguousarray(W[k].T)
    return W


_NC_CACHE = {}


def _nc_get(key, fn):
    if key not in _NC_CACHE:
        _NC_CACHE[key] = fn()
    return _NC_CACHE[key]


def _run(nc, in_maps):
    return run_bass_kernel_spmd(nc, in_maps, core_ids=list(range(NCORES)))


def _ffn_phase(x_fm, W):
    ncA0 = _nc_get("A0", lambda: build_ffnA(pre_ln=True))
    ncA1 = _nc_get("A1", lambda: build_ffnA(pre_ln=False))
    ncB0 = _nc_get("B0", lambda: build_ffnB(final_x=False))
    ncB1 = _nc_get("B1", lambda: build_ffnB(final_x=True))
    res = _run(ncA0, [ffnA_inputs(x_fm, c, 0, W, True) for c in range(NCORES)])
    m_fm = _assemble_fm([res.results[c]["umid"] for c in range(NCORES)])
    res = _run(ncB0, [ffnB_inputs(m_fm, c, 0, W) for c in range(NCORES)])
    u1_fm = _assemble_fm([res.results[c]["uout"] for c in range(NCORES)])
    res = _run(ncA1, [ffnA_inputs(u1_fm, c, 1, W, False) for c in range(NCORES)])
    m2_fm = _assemble_fm([res.results[c]["umid"] for c in range(NCORES)])
    res = _run(ncB1, [ffnB_inputs(m2_fm, c, 1, W, x_fm=x_fm) for c in range(NCORES)])
    return _assemble_fm([res.results[c]["xout"] for c in range(NCORES)])


def _attn_phase(x_fm, W):
    nc1 = _nc_get("Q", build_attn_qkv)
    nc2 = _nc_get("C", build_attn_core)
    nc3 = _nc_get("P", build_attn_proj)
    res = _run(nc1, [attn_qkv_inputs(x_fm, c, W) for c in range(NCORES)])
    qf = np.empty((B, D, SEQ), BF)
    kf = np.empty((B, D, SEQ), BF)
    vf = np.empty((B, SEQ, D), BF)
    for c in range(NCORES):
        b, t0 = core_pos(c)
        qf[b, :, t0:t0 + T] = res.results[c]["q_o"]
        kf[b, :, t0:t0 + T] = res.results[c]["k_o"]
        vo = res.results[c]["v_o"]
        for h in range(16):
            vf[b, t0:t0 + T, h * 64:(h + 1) * 64] = vo[h * T:(h + 1) * T]
    chif = np.empty((16, SEQ), BF)
    clof = np.empty((16, SEQ), BF)
    for c in range(4):
        b, t0 = core_pos(c)
        chif[:, t0:t0 + T] = res.results[c]["chi"]
        clof[:, t0:t0 + T] = res.results[c]["clo"]
    res2 = _run(nc2, [attn_core_inputs(qf, kf, vf, chif, clof, c, W)
                      for c in range(NCORES)])
    o_fm = np.empty((B, D, SEQ), BF)
    for c in range(NCORES):
        b = c // 4
        h0 = (c % 4) * 4
        o4 = res2.results[c]["o4"]
        for lc in range(4):
            for p_ in range(4):
                h = h0 + p_
                o_fm[b, h * 64:(h + 1) * 64, lc * 2048:(lc + 1) * 2048] = \
                    o4[lc * 256 + p_ * 64:lc * 256 + (p_ + 1) * 64, :]
    res3 = _run(nc3, [attn_proj_inputs(x_fm, o_fm, c, W) for c in range(NCORES)])
    return _assemble_fm([res3.results[c]["xout"] for c in range(NCORES)])


def kernel(**inputs):
    return kernel_fused(**inputs)


def kernel_multilaunch(**inputs):
    W = prep_weights(inputs)
    x_fm = _fm(inputs["L0_attn_x"])
    x_fm = _ffn_phase(x_fm, W)
    x_fm = _attn_phase(x_fm, W)
    x_fm = _ffn_phase(x_fm, W)
    ncF = _nc_get("F", build_l1ffn)
    ncL = _nc_get("L", build_l1attn)
    res = _run(ncF, [l1ffn_inputs(x_fm, c, W) for c in range(NCORES)])
    x_fm = _assemble_fm([res.results[c]["xout"] for c in range(NCORES)])
    res = _run(ncL, [l1attn_inputs(x_fm, c, W) for c in range(NCORES)])
    x_fm = _assemble_fm([res.results[c]["xout"] for c in range(NCORES)])
    res = _run(ncF, [l1ffn_inputs(x_fm, c, W) for c in range(NCORES)])
    x_fm = _assemble_fm([res.results[c]["xout"] for c in range(NCORES)])
    return np.ascontiguousarray(np.transpose(x_fm, (0, 2, 1)))



def build_attn_qkv(ctx=None, pfx=""):
    nc, r, sc, ext = _ctx_open(ctx, pfx)

    x_i = _inp(ctx, nc, "x", (D, T))
    wq_i = _inp(ctx, nc, "wqkv", (D, 3072))
    lx_i = _inp(ctx, nc, "logxi", (16, 1))
    eb_i = _inp(ctx, nc, "ebits", (16, 1))
    j16_i = _inp(ctx, nc, "j16", (16, T))
    aq1_i = _inp(ctx, nc, "aq1", (64, T))
    aq2_i = _inp(ctx, nc, "aq2", (64, T))
    ak1_i = _inp(ctx, nc, "ak1", (64, T))
    ak2_i = _inp(ctx, nc, "ak2", (64, T))
    q_o = _outp(ctx, nc, "q_o", (D, T), BF16)
    k_o = _outp(ctx, nc, "k_o", (D, T), BF16)
    v_o = _outp(ctx, nc, "v_o", (16 * T, 64), BF16)
    chi_o = _outp(ctx, nc, "chi", (16, T), BF16)
    clo_o = _outp(ctx, nc, "clo", (16, T), BF16)

    X = sc.sb("X", [128, 8 * T], F32)
    Z = sc.sb("Z", [128, 8 * T], BF16)
    WC = sc.sb("WC", [128, 8 * 512], BF16)
    STG = sc.sb("STG", [128, 512], BF16)
    AQ1 = sc.sb("AQ1", [64, T], BF16)
    AQ2 = sc.sb("AQ2", [64, T], BF16)
    AK1 = sc.sb("AK1", [64, T], BF16)
    AK2 = sc.sb("AK2", [64, T], BF16)
    LX = sc.sb("LX", [16, 1], F32)
    EB = sc.sb("EB", [16, 1], F32)
    J16 = sc.sb("J16", [16, T], F32)
    C32 = sc.sb("C32", [16, 2 * T], F32)
    CBF = sc.sb("CBF", [16, 2 * T], BF16)

    ht = np.arange(16) % 4
    PI = [(int(h) >> 1) & 1 for h in ht]

    kit = ctx["kit"] if ctx else LNKit(nc)
    tiles, caps = (ctx["tiles"], ctx["caps"]) if ctx else load_consts(r, nc, ["ones128", "ones1x128"])
    ps_mm = kit.ps_b0
    ps_n = kit.ps_s0

    r.dma("sync", X[:].rearrange("p (c w) -> p c w", w=T),
          x_i[:].rearrange("(c p) w -> p c w", p=128))
    r.dma("gpsimd", AQ1[:], aq1_i[:])
    r.dma("gpsimd", AQ2[:], aq2_i[:])
    r.dma("gpsimd", AK1[:], ak1_i[:])
    r.dma("gpsimd", AK2[:], ak2_i[:])
    r.dma("sync", LX[:], lx_i[:])
    r.dma("sync", EB[:], eb_i[:])
    r.dma("sync", J16[:], j16_i[:])

    rec_ln_stream(r, kit, tiles, X, T, T, out=Z, out_chnk=T)

    r.on("scalar", lambda s: s.activation(C32[0:16, 0:1], LX[:], AF.Exp,
                                          scale=-1.0), inc="act")
    r.on("vector", lambda v: v.tensor_tensor(C32[0:16, 1:2], C32[0:16, 0:1],
                                             EB[:], ALU.mult), inc="dve")

    def crow_fn(v):
        v.tensor_scalar_mul(C32[0:16, T:2 * T], J16[:], C32[0:16, 1:2])
        v.tensor_scalar_mul(CBF[0:16, 0:T], C32[0:16, T:2 * T], 1.0)
        v.tensor_scalar_mul(C32[0:16, 0:T], CBF[0:16, 0:T], 1.0)
        v.tensor_tensor(C32[0:16, 0:T], C32[0:16, T:2 * T], C32[0:16, 0:T],
                        ALU.subtract)
        return v.tensor_scalar_mul(CBF[0:16, T:2 * T], C32[0:16, 0:T], 1.0)
    r.on("vector", crow_fn, inc="dve")
    r.dma("sync", chi_o[:], CBF[0:16, 0:T], inc="st")
    r.dma("sync", clo_o[:], CBF[0:16, T:2 * T], inc="st")

    for role in range(2):
        base = role * 1024
        t1 = AQ1 if role == 0 else AK1
        t2 = AQ2 if role == 0 else AK2
        out_ap = q_o if role == 0 else k_o
        for oc in range(8):
            r.dma("gpsimd", WC[:, 0:8 * 128].rearrange("p (c w) -> p c w", w=128),
                  wq_i[:, base + oc * 128:base + (oc + 1) * 128]
                  .rearrange("(c p) w -> p c w", p=128))
            for it in range(4):
                j0 = it * 512

                def mm_fn(t, j0=j0):
                    for c in range(8):
                        li = t.matmul(ps_mm[:], WC[:, c * 128:(c + 1) * 128],
                                      Z[:, c * T + j0:c * T + j0 + 512],
                                      start=(c == 0), stop=(c == 7))
                    return li
                r.on("tensor", mm_fn, inc="pe")
                h0, h1 = 2 * oc, 2 * oc + 1
                scl = 0.125 if role == 0 else 1.0
                if PI[h0]:
                    rec_rope_plain(r, kit, ps_mm, t1, t2, j0, 512,
                                   (STG[0:64, :], STG[64:128, :]), True, scl)
                else:
                    rec_rope_plain(r, kit, ps_mm, t1, t2, j0, 512,
                                   (STG[0:64, :], STG[64:128, :]), False, scl)
                r.dma("sync", out_ap[oc * 128:(oc + 1) * 128, j0:j0 + 512],
                      STG[:], inc="st")


    for vh in range(2):
        r.dma("gpsimd", WC[:].rearrange("p (c w) -> p c w", w=512),
              wq_i[:, 2048 + vh * 512:2048 + (vh + 1) * 512]
              .rearrange("(c p) w -> p c w", p=128))
        for tc in range(16):
            def vmm_fn(t, tc=tc):
                for c in range(8):
                    li = t.matmul(ps_mm[:], Z[:, c * T + tc * 128:c * T + (tc + 1) * 128],
                                  WC[:, c * 512:(c + 1) * 512],
                                  start=(c == 0), stop=(c == 7))
                return li
            r.on("tensor", vmm_fn, inc="pe")
            r.on("scalar", lambda s: s.copy(STG[:], ps_mm[:]), inc="act")
            for hh in range(8):
                h = vh * 8 + hh
                r.dma("sync", v_o[h * T + tc * 128:h * T + (tc + 1) * 128, :],
                      STG[:, hh * 64:(hh + 1) * 64], inc="st")

    r.on("sync", lambda e: e.nop(), waits=r.snap())
    if not ext:
        sc.close()
    if ext:
        with nc.Block() as block:
            r.finalize(block)
    return nc


def attn_qkv_inputs(x_fm, c, W):
    cst = consts()
    b, t0 = core_pos(c)
    m = {
        "x": _slice_own(x_fm, c),
        "wqkv": W["att_wqkv_perm"],
        "logxi": np.asarray(W["L0_attn_log_xi"], np.float32).reshape(16, 1),
        "ebits": cst["ebits"],
        "j16": np.ascontiguousarray(cst["j16"][:, t0:t0 + T]),
        "aq1": np.ascontiguousarray(cst["att_t1q"][:, t0:t0 + T]),
        "aq2": np.ascontiguousarray(cst["att_t2q"][:, t0:t0 + T]),
        "ak1": np.ascontiguousarray(cst["att_t1"][:, t0:t0 + T]),
        "ak2": np.ascontiguousarray(cst["att_t2"][:, t0:t0 + T]),
    }
    m.update(const_in_map(["ones128", "ones1x128"]))
    return m



def build_attn_core(ctx=None, pfx="", fills=None):
    nc, r, sc, ext = _ctx_open(ctx, pfx)

    if fills is None:
        q4_i = _inp(ctx, nc, "q4", (256, SEQ), BF16)
        k4_i = _inp(ctx, nc, "k4", (256, SEQ), BF16)
        v4_i = _inp(ctx, nc, "v4", (4 * SEQ, 64), BF16)
        chi_i = _inp(ctx, nc, "chi4", (4, SEQ), BF16)
        clo_i = _inp(ctx, nc, "clo4", (4, SEQ), BF16)
    lxs_i = _inp(ctx, nc, "logxi4", (1, 4))
    ebs_i = _inp(ctx, nc, "ebits4", (1, 4))
    jr_i = _inp(ctx, nc, "jrow", (1, SEQ))
    o_o = _outp(ctx, nc, "o4", (1024, T), BF16)

    KT = sc.sb("KT", [68, SEQ], BF16)
    QT = sc.sb("QT", [68, SEQ], BF16)
    VT = sc.sb("VT", [128, 64 * 65], BF16)
    PTB = [sc.sb(f"PTs{i}", [128, 1024], BF16) for i in range(2)]
    PT = PTB[0]
    MC = sc.sb("MC", [128, 5 * 512], F32)
    JR = sc.sb("JR", [1, SEQ], F32)
    EXI = sc.sb("EXI", [1, 8], F32)
    SC = sc.sb("SCs", [64, 512], F32)
    ST = sc.sb("STs", [1, 4 * 512], F32)
    KMX = sc.sb("KMX", [1, 64], F32)
    OSB = sc.sb("OSB", [64, 1024], F32)
    RCP = sc.sb("RCPs", [1, 1024], F32)
    OST = sc.sb("OST", [64, 1024], BF16)

    ps_S = sc.ps("ps_S", [128, 1024], F32)
    ps_O = sc.ps("ps_O", [65, 1024], F32)

    cst = consts()

    tiles, caps = (ctx["tiles"], ctx["caps"]) if ctx else load_consts(r, nc, ["ones128", "ones1x128"])
    kit = ctx["kit"] if ctx else LNKit(nc)
    ps_n = kit.ps_s0
    ps_bc = kit.ps_b0
    ps_n2 = (kit.ps_b0, kit.ps_b1)
    kpre_ap = _inp(ctx, nc, "c_kpre", cst["kpre"].shape, F32)
    qpre_ap = _inp(ctx, nc, "c_qpre", cst["qpre"].shape, F32)
    maskC_ap = _inp(ctx, nc, "c_maskC", cst["maskC"].shape, F32)

    r.dma("sync", MC[:], maskC_ap[:])
    r.dma("sync", JR[:], jr_i[:])
    EXL = sc.sb("EXL", [1, 8], F32)
    r.dma("sync", EXL[0:1, 0:4], lxs_i[:])
    r.dma("sync", EXL[0:1, 4:8], ebs_i[:])
    r.on("scalar", lambda s: s.activation(EXI[0:1, 0:4], EXL[0:1, 0:4], AF.Exp,
                                          scale=-1.0), inc="act")
    r.on("vector", lambda v: v.tensor_tensor(EXI[0:1, 4:8], EXI[0:1, 0:4],
                                             EXL[0:1, 4:8], ALU.mult), inc="dve")
    r.on("gpsimd", lambda g: g.memset(
        VT[:].rearrange("p (b c) -> p b c", c=65)[:, :, 64:65], 1.0), inc="pool")

    for p_ in range(4):
        r.dma("gpsimd", KT[64:65, :], kpre_ap[0:1, :])
        r.dma("gpsimd", KT[67:68, :], kpre_ap[3:4, :])
        r.dma("gpsimd", QT[65:67, :], qpre_ap[0:2, :])
        r.dma("gpsimd", QT[67:68, :], qpre_ap[2:3, :])
        if fills is None:
            r.dma("sync", KT[0:64, :], k4_i[p_ * 64:(p_ + 1) * 64, :])
            r.dma("sync", KT[65:66, :], chi_i[p_:p_ + 1, :])
            r.dma("sync", KT[66:67, :], clo_i[p_:p_ + 1, :])
            r.dma("sync", QT[0:64, :], q4_i[p_ * 64:(p_ + 1) * 64, :])
            for kb in range(64):
                r.dma("sync", VT[:, kb * 65:kb * 65 + 64],
                      v4_i[p_ * SEQ + kb * 128:p_ * SEQ + (kb + 1) * 128, :])
        else:
            fills(p_, KT, QT, VT, JR, EXI)

        for it in range(16):
            r.on("scalar", lambda s, it=it: s.activation(
                SC[:, 0:512], KT[0:64, it * 512:(it + 1) * 512], AF.Square),
                inc="act")
            r.on("tensor", lambda t: t.matmul(
                ps_n[0:1, 0:512], tiles["ones128"][0:64, :], SC[:, 0:512],
                start=True, stop=True), inc="pe")
            r.on("vector", lambda v, it=it: v.reduce_max(
                KMX[0:1, it:it + 1], ps_n[0:1, 0:512], AX), inc="dve")
        r.on("vector", lambda v: v.reduce_max(KMX[0:1, 32:33], KMX[0:1, 0:16], AX),
             inc="dve")
        r.on("scalar", lambda s: s.activation(KMX[0:1, 33:34], KMX[0:1, 32:33],
                                              AF.Sqrt), inc="act")

        for it in range(16):
            r.on("scalar", lambda s, it=it: s.activation(
                SC[:, 0:512], QT[0:64, it * 512:(it + 1) * 512], AF.Square),
                inc="act")
            r.on("tensor", lambda t: t.matmul(
                ps_n[0:1, 0:512], tiles["ones128"][0:64, :], SC[:, 0:512],
                start=True, stop=True), inc="pe")
            r.on("scalar", lambda s: s.activation(ST[0:1, 0:512],
                                                  ps_n[0:1, 0:512], AF.Sqrt),
                 inc="act")

            def brow_fn(v, it=it, p_=p_):
                v.tensor_scalar_mul(ST[0:1, 512:1024], ST[0:1, 0:512],
                                    KMX[0:1, 33:34])
                v.scalar_tensor_tensor(ST[0:1, 1024:1536],
                                       JR[0:1, it * 512:(it + 1) * 512],
                                       EXI[0:1, 4 + p_:5 + p_],
                                       ST[0:1, 512:1024], ALU.mult, ALU.add)
                return v.tensor_scalar_mul(QT[64:65, it * 512:(it + 1) * 512],
                                           ST[0:1, 1024:1536], -1.0)
            r.on("vector", brow_fn, inc="dve")

        for qt2 in range(8):
            q0 = qt2 * 1024
            nb = 8 * qt2 + 8
            base = r.snap()
            exp_c = []
            pv_c = []
            for kb in range(nb):
                rel = kb - 8 * qt2
                odd = kb % 2
                if odd:
                    sdst = (ps_n2[0][:, 0:512], ps_n2[1][:, 0:512])
                else:
                    sdst = (ps_S[:, 0:512], ps_S[:, 512:1024])
                pt = PTB[odd]
                wv = dict(base)
                if kb >= 2:
                    wv[r.salt + "act"] = exp_c[kb - 2]

                def s_fn(t, kb=kb, q0=q0, sdst=sdst):
                    t.matmul(sdst[0], KT[0:68, kb * 128:(kb + 1) * 128],
                             QT[0:68, q0:q0 + 512], start=True, stop=True)
                    return t.matmul(sdst[1],
                                    KT[0:68, kb * 128:(kb + 1) * 128],
                                    QT[0:68, q0 + 512:q0 + 1024],
                                    start=True, stop=True)
                r.on("tensor", s_fn, inc="pe", waits=wv)
                if rel >= 0:
                    def msk_fn(v, rel=rel, sdst=sdst):
                        li = None
                        if rel < 4:
                            li = v.tensor_tensor(sdst[0], sdst[0],
                                                 MC[:, rel * 512:(rel + 1) * 512],
                                                 ALU.add)
                        else:
                            v.tensor_tensor(sdst[0], sdst[0],
                                            MC[:, 4 * 512:5 * 512], ALU.add)
                            li = v.tensor_tensor(sdst[1], sdst[1],
                                                 MC[:, (rel - 4) * 512:(rel - 3) * 512],
                                                 ALU.add)
                        return li
                    r.on("vector", msk_fn, inc="dve")

                def exp_fn(s, sdst=sdst, pt=pt, odd=odd):
                    if odd:
                        s.activation(pt[:, 0:512], sdst[0], AF.Exp)
                        return s.activation(pt[:, 512:1024], sdst[1], AF.Exp)
                    return s.activation(pt[:], ps_S[:], AF.Exp)
                r.on("scalar", exp_fn, inc="act")
                exp_c.append(r.v("act"))

                def pv_fn(t, kb=kb, pt=pt, first=(kb == 0), last=(kb == nb - 1)):
                    t.matmul(ps_O[:, 0:512], VT[:, kb * 65:kb * 65 + 65],
                             pt[:, 0:512], start=first, stop=last)
                    return t.matmul(ps_O[:, 512:1024], VT[:, kb * 65:kb * 65 + 65],
                                    pt[:, 512:1024], start=first, stop=last)
                r.on("tensor", pv_fn, inc="pe")
                pv_c.append(r.v("pe"))

            r.on("vector", lambda v: v.reciprocal(RCP[0:1, :], ps_O[64:65, :]),
                 inc="dve")
            r.on("scalar", lambda s: s.copy(OSB[:], ps_O[0:64, :]), inc="act")
            for hf in range(2):
                r.on("tensor", lambda t, hf=hf: t.matmul(
                    ps_bc[0:64, 0:512], tiles["ones1x128"][0:1, 0:64],
                    RCP[0:1, hf * 512:(hf + 1) * 512], start=True, stop=True), inc="pe")
                r.on("vector", lambda v, hf=hf: v.tensor_tensor(
                    OST[:, hf * 512:(hf + 1) * 512], OSB[:, hf * 512:(hf + 1) * 512],
                    ps_bc[0:64, 0:512], ALU.mult), inc="dve")
            lc, cs = qt2 // 2, qt2 % 2
            r.dma("sync", o_o[lc * 256 + p_ * 64:lc * 256 + (p_ + 1) * 64,
                              cs * 1024:(cs + 1) * 1024], OST[:], inc="st")

    r.on("sync", lambda e: e.nop(), waits=r.snap())
    if not ext:
        sc.close()
    if ext:
        with nc.Block() as block:
            r.finalize(block)
    return nc


def attn_core_inputs(qf, kf, vf, chif, clof, c, W):
    cst = consts()
    b = c // 4
    h0 = (c % 4) * 4
    hs = list(range(h0, h0 + 4))
    q4 = np.concatenate([qf[b, h * 64:(h + 1) * 64, :] for h in hs], 0)
    k4 = np.concatenate([kf[b, h * 64:(h + 1) * 64, :] for h in hs], 0)
    v4 = np.concatenate([vf[b, :, h * 64:(h + 1) * 64] for h in hs], 0)
    ht = np.arange(16) % 4
    m = {
        "q4": np.ascontiguousarray(q4),
        "k4": np.ascontiguousarray(k4),
        "v4": np.ascontiguousarray(v4),
        "chi4": np.ascontiguousarray(chif[hs]),
        "clo4": np.ascontiguousarray(clof[hs]),
        "logxi4": np.asarray(W["L0_attn_log_xi"], np.float32).reshape(1, 16)[:, hs],
        "ebits4": np.ascontiguousarray(cst["ebits"].reshape(1, 16)[:, hs]),
        "jrow": cst["jrow"],
    }
    m.update(const_in_map(["ones128", "ones1x128", "kpre", "qpre", "maskC"]))
    return m



def build_attn_proj(ctx=None, pfx="", o_fill=None):
    nc, r, sc, ext = _ctx_open(ctx, pfx)

    x_i = _inp(ctx, nc, "x", (D, T))
    o_i = _inp(ctx, nc, "o", (D, T), BF16) if o_fill is None else None
    wo_i = _inp(ctx, nc, "wo", (D, D))
    xo = _outp(ctx, nc, "xout", (D, T))

    X = sc.sb("X", [128, 8 * T], F32)
    O = sc.sb("O", [128, 8 * T], BF16)
    WC = sc.sb("WC", [128, 8 * 128], BF16)
    ps = sc.ps("ps", [128, 512], F32)

    r.dma("sync", X[:].rearrange("p (c w) -> p c w", w=T),
          x_i[:].rearrange("(c p) w -> p c w", p=128))
    if o_fill is None:
        r.dma("sync", O[:].rearrange("p (c w) -> p c w", w=T),
              o_i[:].rearrange("(c p) w -> p c w", p=128))
    else:
        o_fill(O)
    for f in range(8):
        r.dma("gpsimd", WC[:].rearrange("p (c w) -> p c w", w=128),
              wo_i[:, f * 128:(f + 1) * 128].rearrange("(c p) w -> p c w", p=128))
        for it in range(4):
            j0 = it * 512

            def mm_fn(t, f=f, j0=j0):
                for c in range(8):
                    li = t.matmul(ps[:], WC[:, c * 128:(c + 1) * 128],
                                  O[:, c * T + j0:c * T + j0 + 512],
                                  start=(c == 0), stop=(c == 7))
                return li
            r.on("tensor", mm_fn, inc="pe")
            r.on("vector", lambda v, f=f, j0=j0: v.tensor_tensor(
                X[:, f * T + j0:f * T + j0 + 512],
                X[:, f * T + j0:f * T + j0 + 512], ps[:], ALU.add), inc="dve")

    r.dma("sync", xo[:].rearrange("(c p) w -> p c w", p=128),
          X[:].rearrange("p (c w) -> p c w", w=T), inc="st")
    r.on("sync", lambda e: e.nop(), waits=r.snap())
    if not ext:
        sc.close()
    if ext:
        with nc.Block() as block:
            r.finalize(block)
    return nc


def attn_proj_inputs(x_fm, o_fm, c, W):
    return {
        "x": _slice_own(x_fm, c),
        "o": _slice_own_bf(o_fm, c),
        "wo": W["L0_attn_w_o"],
    }


def _slice_own_bf(x_fm, c):
    b, t0 = core_pos(c)
    return np.ascontiguousarray(x_fm[b, :, t0:t0 + T])



def build_l1ffn(ctx=None, pfx=""):
    nc, r, sc, ext = _ctx_open(ctx, pfx)

    x_i = _inp(ctx, nc, "x", (D, T))
    w1_i = _inp(ctx, nc, "w1", (D, 4096))
    w2_i = _inp(ctx, nc, "w2", (D, 4096))
    h1_i = _inp(ctx, nc, "h1", (4096, 4096))
    h2_i = _inp(ctx, nc, "h2", (4096, 4096))
    w3_i = _inp(ctx, nc, "w3", (4096, D))
    s1_i = _inp(ctx, nc, "s1", (128, 32))
    s2_i = _inp(ctx, nc, "s2", (128, 32))
    sh1_i = _inp(ctx, nc, "sh1", (128, 32))
    sh2_i = _inp(ctx, nc, "sh2", (128, 32))
    s3_i = _inp(ctx, nc, "s3", (128, 8))
    xo = _outp(ctx, nc, "xout", (D, T))

    X = sc.sb("X", [128, 8 * T], F32)
    Z = sc.sb("Z", [128, 8 * 512], BF16)
    HA = sc.sb("HA", [128, 32 * 512], BF16)
    HBB = sc.sb("HB2", [128, 32 * 512], BF16)
    WCA = sc.sb("WCA", [128, 8 * 128], BF16)
    WCB = sc.sb("WCB", [128, 8 * 128], BF16)
    WH = sc.sb("WH", [128, 32 * 128], BF16)
    SA = sc.sb("SA", [128, 512], BF16)
    S1 = sc.sb("S1", [128, 32], F32)
    S2 = sc.sb("S2", [128, 32], F32)
    SH1 = sc.sb("SH1", [128, 32], F32)
    SH2 = sc.sb("SH2", [128, 32], F32)
    S3H = sc.sb("S3H", [128, 8], F32)

    ps_a = sc.ps("ps_a", [128, 512], F32)
    ps_g = sc.ps("ps_g", [128, 512], F32)

    kit = ctx["kit"] if ctx else LNKit(nc)
    tiles, caps = (ctx["tiles"], ctx["caps"]) if ctx else load_consts(r, nc, ["ones128", "ones1x128"])

    r.dma("sync", X[:].rearrange("p (c w) -> p c w", w=T),
          x_i[:].rearrange("(c p) w -> p c w", p=128))
    r.dma("sync", S1[:], s1_i[:])
    r.dma("sync", S2[:], s2_i[:])
    r.dma("sync", SH1[:], sh1_i[:])
    r.dma("sync", SH2[:], sh2_i[:])
    r.dma("sync", S3H[:], s3_i[:])
    r.on("vector", lambda v: v.tensor_scalar_mul(S3H[:], S3H[:], 0.5), inc="dve")

    for tt_ in range(4):
        tb = tt_ * 512

        def sl_in(c, j0, w, tb=tb):
            return X[:, c * T + tb + j0: c * T + tb + j0 + w]

        def sl_out(c, j0, w):
            return Z[:, c * 512 + j0: c * 512 + j0 + w]
        rec_ln(r, kit, tiles, sl_in, sl_out, 512)

        for oc in range(32):
            r.dma("gpsimd", WCA[:].rearrange("p (c w) -> p c w", w=128),
                  w1_i[:, oc * 128:(oc + 1) * 128].rearrange("(c p) w -> p c w", p=128))
            r.dma("gpsimd", WCB[:].rearrange("p (c w) -> p c w", w=128),
                  w2_i[:, oc * 128:(oc + 1) * 128].rearrange("(c p) w -> p c w", p=128))

            def mm1_fn(t):
                for c in range(8):
                    t.matmul(ps_a[:], WCA[:, c * 128:(c + 1) * 128],
                             Z[:, c * 512:(c + 1) * 512], start=(c == 0), stop=(c == 7))
                for c in range(8):
                    li = t.matmul(ps_g[:], WCB[:, c * 128:(c + 1) * 128],
                                  Z[:, c * 512:(c + 1) * 512], start=(c == 0), stop=(c == 7))
                return li
            r.on("tensor", mm1_fn, inc="pe")
            r.on("scalar", lambda s, oc=oc: s.activation(
                SA[:], ps_a[:], AF.Silu, scale=S1[:, oc:oc + 1]), inc="act")
            r.on("vector", lambda v, oc=oc: v.scalar_tensor_tensor(
                HA[:, oc * 512:(oc + 1) * 512], ps_g[:], S2[:, oc:oc + 1],
                SA[:], ALU.mult, ALU.mult), inc="dve")

        for oc in range(32):
            r.dma("gpsimd", WH[:].rearrange("p (c w) -> p c w", w=128),
                  h1_i[:, oc * 128:(oc + 1) * 128].rearrange("(c p) w -> p c w", p=128))

            def mm2_fn(t):
                for c in range(32):
                    li = t.matmul(ps_a[:], WH[:, c * 128:(c + 1) * 128],
                                  HA[:, c * 512:(c + 1) * 512],
                                  start=(c == 0), stop=(c == 31))
                return li
            r.on("tensor", mm2_fn, inc="pe")
            r.on("scalar", lambda s, oc=oc: s.activation(
                HBB[:, oc * 512:(oc + 1) * 512], ps_a[:], AF.Gelu,
                scale=SH1[:, oc:oc + 1]), inc="act")

        for oc in range(32):
            r.dma("gpsimd", WH[:].rearrange("p (c w) -> p c w", w=128),
                  h2_i[:, oc * 128:(oc + 1) * 128].rearrange("(c p) w -> p c w", p=128))

            def mm3_fn(t):
                for c in range(32):
                    li = t.matmul(ps_a[:], WH[:, c * 128:(c + 1) * 128],
                                  HBB[:, c * 512:(c + 1) * 512],
                                  start=(c == 0), stop=(c == 31))
                return li
            r.on("tensor", mm3_fn, inc="pe")
            r.on("scalar", lambda s, oc=oc: s.activation(
                HA[:, oc * 512:(oc + 1) * 512], ps_a[:], AF.Gelu,
                scale=SH2[:, oc:oc + 1]), inc="act")

        for f in range(8):
            r.dma("gpsimd", WH[:, 0:32 * 128].rearrange("p (c w) -> p c w", w=128),
                  w3_i[:, f * 128:(f + 1) * 128].rearrange("(c p) w -> p c w", p=128))

            def mm4_fn(t):
                for c in range(32):
                    li = t.matmul(ps_a[:], WH[:, c * 128:(c + 1) * 128],
                                  HA[:, c * 512:(c + 1) * 512],
                                  start=(c == 0), stop=(c == 31))
                return li
            r.on("tensor", mm4_fn, inc="pe")
            r.on("vector", lambda v, f=f, tb=tb: v.scalar_tensor_tensor(
                X[:, f * T + tb:f * T + tb + 512], ps_a[:], S3H[:, f:f + 1],
                X[:, f * T + tb:f * T + tb + 512], ALU.mult, ALU.add), inc="dve")

    r.dma("sync", xo[:].rearrange("(c p) w -> p c w", p=128),
          X[:].rearrange("p (c w) -> p c w", w=T), inc="st")
    r.on("sync", lambda e: e.nop(), waits=r.snap())
    if not ext:
        sc.close()
    if ext:
        with nc.Block() as block:
            r.finalize(block)
    return nc


def l1ffn_inputs(x_fm, c, W):
    m = {
        "x": _slice_own(x_fm, c),
        "w1": W["L1_ffn_w1_w_t"],
        "w2": W["L1_ffn_w2_w_t"],
        "h1": W["L1_ffn_h1_w_t"],
        "h2": W["L1_ffn_h2_w_t"],
        "w3": W["L1_ffn_w3_w_t"],
        "s1": _pp(W["L1_ffn_w1_s"], 32),
        "s2": _pp(W["L1_ffn_w2_s"], 32),
        "sh1": _pp(W["L1_ffn_h1_s"], 32),
        "sh2": _pp(W["L1_ffn_h2_s"], 32),
        "s3": _pp(W["L1_ffn_w3_s"], 8),
    }
    m.update(const_in_map(["ones128", "ones1x128"]))
    return m



def build_l1attn(ctx=None, pfx=""):
    nc, r, sc, ext = _ctx_open(ctx, pfx)

    x_i = _inp(ctx, nc, "x", (D, T))
    ln1g_i = _inp(ctx, nc, "ln1g", (128, 8))
    ln1b_i = _inp(ctx, nc, "ln1b", (128, 8))
    ln2g_i = _inp(ctx, nc, "ln2g", (128, 8))
    ln2b_i = _inp(ctx, nc, "ln2b", (128, 8))
    wq_i = _inp(ctx, nc, "wq", (D, D))
    wk_i = _inp(ctx, nc, "wk", (D, D))
    wv_i = _inp(ctx, nc, "wv", (D, D))
    wo_i = _inp(ctx, nc, "wo", (D, D))
    qb_i = _inp(ctx, nc, "qb", (1, D))
    kb_i = _inp(ctx, nc, "kb", (1, D))
    vb_i = _inp(ctx, nc, "vb", (1, D))
    ob_i = _inp(ctx, nc, "ob", (128, 8))
    fc_i = _inp(ctx, nc, "fc", (D, 4096))
    fcb_i = _inp(ctx, nc, "fcb", (128, 32))
    pj_i = _inp(ctx, nc, "pj", (4096, D))
    pjb_i = _inp(ctx, nc, "pjb", (128, 8))
    id_i = _inp(ctx, nc, "c_ident", (128, 128))
    xo = _outp(ctx, nc, "xout", (D, T))

    X = sc.sb("X", [128, 8 * T], F32)
    ZARENA = sc.sb("ZARENA", [128, 20 * 1024], BF16)
    Z = ZARENA[:, 0:8 * T]
    ZT = ZARENA[:, 0:8 * 512]
    GT = ZARENA[:, 4096:4096 + 32 * 512]
    WAR = sc.sb("WAR", [128, 24 * 1024], BF16)
    OW = sc.sb("OW", [128, 8 * 128], BF16)
    LN1G = sc.sb("LN1G", [128, 8], F32)
    LN1B = sc.sb("LN1B", [128, 8], F32)
    LN2G = sc.sb("LN2G", [128, 8], F32)
    LN2B = sc.sb("LN2B", [128, 8], F32)
    OBP = sc.sb("OBP", [128, 8], F32)
    FCB = sc.sb("FCB", [128, 32], F32)
    PJB = sc.sb("PJB", [128, 8], F32)
    BB = sc.sb("BB", [128, 3 * D], F32)
    QTok = sc.sb("QTok", [128, D], BF16)
    KTok = sc.sb("KTok", [128, D], BF16)
    VTok = sc.sb("VTok", [128, D], BF16)
    PRD = sc.sb("PRD", [128, D], F32)
    AA = sc.sb("AA", [128, 256], F32)
    DN = sc.sb("DN", [128, 32], F32)
    OACC = sc.sb("OACC", [128, D], F32)
    ATok = sc.sb("ATok", [128, D], BF16)
    AFM = sc.sb("AFM", [128, D], BF16)
    IDB = sc.sb("IDB", [128, 128], BF16)

    ps_a = sc.ps("ps_a", [128, 512], F32)
    ps_tr = sc.ps("ps_tr", [128, 128], BF16)

    kit = ctx["kit"] if ctx else LNKit(nc)
    tiles, caps = (ctx["tiles"], ctx["caps"]) if ctx else load_consts(r, nc, ["ones128", "ones1x128"])

    r.dma("sync", X[:].rearrange("p (c w) -> p c w", w=T),
          x_i[:].rearrange("(c p) w -> p c w", p=128))
    for tin, tsb in ((ln1g_i, LN1G), (ln1b_i, LN1B), (ln2g_i, LN2G),
                     (ln2b_i, LN2B), (ob_i, OBP), (fcb_i, FCB), (pjb_i, PJB)):
        r.dma("sync", tsb[:], tin[:])
    r.dma("sync", BB[:, 0:D], qb_i[:].broadcast_to([128, D]))
    r.dma("sync", BB[:, D:2 * D], kb_i[:].broadcast_to([128, D]))
    r.dma("sync", BB[:, 2 * D:3 * D], vb_i[:].broadcast_to([128, D]))
    r.dma("gpsimd", IDB[:], id_i[:])
    SCQ = kit.sq[:, 0:512]
    SCR = kit.sq[:, 512:1024]
    SCM = kit.sq[:, 1024:1536]

    rec_ln_stream(r, kit, tiles, X, T, T)

    def zaff_fn(v):
        for c in range(8):
            li = v.tensor_scalar(Z[:, c * T:(c + 1) * T], X[:, c * T:(c + 1) * T],
                                 LN1G[:, c:c + 1], LN1B[:, c:c + 1],
                                 ALU.mult, ALU.add)
        return li
    r.on("vector", zaff_fn, inc="dve")

    for wi, base in ((wq_i, 0), (wk_i, 8), (wv_i, 16)):
        r.dma("gpsimd", WAR[:, base * 1024:(base + 8) * 1024]
              .rearrange("p (c w) -> p c w", w=1024),
              wi[:].rearrange("(c p) w -> p c w", p=128))

    for tc in range(16):
        tb = tc * 128
        for pi_, (wbase, bseg, dst, do_elu) in enumerate(
                ((0, 0, QTok, True), (8, 1, KTok, True), (16, 2, VTok, False))):
            for half in range(2):
                def mm_fn(t, wbase=wbase, half=half, tb=tb):
                    for c in range(8):
                        li = t.matmul(ps_a[:],
                                      Z[:, c * T + tb:c * T + tb + 128],
                                      WAR[:, (wbase + c) * 1024 + half * 512:
                                          (wbase + c) * 1024 + (half + 1) * 512],
                                      start=(c == 0), stop=(c == 7))
                    return li
                r.on("tensor", mm_fn, inc="pe")

                def bias_fn(v, bseg=bseg, half=half):
                    return v.tensor_tensor(
                        SCQ[:], ps_a[:],
                        BB[:, bseg * 1024 + half * 512:bseg * 1024 + (half + 1) * 512],
                        ALU.add)
                r.on("vector", bias_fn, inc="dve")
                if do_elu:
                    def elu1_fn(v):
                        v.tensor_scalar_max(SCR[:], SCQ[:], 0.0)
                        return v.tensor_tensor(SCM[:], SCQ[:], SCR[:], ALU.subtract)
                    r.on("vector", elu1_fn, inc="dve")
                    r.on("scalar", lambda s: s.activation(SCM[:], SCM[:], AF.Exp),
                         inc="act")
                    r.on("vector", lambda v, dst=dst, half=half: v.tensor_tensor(
                        dst[:, half * 512:(half + 1) * 512], SCR[:], SCM[:],
                        ALU.add), inc="dve")
                else:
                    r.on("vector", lambda v, dst=dst, half=half: v.tensor_scalar_mul(
                        dst[:, half * 512:(half + 1) * 512], SCQ[:], 1.0), inc="dve")

        Q3 = QTok[:].rearrange("p (h d) -> p h d", d=64)
        K3 = KTok[:].rearrange("p (h d) -> p h d", d=64)
        V3 = VTok[:].rearrange("p (h d) -> p h d", d=64)
        P3 = PRD[:].rearrange("p (h d) -> p h d", d=64)
        A3 = AA[:].rearrange("p (h g) -> p h g", g=16)
        for h in range(16):
            def amul_fn(v, h=h):
                v.tensor_tensor(P3, K3, Q3[:, h:h + 1, :].broadcast_to([128, 16, 64]),
                                ALU.mult)
                return v.tensor_reduce(A3[:, h:h + 1, :].transpose([0, 2, 1])
                                       if False else
                                       AA[:, h * 16:(h + 1) * 16]
                                       .rearrange("p (g o) -> p g o", o=1),
                                       P3, AX, ALU.add)
            r.on("vector", amul_fn, inc="dve")
        r.on("vector", lambda v: v.tensor_reduce(
            DN[:, 0:16].rearrange("p (h o) -> p h o", o=1),
            AA[:].rearrange("p (h g) -> p h g", g=16), AX, ALU.add), inc="dve")
        r.on("vector", lambda v: v.reciprocal(DN[:, 16:32], DN[:, 0:16]), inc="dve")
        for g in range(16):
            def oacc_fn(v, g=g):
                src = AA[:].rearrange("p (h g) -> p h g", g=16)[:, :, g:g + 1] \
                    .broadcast_to([128, 16, 64])
                vv = V3[:, g:g + 1, :].broadcast_to([128, 16, 64])
                o3 = OACC[:].rearrange("p (h d) -> p h d", d=64)
                if g == 0:
                    return v.tensor_tensor(o3, src, vv, ALU.mult)
                v.tensor_tensor(P3, src, vv, ALU.mult)
                return v.tensor_tensor(OACC[:], OACC[:], PRD[:], ALU.add)
            r.on("vector", oacc_fn, inc="dve")
        r.on("vector", lambda v: v.tensor_tensor(
            ATok[:].rearrange("p (h d) -> p h d", d=64),
            OACC[:].rearrange("p (h d) -> p h d", d=64),
            DN[:, 16:32].rearrange("p (h o) -> p h o", o=1)
            .broadcast_to([128, 16, 64]), ALU.mult), inc="dve")

        for c8 in range(8):
            r.on("tensor", lambda t, c8=c8: t.transpose(
                ps_tr[:], ATok[:, c8 * 128:(c8 + 1) * 128], IDB[:]), inc="pe")
            r.on("scalar", lambda s, c8=c8: s.copy(
                AFM[:, c8 * 128:(c8 + 1) * 128], ps_tr[:]), inc="act")
        for f in range(8):
            r.dma("gpsimd", OW[:].rearrange("p (c w) -> p c w", w=128),
                  wo_i[:, f * 128:(f + 1) * 128].rearrange("(c p) w -> p c w", p=128))

            def opj_fn(t, f=f):
                for c in range(8):
                    li = t.matmul(ps_a[0:128, 0:128],
                                  OW[:, c * 128:(c + 1) * 128],
                                  AFM[:, c * 128:(c + 1) * 128],
                                  start=(c == 0), stop=(c == 7))
                return li
            r.on("tensor", opj_fn, inc="pe")
            r.on("vector", lambda v, f=f, tb=tb: v.scalar_tensor_tensor(
                X[:, f * T + tb:f * T + tb + 128], ps_a[0:128, 0:128],
                OBP[:, f:f + 1], X[:, f * T + tb:f * T + tb + 128],
                ALU.add, ALU.add), inc="dve")

    for tt_ in range(4):
        tb = tt_ * 512

        def sl_in(c, j0, w, tb=tb):
            return X[:, c * T + tb + j0: c * T + tb + j0 + w]

        def sl_out(c, j0, w):
            return ZT[:, c * 512 + j0: c * 512 + j0 + w]
        rec_ln(r, kit, tiles, sl_in, sl_out, 512, gb=(LN2G, LN2B))

        for oc in range(32):
            r.dma("gpsimd", WAR[:, 0:8 * 128].rearrange("p (c w) -> p c w", w=128),
                  fc_i[:, oc * 128:(oc + 1) * 128].rearrange("(c p) w -> p c w", p=128))

            def fc_fn(t):
                for c in range(8):
                    li = t.matmul(ps_a[:], WAR[:, c * 128:(c + 1) * 128],
                                  ZT[:, c * 512:(c + 1) * 512],
                                  start=(c == 0), stop=(c == 7))
                return li
            r.on("tensor", fc_fn, inc="pe")
            r.on("scalar", lambda s, oc=oc: s.activation(
                GT[:, oc * 512:(oc + 1) * 512], ps_a[:], AF.Gelu,
                bias=FCB[:, oc:oc + 1]), inc="act")

        for f in range(8):
            r.dma("gpsimd", WAR[:, 0:32 * 128].rearrange("p (c w) -> p c w", w=128),
                  pj_i[:, f * 128:(f + 1) * 128].rearrange("(c p) w -> p c w", p=128))

            def pj_fn(t):
                for c in range(32):
                    li = t.matmul(ps_a[:], WAR[:, c * 128:(c + 1) * 128],
                                  GT[:, c * 512:(c + 1) * 512],
                                  start=(c == 0), stop=(c == 31))
                return li
            r.on("tensor", pj_fn, inc="pe")
            r.on("vector", lambda v, f=f, tb=tb: v.scalar_tensor_tensor(
                X[:, f * T + tb:f * T + tb + 512], ps_a[:], PJB[:, f:f + 1],
                X[:, f * T + tb:f * T + tb + 512], ALU.add, ALU.add), inc="dve")

    r.on("gpsimd", lambda g: g.dma_start(
        out=X[:].rearrange("p (c w) -> p c w", w=T),
        in_=x_i[:].rearrange("(c p) w -> p c w", p=128),
        accum_op=ALU.add), inc="wdma", inc_by=16)
    r.dma("sync", xo[:].rearrange("(c p) w -> p c w", p=128),
          X[:].rearrange("p (c w) -> p c w", w=T), inc="st")
    r.on("sync", lambda e: e.nop(), waits=r.snap())
    if not ext:
        sc.close()
    if ext:
        with nc.Block() as block:
            r.finalize(block)
    return nc


def l1attn_inputs(x_fm, c, W):
    m = {
        "x": _slice_own(x_fm, c),
        "ln1g": _pp(W["L1_attn_ln1_w"]),
        "ln1b": _pp(W["L1_attn_ln1_b"]),
        "ln2g": _pp(W["L1_attn_ln2_w"]),
        "ln2b": _pp(W["L1_attn_ln2_b"]),
        "wq": W["L1_attn_q_w_t"],
        "wk": W["L1_attn_k_w_t"],
        "wv": W["L1_attn_v_w_t"],
        "wo": W["L1_attn_o_w_t"],
        "qb": np.asarray(W["L1_attn_q_b"], np.float32).reshape(1, D),
        "kb": np.asarray(W["L1_attn_k_b"], np.float32).reshape(1, D),
        "vb": np.asarray(W["L1_attn_v_b"], np.float32).reshape(1, D),
        "ob": _pp(W["L1_attn_o_b"]),
        "fc": W["L1_attn_fc_w_t"],
        "fcb": _pp(W["L1_attn_fc_b"], 32),
        "pj": W["L1_attn_proj_w_t"],
        "pjb": _pp(W["L1_attn_proj_b"]),
        "c_ident": consts()["ident"],
    }
    m.update(const_in_map(["ones128", "ones1x128"]))
    return m



def build_fused(upto=99):
    nc = _new_nc()
    r = Rec(nc)
    cst = consts()
    kit = LNKit(nc)

    r.salt = "ld_"
    tiles, _unused = load_consts(r, nc, ["ones128", "ones1x128", "head2",
                                         "head2exp"])
    capd = {}
    for n_ in ["kpre", "qpre", "ident", "maskL", "maskC"]:
        capd["c_" + n_] = _din(nc, "c_" + n_, cst[n_].shape, F32)

    x_i = _din(nc, "x", (D, T))
    srch1_i = _din(nc, "srch1", (D, HALO))
    gate_i = _din(nc, "gate", (128, 1))
    ipv_i = _din(nc, "ipv", (128, 8), I32)
    qki_i = _din(nc, "qki", (64, 16), I32)
    vdi_i = _din(nc, "vdi", (128, 256), I32)
    odi_i = _din(nc, "odi", (128, 8), I32)
    xo = _dout(nc, "xout", (D, T))

    GATE = _sb(nc, "GATEf", [128, 1], F32)
    IPV = _sb(nc, "IPVf", [128, 8], I32)
    QKI = _sb(nc, "QKIf", [64, 16], I32)
    VDI = _sb(nc, "VDIf", [128, 256], I32)
    ODI = _sb(nc, "ODIf", [128, 8], I32)
    for tsb, tin in ((GATE, gate_i), (IPV, ipv_i), (QKI, qki_i), (VDI, vdi_i),
                     (ODI, odi_i)):
        r.dma("sync", tsb[:], tin[:])

    WA = {}
    for l in range(2):
        WA[f"wqkv{l}"] = _din(nc, f"wqkv{l}", (D, 1536))
        WA[f"wo{l}"] = _din(nc, f"wo{l}", (512, D))
        WA[f"qs{l}"] = _din(nc, f"qs{l}", (128, 1))
        WA[f"ks{l}"] = _din(nc, f"ks{l}", (128, 1))
        WA[f"lng{l}"] = _din(nc, f"lng{l}", (128, 8))
        WA[f"lnb{l}"] = _din(nc, f"lnb{l}", (128, 8))
        WA[f"w1{l}"] = _din(nc, f"w1{l}", (D, 5460))
        WA[f"b1a{l}"] = _din(nc, f"b1a{l}", (128, 22))
        WA[f"b1g{l}"] = _din(nc, f"b1g{l}", (128, 22))
        WA[f"w2{l}"] = _din(nc, f"w2{l}", (2730, D))
        WA[f"b2{l}"] = _din(nc, f"b2{l}", (128, 8))
    WA["awqkv"] = _din(nc, "awqkv", (D, 3072))
    WA["awo"] = _din(nc, "awo", (D, D))
    WA["logxi"] = _din(nc, "logxi", (16, 1))
    WA["ebits"] = _din(nc, "ebits", (16, 1))
    WA["j16"] = _din(nc, "j16", (16, T))
    WA["jrow"] = _din(nc, "jrow", (1, SEQ))
    WA["logxi4"] = _din(nc, "logxi4", (1, 4))
    WA["ebits4"] = _din(nc, "ebits4", (1, 4))
    for n_ in ["aq1", "aq2", "ak1", "ak2", "ffq1", "ffq2"]:
        WA[n_] = _din(nc, n_, (64, T))
    for n_ in ["ffka1", "ffka2", "ffkb1", "ffkb2"]:
        WA[n_] = _din(nc, n_, (64, 2560))
    WA["mW0"] = _din(nc, "mW0", (128, 4 * 512))
    for n_ in ["f_w1", "f_w2"]:
        WA[n_] = _din(nc, n_, (D, 4096))
    for n_ in ["f_h1", "f_h2"]:
        WA[n_] = _din(nc, n_, (4096, 4096))
    WA["f_w3"] = _din(nc, "f_w3", (4096, D))
    for n_ in ["f_s1", "f_s2", "f_sh1", "f_sh2"]:
        WA[n_] = _din(nc, n_, (128, 32))
    WA["f_s3"] = _din(nc, "f_s3", (128, 8))
    for n_ in ["ln1g", "ln1b", "ln2g", "ln2b", "ob", "pjb"]:
        WA["a_" + n_] = _din(nc, "a_" + n_, (128, 8))
    WA["a_fcb"] = _din(nc, "a_fcb", (128, 32))
    for n_ in ["wq", "wk", "wv", "wo"]:
        WA["a_" + n_] = _din(nc, "a_" + n_, (D, D))
    for n_ in ["qb", "kb", "vb"]:
        WA["a_" + n_] = _din(nc, "a_" + n_, (1, D))
    WA["a_fc"] = _din(nc, "a_fc", (D, 4096))
    WA["a_pj"] = _din(nc, "a_pj", (4096, D))

    xs = nc.dram_tensor("xs", [D, T], F32)
    ud = nc.dram_tensor("ud", [D, T], F32)
    ccA = {}
    for i_ in range(3):
        ccA[i_] = (nc.dram_tensor(f"ccA_in{i_}", [D, 513], F32),
                   nc.dram_tensor(f"ccA_out{i_}", [8 * D, 513], F32))
    ccB = {}
    for i_ in range(4):
        ccB[i_] = (nc.dram_tensor(f"ccB_in{i_}", [D, 1], F32),
                   nc.dram_tensor(f"ccB_out{i_}", [8 * D, 1], F32))
    ccQ_in = nc.dram_tensor("ccQ_in", [D, T], BF16)
    ccQ_out = nc.dram_tensor("ccQ_out", [8 * D, T], BF16)
    ccK_in = nc.dram_tensor("ccK_in", [D, T], BF16)
    ccK_out = nc.dram_tensor("ccK_out", [8 * D, T], BF16)
    ccV_in = nc.dram_tensor("ccV_in", [16 * T, 64], BF16)
    ccV_out = nc.dram_tensor("ccV_out", [8 * 16 * T, 64], BF16)
    ccC_in = nc.dram_tensor("ccC_in", [32, T], BF16)
    ccO_in = nc.dram_tensor("ccO_in", [1024, T], BF16)
    ccO_out = nc.dram_tensor("ccO_out", [8 * 1024, T], BF16)

    ctx = {"nc": nc, "r": r, "kit": kit, "tiles": tiles, "caps": capd,
           "aps": None}

    def ag(cin, cout):
        r.on("gpsimd", lambda g: g.collective_compute(
            "AllGather", ALU.bypass, replica_groups=[list(range(8))],
            ins=[cin.ap().opt()], outs=[cout.ap().opt()]), inc="pool")

    def tails513(stream_ap, buf):
        cin, cout = ccA[buf]
        r.dma("gpsimd", cin.ap(), stream_ap[:, T - 513:T])
        ag(cin, cout)

    def mk_halo_fill(buf):
        cout = ccA[buf][1]

        def halo_fill(UH):
            for c8 in range(8):
                r.on("gpsimd", lambda g, c8=c8, UH=UH: g.indirect_dma_start(
                    UH[:, c8 * 513:(c8 + 1) * 513], None, cout.ap(),
                    IndirectOffsetOnAxis(IPV[:, c8:c8 + 1], 0)),
                    inc="wdma", inc_by=16)
            r.on("vector", lambda v, UH=UH: v.tensor_scalar_mul(
                UH[:], UH[:], GATE[:, 0:1]), inc="dve")
        return halo_fill

    def tails1(stream_ap, buf):
        cin, cout = ccB[buf]
        r.dma("gpsimd", cin.ap(), stream_ap[:, T - 1:T], slow=True)
        ag(cin, cout)

    def mk_prev_fill(buf):
        cout = ccB[buf][1]

        def prev_fill(U):
            UW = T + 1
            for c8 in range(8):
                r.on("gpsimd", lambda g, c8=c8, U=U: g.indirect_dma_start(
                    U[:, c8 * UW:c8 * UW + 1], None, cout.ap(),
                    IndirectOffsetOnAxis(IPV[:, c8:c8 + 1], 0)),
                    inc="wdma", inc_by=16)
            r.on("vector", lambda v, U=U: v.tensor_scalar_mul(
                U[:].rearrange("p (c w) -> p c w", w=UW)[:, :, 0:1],
                U[:].rearrange("p (c w) -> p c w", w=UW)[:, :, 0:1],
                GATE[:, 0:1]), inc="dve")
        return prev_fill

    def l0_aps(l, src_ap, umid_ap):
        return {"src": src_ap, "wqkv": WA[f"wqkv{l}"], "wo": WA[f"wo{l}"],
                "qs": WA[f"qs{l}"], "ks": WA[f"ks{l}"], "mW0": WA["mW0"],
                "ffq1": WA["ffq1"], "ffq2": WA["ffq2"],
                "ffka1": WA["ffka1"], "ffka2": WA["ffka2"],
                "ffkb1": WA["ffkb1"], "ffkb2": WA["ffkb2"],
                "c_kpre": capd["c_kpre"], "c_ident": capd["c_ident"],
                "c_maskL": capd["c_maskL"], "umid": umid_ap}

    def b_aps(l, u_ap, out_key, out_ap, x_ap=None):
        m = {"u": u_ap, "lng": WA[f"lng{l}"], "lnb": WA[f"lnb{l}"],
             "w1": WA[f"w1{l}"], "b1a": WA[f"b1a{l}"], "b1g": WA[f"b1g{l}"],
             "w2": WA[f"w2{l}"], "b2": WA[f"b2{l}"], out_key: out_ap}
        if x_ap is not None:
            m["x"] = x_ap
        return m

    def ffn_phase(tag, x_ap, xout_ap, first_srch, bufbase):
        r.salt = tag + "a0_"
        ap0 = l0_aps(0, x_ap, ud.ap())
        if first_srch is not None:
            ap0["srch"] = first_srch
            ctx["aps"] = ap0
            build_ffnA(True, ctx=ctx, pfx=tag + "a0")
        else:
            tails513(x_ap, 2)
            ctx["aps"] = ap0
            build_ffnA(True, ctx=ctx, pfx=tag + "a0", halo_fill=mk_halo_fill(2))
        r.salt = tag + "b0_"
        tails1(ud.ap(), bufbase)
        ctx["aps"] = b_aps(0, ud.ap(), "uout", ud.ap())
        build_ffnB(False, ctx=ctx, pfx=tag + "b0", prev_fill=mk_prev_fill(bufbase))
        r.salt = tag + "a1_"
        tails513(ud.ap(), bufbase // 2)
        ctx["aps"] = l0_aps(1, ud.ap(), ud.ap())
        build_ffnA(False, ctx=ctx, pfx=tag + "a1", halo_fill=mk_halo_fill(bufbase // 2))
        r.salt = tag + "b1_"
        tails1(ud.ap(), bufbase + 1)
        ctx["aps"] = b_aps(1, ud.ap(), "xout", xout_ap, x_ap=x_ap)
        build_ffnB(True, ctx=ctx, pfx=tag + "b1", prev_fill=mk_prev_fill(bufbase + 1))

    def finish_early():
        r.salt = "zz_"
        r.dma("sync", xo[:], xs.ap()[:], inc="st")
        r.on("sync", lambda e: e.nop(), waits=r.snap())
        with nc.Block() as block:
            r.finalize(block)
        return nc

    ffn_phase("p1", x_i, xs.ap(), srch1_i, 0)
    if upto <= 1:
        return finish_early()

    r.salt = "aq_"
    ctx["aps"] = {"x": xs.ap(), "wqkv": WA["awqkv"], "logxi": WA["logxi"],
                  "ebits": WA["ebits"], "j16": WA["j16"],
                  "aq1": WA["aq1"], "aq2": WA["aq2"], "ak1": WA["ak1"],
                  "ak2": WA["ak2"], "q_o": ccQ_in.ap(), "k_o": ccK_in.ap(),
                  "v_o": ccV_in.ap(), "chi": ccC_in.ap()[0:16, :],
                  "clo": ccC_in.ap()[16:32, :]}
    build_attn_qkv(ctx=ctx, pfx="aq")
    if upto <= 12:
        return finish_early()
    ag(ccQ_in, ccQ_out)
    ag(ccK_in, ccK_out)
    ag(ccV_in, ccV_out)
    if upto <= 14:
        return finish_early()

    asc = Scope(nc, "at_")
    FCSa = asc.sb("FCSaf", [1, T], F32)
    FCSb = asc.sb("FCSbf", [1, T], F32)
    CSTa = asc.sb("CSTaf", [1, T], BF16)
    CSTb = asc.sb("CSTbf", [1, T], BF16)

    def core_fills(p_, KT, QT, VT, JRc, EXIc):
        for s in range(4):
            idx = QKI[0:64, p_ * 4 + s:p_ * 4 + s + 1]
            r.on("gpsimd", lambda g, s=s, idx=idx, KT=KT: g.indirect_dma_start(
                KT[0:64, s * T:(s + 1) * T], None, ccK_out.ap(),
                IndirectOffsetOnAxis(idx, 0)), inc="wdma", inc_by=16)
            r.on("gpsimd", lambda g, s=s, idx=idx, QT=QT: g.indirect_dma_start(
                QT[0:64, s * T:(s + 1) * T], None, ccQ_out.ap(),
                IndirectOffsetOnAxis(idx, 0)), inc="wdma", inc_by=16)
            def crow_fn(v, s=s, p_=p_, JRc=JRc, EXIc=EXIc):
                v.tensor_scalar_mul(FCSa[0:1, :], JRc[0:1, s * T:(s + 1) * T],
                                    EXIc[0:1, 4 + p_:5 + p_])
                v.tensor_scalar_mul(CSTa[0:1, :], FCSa[0:1, :], 1.0)
                v.tensor_scalar_mul(FCSb[0:1, :], CSTa[0:1, :], 1.0)
                v.tensor_tensor(FCSa[0:1, :], FCSa[0:1, :], FCSb[0:1, :],
                                ALU.subtract)
                return v.tensor_scalar_mul(CSTb[0:1, :], FCSa[0:1, :], 1.0)
            r.on("vector", crow_fn, inc="dve")
            r.dma("sync", KT[65:66, s * T:(s + 1) * T], CSTa[0:1, :])
            r.dma("sync", KT[66:67, s * T:(s + 1) * T], CSTb[0:1, :])
        for kb in range(64):
            r.on("gpsimd", lambda g, kb=kb, p_=p_, VT=VT: g.indirect_dma_start(
                VT[:, kb * 65:kb * 65 + 64], None, ccV_out.ap(),
                IndirectOffsetOnAxis(VDI[:, p_ * 64 + kb:p_ * 64 + kb + 1], 0)),
                inc="wdma", inc_by=16)

    r.salt = "ac_"
    ctx["aps"] = {"logxi4": WA["logxi4"], "ebits4": WA["ebits4"],
                  "jrow": WA["jrow"], "o4": ccO_in.ap(),
                  "c_kpre": capd["c_kpre"], "c_qpre": capd["c_qpre"],
                  "c_maskC": capd["c_maskC"]}
    build_attn_core(ctx=ctx, pfx="ac", fills=core_fills)
    if upto <= 16:
        return finish_early()
    ag(ccO_in, ccO_out)
    if upto == 17:
        dbg = _dout(nc, "dbg", (8 * 1024, T), BF16)
        r.dma("sync", dbg[:], ccO_out.ap()[:], inc="st")
        return finish_early()
    if upto == 18:
        dbg = _dout(nc, "dbg", (D, T), BF16)
        OD = _sb(nc, "ODbg", [128, 8 * T], BF16)
        o_fill_dbg = None
        def o_fill18(O):
            for j in range(8):
                idx = ODI[:, j:j + 1]
                r.on("gpsimd", lambda g, j=j, idx=idx, O=O: g.indirect_dma_start(
                    O[:, j * T:(j + 1) * T], None, ccO_out.ap(),
                    IndirectOffsetOnAxis(idx, 0)), inc="wdma", inc_by=16)
        o_fill18(OD)
        r.dma("sync", dbg[:].rearrange("(c p) w -> p c w", p=128),
              OD[:].rearrange("p (c w) -> p c w", w=T), inc="st")
        return finish_early()

    def o_fill(O):
        for j in range(8):
            idx = ODI[:, j:j + 1]
            r.on("gpsimd", lambda g, j=j, idx=idx, O=O: g.indirect_dma_start(
                O[:, j * T:(j + 1) * T], None, ccO_out.ap(),
                IndirectOffsetOnAxis(idx, 0)), inc="wdma", inc_by=16)

    r.salt = "ap_"
    od_dram = nc.dram_tensor("od_dram", [D, T], BF16)
    osc = Scope(nc, "og_")
    OG = osc.sb("OG", [128, 8 * T], BF16)
    o_fill(OG)
    r.dma("sync", od_dram.ap()[:].rearrange("(c p) w -> p c w", p=128),
          OG[:].rearrange("p (c w) -> p c w", w=T), inc="st")
    osc.close()
    if upto == 22:
        dsc = Scope(nc, "dj_")
        OD2 = dsc.sb("OD2", [128, 8 * T], BF16)
        WCD = dsc.sb("WCD", [128, 8 * 128], BF16)
        PSD = dsc.ps("PSD", [128, 512], F32)
        RES = dsc.sb("RES", [128, 512], F32)
        dbg = _dout(nc, "dbg", (128, 512))
        r.dma("sync", OD2[:].rearrange("p (c w) -> p c w", w=T),
              od_dram.ap()[:].rearrange("(c p) w -> p c w", p=128))
        r.dma("gpsimd", WCD[:].rearrange("p (c w) -> p c w", w=128),
              WA["awo"][:, 0:128].rearrange("(c p) w -> p c w", p=128))

        def dmm_fn(t):
            for c in range(8):
                li = t.matmul(PSD[:], WCD[:, c * 128:(c + 1) * 128],
                              OD2[:, c * T:c * T + 512], start=(c == 0), stop=(c == 7))
            return li
        r.on("tensor", dmm_fn, inc="pe")
        r.on("scalar", lambda s: s.copy(RES[:], PSD[:]), inc="act")
        r.dma("sync", dbg[:], RES[:], inc="st")
        r.salt = "zz_"
        r.on("sync", lambda e: e.nop(), waits=r.snap())
        with nc.Block() as block:
            r.finalize(block)
        return nc
    ctx["aps"] = {"x": xs.ap(), "wo": WA["awo"], "xout": xs.ap(), "o": od_dram.ap()}
    build_attn_proj(ctx=ctx, pfx="ap", o_fill=None)
    asc.close()
    if upto <= 20:
        return finish_early()

    ffn_phase("p3", xs.ap(), xs.ap(), None, 2)
    if upto <= 30:
        return finish_early()

    r.salt = "f1_"
    def f_aps(xout_ap):
        return {"x": xs.ap(), "w1": WA["f_w1"], "w2": WA["f_w2"],
                "h1": WA["f_h1"], "h2": WA["f_h2"], "w3": WA["f_w3"],
                "s1": WA["f_s1"], "s2": WA["f_s2"], "sh1": WA["f_sh1"],
                "sh2": WA["f_sh2"], "s3": WA["f_s3"], "xout": xout_ap}
    ctx["aps"] = f_aps(xs.ap())
    build_l1ffn(ctx=ctx, pfx="f1")
    if upto <= 40:
        return finish_early()

    r.salt = "la_"
    ctx["aps"] = {"x": xs.ap(), "ln1g": WA["a_ln1g"], "ln1b": WA["a_ln1b"],
                  "ln2g": WA["a_ln2g"], "ln2b": WA["a_ln2b"],
                  "wq": WA["a_wq"], "wk": WA["a_wk"], "wv": WA["a_wv"],
                  "wo": WA["a_wo"], "qb": WA["a_qb"], "kb": WA["a_kb"],
                  "vb": WA["a_vb"], "ob": WA["a_ob"], "fc": WA["a_fc"],
                  "fcb": WA["a_fcb"], "pj": WA["a_pj"], "pjb": WA["a_pjb"],
                  "c_ident": capd["c_ident"], "xout": xs.ap()}
    build_l1attn(ctx=ctx, pfx="la")
    if upto <= 50:
        return finish_early()

    r.salt = "f2_"
    ctx["aps"] = f_aps(xo)
    build_l1ffn(ctx=ctx, pfx="f2")

    r.salt = "zz_"
    r.on("sync", lambda e: e.nop(), waits=r.snap())
    with nc.Block() as block:
        r.finalize(block)
    return nc


def fused_inputs(inputs, W, c):
    cst = consts()
    b, t0 = core_pos(c)
    x_fm = W["_x_fm"]
    m = {
        "x": _slice_own(x_fm, c),
        "srch1": _slice_halo(x_fm, c),
        "gate": np.full((128, 1), 0.0 if t0 == 0 else 1.0, np.float32),
        "ipv": np.tile((((c - 1) % 8) * 1024 + np.arange(128, dtype=np.int32))[:, None],
                       (1, 8)) + (np.arange(8, dtype=np.int32) * 128)[None, :],
        "xout": None,
    }
    del m["xout"]
    h0 = (c % 4) * 4
    qki = np.zeros((64, 16), np.int32)
    for p_ in range(4):
        h = h0 + p_
        for s in range(4):
            rk = b * 4 + s
            qki[:, p_ * 4 + s] = rk * 1024 + h * 64 + np.arange(64)
    m["qki"] = qki
    vdi = np.zeros((128, 256), np.int32)
    for p_ in range(4):
        h = h0 + p_
        for kb in range(64):
            gtok = kb * 128 + np.arange(128)
            rk = b * 4 + gtok // T
            vdi[:, p_ * 64 + kb] = rk * (16 * T) + h * T + (gtok % T)
    m["vdi"] = vdi
    odi = np.zeros((128, 8), np.int32)
    lc = c % 4
    for j in range(8):
        for half in range(2):
            h = 2 * j + half
            rk = b * 4 + h // 4
            p_ = h % 4
            odi[half * 64:(half + 1) * 64, j] = \
                rk * 1024 + lc * 256 + p_ * 64 + np.arange(64)
    m["odi"] = odi

    for l in range(2):
        m[f"wqkv{l}"] = np.ascontiguousarray(W["L0_ffn_wqkv"][l])
        m[f"wo{l}"] = np.ascontiguousarray(W["L0_ffn_wo"][l])
        m[f"qs{l}"] = np.tile(np.asarray(W["L0_ffn_q_scale"][l], np.float32).reshape(64, 1), (2, 1))
        m[f"ks{l}"] = np.tile(np.asarray(W["L0_ffn_k_scale"][l], np.float32).reshape(64, 1), (2, 1))
        m[f"lng{l}"] = _pp(W["L0_ffn_ln_g"][l])
        m[f"lnb{l}"] = _pp(W["L0_ffn_ln_b"][l])
        m[f"w1{l}"] = np.ascontiguousarray(W["L0_ffn_w1"][l])
        m[f"b1a{l}"] = _pp_pad(W["L0_ffn_b1"][l][:2730], 22, 2730)
        m[f"b1g{l}"] = _pp_pad(W["L0_ffn_b1"][l][2730:], 22, 2730)
        m[f"w2{l}"] = np.ascontiguousarray(W["L0_ffn_w2"][l])
        m[f"b2{l}"] = _pp(W["L0_ffn_b2"][l])
    m["awqkv"] = W["att_wqkv_perm"]
    m["awo"] = W["L0_attn_w_o"]
    m["logxi"] = np.asarray(W["L0_attn_log_xi"], np.float32).reshape(16, 1)
    m["ebits"] = cst["ebits"]
    m["j16"] = np.ascontiguousarray(cst["j16"][:, t0:t0 + T])
    m["jrow"] = cst["jrow"]
    m["logxi4"] = np.asarray(W["L0_attn_log_xi"], np.float32).reshape(1, 16)[:, h0:h0 + 4]
    m["ebits4"] = np.ascontiguousarray(cst["ebits"].reshape(1, 16)[:, h0:h0 + 4])
    m["aq1"] = np.ascontiguousarray(cst["att_t1q"][:, t0:t0 + T])
    m["aq2"] = np.ascontiguousarray(cst["att_t2q"][:, t0:t0 + T])
    m["ak1"] = np.ascontiguousarray(cst["att_t1"][:, t0:t0 + T])
    m["ak2"] = np.ascontiguousarray(cst["att_t2"][:, t0:t0 + T])
    m["ffq1"] = np.ascontiguousarray(cst["ffq_t1"][:, t0:t0 + T])
    m["ffq2"] = np.ascontiguousarray(cst["ffq_t2"][:, t0:t0 + T])

    def ksl(name):
        out = np.zeros((64, 2560), np.float32)
        lo = t0 - 512
        s = max(lo, 0)
        out[:, s - lo:] = cst[name][:, s:t0 + T]
        return out
    m["ffka1"] = ksl("ffka_t1")
    m["ffka2"] = ksl("ffka_t2")
    m["ffkb1"] = ksl("ffkb_t1")
    m["ffkb2"] = ksl("ffkb_t2")
    m["mW0"] = cst["maskL0"] if t0 == 0 else np.ascontiguousarray(cst["maskL"][:, 0:4 * 512])
    m["f_w1"] = W["L1_ffn_w1_w_t"]
    m["f_w2"] = W["L1_ffn_w2_w_t"]
    m["f_h1"] = W["L1_ffn_h1_w_t"]
    m["f_h2"] = W["L1_ffn_h2_w_t"]
    m["f_w3"] = W["L1_ffn_w3_w_t"]
    m["f_s1"] = _pp(W["L1_ffn_w1_s"], 32)
    m["f_s2"] = _pp(W["L1_ffn_w2_s"], 32)
    m["f_sh1"] = _pp(W["L1_ffn_h1_s"], 32)
    m["f_sh2"] = _pp(W["L1_ffn_h2_s"], 32)
    m["f_s3"] = _pp(W["L1_ffn_w3_s"], 8)
    m["a_ln1g"] = _pp(W["L1_attn_ln1_w"])
    m["a_ln1b"] = _pp(W["L1_attn_ln1_b"])
    m["a_ln2g"] = _pp(W["L1_attn_ln2_w"])
    m["a_ln2b"] = _pp(W["L1_attn_ln2_b"])
    m["a_wq"] = W["L1_attn_q_w_t"]
    m["a_wk"] = W["L1_attn_k_w_t"]
    m["a_wv"] = W["L1_attn_v_w_t"]
    m["a_wo"] = W["L1_attn_o_w_t"]
    m["a_qb"] = np.asarray(W["L1_attn_q_b"], np.float32).reshape(1, D)
    m["a_kb"] = np.asarray(W["L1_attn_k_b"], np.float32).reshape(1, D)
    m["a_vb"] = np.asarray(W["L1_attn_v_b"], np.float32).reshape(1, D)
    m["a_ob"] = _pp(W["L1_attn_o_b"])
    m["a_fc"] = W["L1_attn_fc_w_t"]
    m["a_fcb"] = _pp(W["L1_attn_fc_b"], 32)
    m["a_pj"] = W["L1_attn_proj_w_t"]
    m["a_pjb"] = _pp(W["L1_attn_proj_b"])
    m.update(const_in_map(["ones128", "ones1x128", "head2", "head2exp",
                           "kpre", "qpre", "ident", "maskL", "maskC"]))
    return m


def kernel_fused(**inputs):
    W = prep_weights(inputs)
    W["_x_fm"] = _fm(inputs["L0_attn_x"])
    nc = _nc_get("FUSED", build_fused)
    in_maps = [fused_inputs(inputs, W, c) for c in range(NCORES)]
    res = run_bass_kernel_spmd(nc, in_maps, core_ids=list(range(NCORES)))
    x_fm = _assemble_fm([res.results[c]["xout"] for c in range(NCORES)])
    return np.ascontiguousarray(np.transpose(x_fm, (0, 2, 1)))


# version 06
# speedup vs baseline: 1.1927x; 1.1927x over previous
import numpy as np
import ml_dtypes
import concourse.bass as bass
import concourse.mybir as mybir
from concourse.bass_utils import run_bass_kernel_spmd
from concourse.bass_core_rust import IndirectOffsetOnAxis

F32 = mybir.dt.float32
BF16 = mybir.dt.bfloat16
AF = mybir.ActivationFunctionType
ALU = mybir.AluOpType
AX = mybir.AxisListType.X
I32 = mybir.dt.int32

NCORES = 8
B = 2
SEQ = 8192
D = 1024
T = 2048
HALO = 513
NEG = -1.0e30
BF = ml_dtypes.bfloat16

ENGINES = ("sync", "gpsimd", "tensor", "vector", "scalar")



def _host_consts():
    c = {}
    c["ones128"] = np.ones((128, 1), np.float32)
    c["ones1x128"] = np.ones((1, 128), np.float32)
    h2 = np.zeros((128, 2), np.float32)
    h2[0:64, 0] = 1.0
    h2[64:128, 1] = 1.0
    c["head2"] = h2
    h2e = np.zeros((2, 128), np.float32)
    h2e[0, 0:64] = 1.0
    h2e[1, 64:128] = 1.0
    c["head2exp"] = h2e
    c["ident"] = np.eye(128, dtype=np.float32)
    c["onesrow"] = np.ones((1, SEQ), np.float32)
    kp = np.zeros((4, SEQ), np.float32)
    kp[0, :] = 1.0
    c["kpre"] = kp
    qp = np.zeros((3, SEQ), np.float32)
    qp[0:2, :] = 1.0
    c["qpre"] = qp
    c["jrow"] = np.arange(SEQ, dtype=np.float32)[None, :]
    c["j16"] = np.tile(np.arange(SEQ, dtype=np.float32)[None, :], (16, 1))
    ht_ = np.arange(16) % 4
    c["ebits"] = (ht_ & 1).astype(np.float32)[:, None]

    mL = np.zeros((8, 128, 512), np.float32)
    for m in range(8):
        r = np.arange(128)[:, None]
        cc = np.arange(512)[None, :]
        diff = cc + 512 - 128 * m - r
        mL[m] = np.where((diff >= 0) & (diff <= 512), 0.0, NEG)
    c["maskL"] = np.ascontiguousarray(mL.transpose(1, 0, 2).reshape(128, 8 * 512))
    c["maskL0"] = np.ascontiguousarray(
        np.full((128, 4 * 512), NEG, np.float32))

    mC = np.zeros((5, 128, 512), np.float32)
    for rel in range(4):
        r = np.arange(128)[:, None]
        cc = np.arange(512)[None, :]
        mC[rel] = np.where(rel * 128 + r <= cc, 0.0, NEG)
    mC[4] = NEG
    c["maskC"] = np.ascontiguousarray(mC.transpose(1, 0, 2).reshape(128, 5 * 512))

    pos = np.arange(SEQ, dtype=np.float64)
    inv = 10000.0 ** (-(np.arange(0, 64, 2, dtype=np.float64) / 64))
    th = pos[None, :] * inv[:, None]
    cosA = np.cos(th).astype(np.float32)
    sinA = np.sin(th).astype(np.float32)
    c["att_t1"] = np.concatenate([cosA, cosA], 0)
    c["att_t2"] = np.concatenate([-sinA, sinA], 0)
    c["att_t1q"] = c["att_t1"] * 0.125
    c["att_t2q"] = c["att_t2"] * 0.125

    DH = 64
    WIN = 512
    half = np.arange(0, DH, 2, dtype=np.float64)
    invf = 1.0 / 10000.0 ** (half / DH)
    base = (half + 0.4 * DH) / (1.4 * DH)

    def tabs(tglob, expo):
        fr = tglob[None, :] * invf[:, None]
        sc = base[:, None] ** (expo[None, :] / 256.0)
        cosF = np.cos(fr) * sc
        sinF = np.sin(fr) * sc
        t1 = np.concatenate([cosF, cosF], 0)
        t2 = np.concatenate([-sinF, sinF], 0)
        return t1.astype(np.float32), t2.astype(np.float32)

    tmod = np.arange(SEQ, dtype=np.float64) % WIN
    tg = np.arange(SEQ, dtype=np.float64)
    c["ffq_t1"], c["ffq_t2"] = tabs(tg, tmod)
    c["ffka_t1"], c["ffka_t2"] = tabs(tg, -tmod)
    c["ffkb_t1"], c["ffkb_t2"] = tabs(tg, (WIN - tmod))
    c["ffq_t1"] = c["ffq_t1"] * 8.0
    c["ffq_t2"] = c["ffq_t2"] * 8.0
    return c


_CONSTS = None


def consts():
    global _CONSTS
    if _CONSTS is None:
        _CONSTS = _host_consts()
    return _CONSTS


def core_pos(c):
    return c // 4, (c % 4) * T


def _pp(vec, nchunk=8):
    v = np.asarray(vec, np.float32).reshape(nchunk, 128)
    return np.ascontiguousarray(v.T)


def _pp_pad(vec, nchunk, size):
    v = np.zeros(nchunk * 128, np.float32)
    v[:size] = np.asarray(vec, np.float32).ravel()[:size]
    return _pp(v, nchunk)


def _fm(x_btd):
    return np.ascontiguousarray(np.transpose(np.asarray(x_btd, np.float32), (0, 2, 1)))


def _slice_own(x_fm, c):
    b, t0 = core_pos(c)
    return np.ascontiguousarray(x_fm[b, :, t0:t0 + T])


def _slice_halo(x_fm, c, n=HALO):
    b, t0 = core_pos(c)
    out = np.zeros((D, n), np.float32)
    lo = t0 - n
    if lo >= 0:
        out[:] = x_fm[b, :, lo:t0]
    elif t0 > 0:
        out[:, n - t0:] = x_fm[b, :, 0:t0]
    return out


def _assemble_fm(outs):
    full = np.empty((B, D, SEQ), np.float32)
    for c in range(NCORES):
        b, t0 = core_pos(c)
        full[b, :, t0:t0 + T] = outs[c]
    return full



class Rec:
    def __init__(self, nc):
        self.nc = nc
        self.ops = {e: [] for e in ENGINES}
        self.s = {}
        self.c = {}

    salt = ""

    def sem(self, n):
        n = self.salt + n
        if n not in self.s:
            self.s[n] = self.nc.alloc_semaphore(name="sm_" + n)
            self.c[n] = 0
        return self.s[n]

    def snap(self):
        return dict(self.c)

    def v(self, n):
        self.sem(n)
        return self.c[self.salt + n]

    def on(self, engine, fn, inc=None, inc_by=1, waits=None):
        if waits is None:
            waits = self.snap()
        waits = {n: v for n, v in waits.items() if v > 0}
        if inc is not None:
            self.sem(inc)
            self.c[self.salt + inc] += inc_by
            inc = self.salt + inc
        self.ops[engine].append((fn, waits, inc, inc_by))

    def dma(self, queue, out, in_, inc=None, slow=False):
        if inc is None:
            inc = {"sync": "dma", "gpsimd": "wdma", "scalar": "adma"}[queue]
        self.on(queue, lambda e, o=out, i=in_: e.dma_start(
            out=o, in_=i, allow_slow_non_contiguous=slow), inc=inc, inc_by=16)

    def finalize(self, block):
        for eng_name in ENGINES:
            ops = self.ops[eng_name]
            if not ops:
                continue

            def body(eng, ops=ops):
                last_waits = {}
                for fn, waits, inc, inc_by in ops:
                    for n, v in waits.items():
                        if last_waits.get(n, -1) < v:
                            eng.wait_ge(self.s[n], v)
                            last_waits[n] = v
                    li = fn(eng)
                    if inc is not None:
                        assert li is not None, f"op for inc={inc} returned None"
                        li.then_inc(self.s[inc], inc_by)
            getattr(block, eng_name)(body)


def _new_nc():
    return bass.Bass("TRN2", target_bir_lowering=False, debug=False)


def _inp(ctx, nc, name, shape, dtype=F32):
    if ctx is not None:
        return ctx["aps"][name]
    return _din(nc, name, shape, dtype)


def _outp(ctx, nc, name, shape, dtype=F32):
    if ctx is not None:
        return ctx["aps"][name]
    return _dout(nc, name, shape, dtype)


def _ctx_open(ctx, pfx):
    if ctx is None:
        nc = _new_nc()
        return nc, Rec(nc), Scope(nc, pfx), True
    return ctx["nc"], ctx["r"], Scope(ctx["nc"], pfx), False


class Scope:

    def __init__(self, nc, pfx=""):
        self.nc = nc
        self.pfx = pfx
        self.gs = []

    def sb(self, name, shape, dtype=F32):
        g = self.nc.sbuf_tensor(self.pfx + name, list(shape), dtype)
        t = g.__enter__()
        self.gs.append(g)
        return t

    def ps(self, name, shape, dtype=F32):
        g = self.nc.psum_tensor(self.pfx + name, list(shape), dtype)
        t = g.__enter__()
        self.gs.append(g)
        return t

    def close(self):
        for g in reversed(self.gs):
            g.__exit__(None, None, None)


def _din(nc, name, shape, dtype=F32):
    return nc.dram_tensor(name, list(shape), dtype, kind="ExternalInput").ap()


def _dout(nc, name, shape, dtype=F32):
    return nc.dram_tensor(name, list(shape), dtype, kind="ExternalOutput").ap()


def _sb(nc, name, shape, dtype=F32):
    return nc.alloc_sbuf_tensor(name, list(shape), dtype)


def _psum(nc, name, shape, dtype=F32):
    return nc.alloc_psum_tensor(name, list(shape), dtype)


def load_consts(r, nc, names):
    cst = consts()
    tiles = {}
    aps = {}
    for n in names:
        arr = cst[n]
        aps[n] = _din(nc, "c_" + n, arr.shape, F32)
        tiles[n] = _sb(nc, "t_" + n, arr.shape, F32)
        r.dma("sync", tiles[n][:], aps[n][:])
    return tiles, aps


def load_consts_bf(r, nc, names):
    cst = consts()
    tiles = {}
    aps = {}
    for n in names:
        arr = cst[n]
        aps[n] = _din(nc, "c_" + n, arr.shape, F32)
        tiles[n] = _sb(nc, "b_" + n, arr.shape, BF16)
        r.dma("gpsimd", tiles[n][:], aps[n][:])
    return tiles, aps


def const_in_map(names):
    cst = consts()
    return {"c_" + n: cst[n] for n in names}



class LNKit:
    def __init__(self, nc):
        self.sq = _sb(nc, "ln_sq", [128, 3 * 512], F32)
        self.st = _sb(nc, "ln_st", [1, 4 * 512], F32)
        self.nsc = _sb(nc, "ln_nsc", [2, 2 * 512], F32)
        self.ps_s0 = _psum(nc, "ln_ps0", [2, 512], F32)
        self.ps_s1 = _psum(nc, "ln_ps1", [1, 512], F32)
        self.ps_b0 = _psum(nc, "ln_pb0", [128, 512], F32)
        self.ps_b1 = _psum(nc, "ln_pb1", [128, 512], F32)


def rec_ln(r, kit, tiles, sl_in, sl_out, width, gb=None):
    nt = (width + 511) // 512
    for it in range(nt):
        j0 = it * 512
        w = min(512, width - j0)

        def sums_fn(t, j0=j0, w=w):
            for c in range(8):
                li = t.matmul(kit.ps_s0[0:1, 0:w], tiles["ones128"][:],
                              sl_in(c, j0, w), start=(c == 0), stop=(c == 7))
            return li
        r.on("tensor", sums_fn, inc="pe")
        for c in range(8):
            r.on("scalar", lambda s, c=c, j0=j0, w=w: s.activation(
                kit.sq[:, 0:w], sl_in(c, j0, w), AF.Square), inc="act")
            r.on("tensor", lambda t, c=c, w=w: t.matmul(
                kit.ps_s1[0:1, 0:w], tiles["ones128"][:], kit.sq[:, 0:w],
                start=(c == 0), stop=(c == 7)), inc="pe")

        MU, WRK, RSTD, NMR = (kit.st[0:1, i * 512:i * 512 + 512] for i in range(4))

        def st1_fn(v, w=w):
            v.tensor_scalar_mul(MU[:, 0:w], kit.ps_s0[0:1, 0:w], 1.0 / D)
            v.tensor_scalar_mul(WRK[:, 0:w], kit.ps_s1[0:1, 0:w], 1.0 / D)
            v.tensor_tensor(RSTD[:, 0:w], MU[:, 0:w], MU[:, 0:w], ALU.mult)
            v.tensor_tensor(WRK[:, 0:w], WRK[:, 0:w], RSTD[:, 0:w], ALU.subtract)
            return v.tensor_scalar_add(WRK[:, 0:w], WRK[:, 0:w], 1e-5)
        r.on("vector", st1_fn, inc="dve")

        r.on("scalar", lambda s, w=w: s.activation(NMR[:, 0:w], WRK[:, 0:w],
                                                   AF.Sqrt), inc="act")

        def st2_fn(v, w=w):
            v.reciprocal(RSTD[:, 0:w], NMR[:, 0:w])
            return v.scalar_tensor_tensor(NMR[:, 0:w], MU[:, 0:w], -1.0,
                                          RSTD[:, 0:w], ALU.mult, ALU.mult)
        r.on("vector", st2_fn, inc="dve")

        def bc_fn(t, w=w):
            t.matmul(kit.ps_b0[:, 0:w], tiles["ones1x128"][:], RSTD[:, 0:w],
                     start=True, stop=True)
            return t.matmul(kit.ps_b1[:, 0:w], tiles["ones1x128"][:], NMR[:, 0:w],
                            start=True, stop=True)
        r.on("tensor", bc_fn, inc="pe")

        def ap_fn(v, j0=j0, w=w):
            for c in range(8):
                src = sl_in(c, j0, w)
                dst = sl_out(c, j0, w)
                v.tensor_tensor(kit.sq[:, 0:w], src, kit.ps_b0[:, 0:w], ALU.mult)
                if gb is None:
                    li = v.tensor_tensor(dst, kit.sq[:, 0:w],
                                         kit.ps_b1[:, 0:w], ALU.add)
                else:
                    v.tensor_tensor(kit.sq[:, 0:w], kit.sq[:, 0:w],
                                    kit.ps_b1[:, 0:w], ALU.add)
                    li = v.tensor_scalar(dst, kit.sq[:, 0:w],
                                         gb[0][:, c:c + 1], gb[1][:, c:c + 1],
                                         ALU.mult, ALU.add)
            return li
        r.on("vector", ap_fn, inc="dve")


def rec_ln_stream(r, kit, tiles, stream, chnk, width, out=None, out_chnk=None, gb=None):
    def sl_in(c, j0, w):
        return stream[:, c * chnk + j0:c * chnk + j0 + w]
    if out is None:
        sl_out = sl_in
    else:
        def sl_out(c, j0, w):
            return out[:, c * out_chnk + j0:c * out_chnk + j0 + w]
    rec_ln(r, kit, tiles, sl_in, sl_out, width, gb=gb)



def rec_norm_rope(r, tiles, kit, ps_in, ps_n, ps_bc, scale_pp, t1, t2, tj0, w, dsts):
    sc1 = kit.sq[:, 0:512]
    sc2 = kit.sq[:, 512:1024]
    sc3 = kit.sq[:, 1024:1536]
    nrm = kit.nsc[0:2, 0:512]
    nri = kit.nsc[0:2, 512:1024]

    r.on("scalar", lambda s: s.activation(sc1[:, 0:w], ps_in[:, 0:w], AF.Square),
         inc="act")
    r.on("tensor", lambda t: t.matmul(ps_n[0:2, 0:w], tiles["head2"][:], sc1[:, 0:w],
                                      start=True, stop=True), inc="pe")
    r.on("scalar", lambda s: s.activation(nrm[:, 0:w], ps_n[0:2, 0:w], AF.Sqrt),
         inc="act")

    def clip_fn(v):
        v.tensor_scalar_max(nrm[:, 0:w], nrm[:, 0:w], 1e-12)
        return v.reciprocal(nri[:, 0:w], nrm[:, 0:w])
    r.on("vector", clip_fn, inc="dve")

    r.on("tensor", lambda t: t.matmul(ps_bc[:, 0:w], tiles["head2exp"][:],
                                      nri[:, 0:w], start=True, stop=True), inc="pe")
    r.on("scalar", lambda s: s.copy(sc1[:, 0:w], ps_in[:, 0:w]), inc="act")

    def fin_fn(v):
        v.scalar_tensor_tensor(sc2[:, 0:w], sc1[:, 0:w], scale_pp[:, 0:1],
                               ps_bc[:, 0:w], ALU.mult, ALU.mult)
        li = None
        for sub in range(2):
            b0 = 64 * sub
            dst = dsts[sub]
            v.tensor_scalar_mul(sc1[0:64, 0:w], sc2[b0:b0 + 64, 0:w], 1.0)
            v.tensor_scalar_mul(sc3[0:32, 0:w], sc2[b0 + 32:b0 + 64, 0:w], 1.0)
            v.tensor_scalar_mul(sc3[32:64, 0:w], sc2[b0:b0 + 32, 0:w], 1.0)
            v.tensor_tensor(sc1[0:64, 0:w], sc1[0:64, 0:w],
                            t1[0:64, tj0:tj0 + w], ALU.mult)
            v.tensor_tensor(sc3[0:64, 0:w], sc3[0:64, 0:w],
                            t2[0:64, tj0:tj0 + w], ALU.mult)
            li = v.tensor_tensor(dst[0:64, 0:w], sc1[0:64, 0:w], sc3[0:64, 0:w], ALU.add)
        return li
    r.on("vector", fin_fn, inc="dve")


def rec_rope_plain(r, kit, ps_in, t1, t2, tj0, w, dsts, do_rope, scl):
    sc1 = kit.sq[:, 0:512]
    sc2 = kit.sq[:, 512:1024]
    sc3 = kit.sq[:, 1024:1536]
    if not do_rope:
        def cp_fn(s):
            s.activation(dsts[0], ps_in[0:64, 0:w], AF.Copy, scale=scl)
            return s.activation(dsts[1], ps_in[64:128, 0:w], AF.Copy, scale=scl)
        r.on("scalar", cp_fn, inc="act")
        return
    r.on("scalar", lambda s: s.copy(sc1[:, 0:w], ps_in[:, 0:w]), inc="act")

    def fin_fn(v):
        li = None
        for sub in range(2):
            b0 = 64 * sub
            dst = dsts[sub]
            v.tensor_scalar_mul(sc2[0:64, 0:w], sc1[b0:b0 + 64, 0:w], 1.0)
            v.tensor_scalar_mul(sc3[0:32, 0:w], sc1[b0 + 32:b0 + 64, 0:w], 1.0)
            v.tensor_scalar_mul(sc3[32:64, 0:w], sc1[b0:b0 + 32, 0:w], 1.0)
            v.tensor_tensor(sc2[0:64, 0:w], sc2[0:64, 0:w],
                            t1[0:64, tj0:tj0 + w], ALU.mult)
            v.tensor_tensor(sc3[0:64, 0:w], sc3[0:64, 0:w],
                            t2[0:64, tj0:tj0 + w], ALU.mult)
            li = v.tensor_tensor(dst[0:64, 0:w], sc2[0:64, 0:w], sc3[0:64, 0:w], ALU.add)
        return li
    r.on("vector", fin_fn, inc="dve")



def rec_attn_blocks(r, kt, kdim, qt, vt, pt, ps_S, ps_O, qj0, qw, kblocks,
                    add_masks, vt_stride=65):
    n = len(kblocks)
    for i, kb in enumerate(kblocks):
        r.on("tensor", lambda t, kb=kb: t.matmul(
            ps_S[:, 0:qw], kt[0:kdim, kb * 128:(kb + 1) * 128],
            qt[0:kdim, qj0:qj0 + qw], start=True, stop=True), inc="pe")
        if add_masks is not None and add_masks[i] is not None:
            r.on("vector", lambda v, m=add_masks[i]: v.tensor_tensor(
                ps_S[:, 0:qw], ps_S[:, 0:qw], m, ALU.add), inc="dve")
        r.on("scalar", lambda s: s.activation(pt[:, 0:qw], ps_S[:, 0:qw], AF.Exp),
             inc="act")
        r.on("tensor", lambda t, kb=kb, i=i: t.matmul(
            ps_O[:, 0:qw], vt[:, kb * vt_stride:kb * vt_stride + 65],
            pt[:, 0:qw], start=(i == 0), stop=(i == n - 1)), inc="pe")


def rec_o_norm(r, tiles, kit, ps_O, ps_bc, osb, rcp, dst, w):
    r.on("vector", lambda v: v.reciprocal(rcp[0:1, 0:w], ps_O[64:65, 0:w]), inc="dve")
    r.on("scalar", lambda s: s.copy(osb[0:64, 0:w], ps_O[0:64, 0:w]), inc="act")
    r.on("tensor", lambda t: t.matmul(ps_bc[0:64, 0:w], tiles["ones1x128"][0:1, 0:64],
                                      rcp[0:1, 0:w], start=True, stop=True), inc="pe")

    def div_fn(v):
        v.tensor_tensor(kit.sq[0:64, 0:w], osb[0:64, 0:w], ps_bc[0:64, 0:w], ALU.mult)
        return v.tensor_scalar_mul(dst, kit.sq[0:64, 0:w], 1.0)
    r.on("vector", div_fn, inc="dve")



def rec_attn_win(r, kt, kdim, qt, vt, pt, ps_S, ps_O, qw, blocks):
    n = len(blocks)
    for i, (kbk, kbv, msk) in enumerate(blocks):
        r.on("tensor", lambda t, kbk=kbk: t.matmul(
            ps_S[:, 0:qw], kt[0:kdim, kbk * 128:(kbk + 1) * 128],
            qt[0:kdim, 0:qw], start=True, stop=True), inc="pe")
        if msk is not None:
            r.on("vector", lambda v, m=msk: v.tensor_tensor(
                ps_S[:, 0:qw], ps_S[:, 0:qw], m, ALU.add), inc="dve")
        r.on("scalar", lambda s: s.activation(pt[:, 0:qw], ps_S[:, 0:qw], AF.Exp),
             inc="act")
        r.on("tensor", lambda t, kbv=kbv, i=i: t.matmul(
            ps_O[:, 0:qw], vt[:, kbv * 65:kbv * 65 + 65],
            pt[:, 0:qw], start=(i == 0), stop=(i == n - 1)), inc="pe")



def build_ffnA(pre_ln, ctx=None, pfx="", halo_fill=None):
    nc, r, sc, ext = _ctx_open(ctx, pfx)
    KW = 2560
    HBW = 2561

    src = _inp(ctx, nc, "src", (D, T))
    srch = _inp(ctx, nc, "srch", (D, HALO)) if halo_fill is None else None
    wqkv_i = _inp(ctx, nc, "wqkv", (D, 1536))
    wo_i = _inp(ctx, nc, "wo", (512, D))
    qs_i = _inp(ctx, nc, "qs", (128, 1))
    ks_i = _inp(ctx, nc, "ks", (128, 1))
    mW0_i = _inp(ctx, nc, "mW0", (128, 4 * 512))
    ffq1_i = _inp(ctx, nc, "ffq1", (64, T))
    ffq2_i = _inp(ctx, nc, "ffq2", (64, T))
    ffka1_i = _inp(ctx, nc, "ffka1", (64, KW))
    ffka2_i = _inp(ctx, nc, "ffka2", (64, KW))
    ffkb1_i = _inp(ctx, nc, "ffkb1", (64, KW))
    ffkb2_i = _inp(ctx, nc, "ffkb2", (64, KW))
    umid_o = _outp(ctx, nc, "umid", (D, T))

    U = sc.sb("U", [128, 8 * T], F32)
    HB = sc.sb("HBs", [128, 8 * HBW], BF16)
    WQ = sc.sb("WQs", [128, 8 * 384], BF16)
    WO = sc.sb("WOs", [128, 4 * 128], BF16)
    QS = sc.sb("QSs", [128, 1], F32)
    KS = sc.sb("KSs", [128, 1], F32)
    MW0 = sc.sb("MW0", [128, 4 * 512], BF16)
    MKL = sc.sb("MKL", [128, 8 * 512], BF16)
    FQ1 = sc.sb("FQ1", [64, T], BF16)
    FQ2 = sc.sb("FQ2", [64, T], BF16)
    FKA1 = sc.sb("FKA1", [64, KW], BF16)
    FKA2 = sc.sb("FKA2", [64, KW], BF16)
    FKB1 = sc.sb("FKB1", [64, KW], BF16)
    FKB2 = sc.sb("FKB2", [64, KW], BF16)
    kit = ctx["kit"] if ctx else LNKit(nc)
    tiles, caps = (ctx["tiles"], ctx["caps"]) if ctx else load_consts(r, nc, ["ones128", "ones1x128", "head2",
                                      "head2exp"])
    cst = consts()
    kpre_ap = _inp(ctx, nc, "c_kpre", cst["kpre"].shape, F32)
    ident_ap = _inp(ctx, nc, "c_ident", cst["ident"].shape, F32)
    maskL_ap = _inp(ctx, nc, "c_maskL", cst["maskL"].shape, F32)

    uh_guard = nc.sbuf_tensor(pfx + "UH", [128, 8 * HALO], F32)
    UH = uh_guard.__enter__()

    r.dma("sync", U[:].rearrange("p (c w) -> p c w", w=T),
          src[:].rearrange("(c p) w -> p c w", p=128))
    if halo_fill is None:
        r.dma("sync", UH[:].rearrange("p (c w) -> p c w", w=HALO),
              srch[:].rearrange("(c p) w -> p c w", p=128))
    else:
        halo_fill(UH)
    r.dma("sync", QS[:], qs_i[:])
    r.dma("sync", KS[:], ks_i[:])
    r.dma("gpsimd", MW0[:], mW0_i[:])
    r.dma("gpsimd", MKL[:], maskL_ap[:])
    r.dma("gpsimd", FQ1[:], ffq1_i[:])
    r.dma("gpsimd", FQ2[:], ffq2_i[:])
    r.dma("gpsimd", FKA1[:], ffka1_i[:])
    r.dma("gpsimd", FKA2[:], ffka2_i[:])
    r.dma("gpsimd", FKB1[:], ffkb1_i[:])
    r.dma("gpsimd", FKB2[:], ffkb2_i[:])

    if pre_ln:
        rec_ln_stream(r, kit, tiles, U, T, T)
        rec_ln_stream(r, kit, tiles, UH, HALO, HALO)

    def hb_fn(v):
        for c in range(8):
            v.tensor_copy(HB[:, c * HBW:c * HBW + HALO], UH[:, c * HALO:(c + 1) * HALO])
            li = v.tensor_copy(HB[:, c * HBW + HALO:(c + 1) * HBW], U[:, c * T:(c + 1) * T])
        return li
    r.on("vector", hb_fn, inc="dve")

    uh_guard.__exit__(None, None, None)
    IDB = sc.sb("IDB", [128, 128], BF16)
    KTw = [sc.sb(f"KTw{i}", [65, 1024], BF16) for i in range(2)]
    QTw = [sc.sb(f"QTw{i}", [65, 512], BF16) for i in range(2)]
    VT = [sc.sb(f"VT{i}", [128, 20 * 65], BF16) for i in range(2)]
    OALL = [sc.sb(f"OALL{i}", [128, T], BF16) for i in range(4)]
    PT = sc.sb("PTs", [128, 512], BF16)
    MXC = sc.sb("MXC", [128, 4], F32)
    MXB = sc.sb("MXB", [128, 4], BF16)
    MT = sc.sb("MTs", [4, 128], BF16)
    OSB = sc.sb("OSB", [64, 512], F32)
    RCP = sc.sb("RCPs", [1, 512], F32)

    ps_qkv = kit.ps_b0
    ps_n = kit.ps_s0
    ps_bc = kit.ps_b1
    ps_S = sc.ps("ps_S", [128, 640], F32)
    ps_O = sc.ps("ps_O", [65, 512], F32)
    ps_T = sc.ps("ps_T", [4, 128], BF16)

    r.dma("gpsimd", IDB[:], ident_ap[:])
    for i in range(2):
        r.dma("gpsimd", KTw[i][64:65, :], kpre_ap[0:1, 0:1024])
        r.on("gpsimd", lambda g, i=i: g.memset(
            VT[i][:].rearrange("p (b c) -> p b c", c=65)[:, :, 64:65], 1.0),
            inc="pool")

    def hslice(c, j0, w):
        off = 1 if c < 4 else 0
        return HB[:, c * HBW + j0 + off: c * HBW + j0 + off + w]

    for hp in range(4):
        r.dma("gpsimd", WQ[:].rearrange("p (c w) -> p c w", w=384)[:, :, 0:128],
              wqkv_i[:, hp * 128:(hp + 1) * 128].rearrange("(c p) w -> p c w", p=128))
        r.dma("gpsimd", WQ[:].rearrange("p (c w) -> p c w", w=384)[:, :, 128:256],
              wqkv_i[:, 512 + hp * 128:512 + (hp + 1) * 128].rearrange("(c p) w -> p c w", p=128))
        r.dma("gpsimd", WQ[:].rearrange("p (c w) -> p c w", w=384)[:, :, 256:384],
              wqkv_i[:, 1024 + hp * 128:1024 + (hp + 1) * 128].rearrange("(c p) w -> p c w", p=128))

        for blk in range(20):
            j0 = blk * 128

            def vmm_fn(t, j0=j0):
                for sub in range(2):
                    for c in range(8):
                        li = t.matmul(ps_qkv[0:128, sub * 64:sub * 64 + 64],
                                      hslice(c, j0, 128),
                                      WQ[:, c * 384 + 256 + sub * 64:c * 384 + 256 + sub * 64 + 64],
                                      start=(c == 0), stop=(c == 7))
                return li
            r.on("tensor", vmm_fn, inc="pe")

            def vev_fn(s, blk=blk):
                s.copy(VT[0][:, blk * 65:blk * 65 + 64], ps_qkv[0:128, 0:64])
                return s.copy(VT[1][:, blk * 65:blk * 65 + 64], ps_qkv[0:128, 64:128])
            r.on("scalar", vev_fn, inc="act")

        for w in range(4):
            w0 = w * 512
            for half in range(2):
                kj0 = w0 + half * 512

                def kmm_fn(t, kj0=kj0):
                    for c in range(8):
                        li = t.matmul(ps_qkv[:], WQ[:, c * 384 + 128:c * 384 + 256],
                                      hslice(c, kj0, 512), start=(c == 0), stop=(c == 7))
                    return li
                r.on("tensor", kmm_fn, inc="pe")
                t1 = FKA1 if half == 1 else FKB1
                t2 = FKA2 if half == 1 else FKB2
                rec_norm_rope(r, tiles, kit, ps_qkv, ps_n, ps_bc, KS,
                              t1, t2, kj0, 512,
                              (KTw[0][0:64, half * 512:(half + 1) * 512],
                               KTw[1][0:64, half * 512:(half + 1) * 512]))

            def qmm_fn(t, w0=w0):
                for c in range(8):
                    li = t.matmul(ps_qkv[:], WQ[:, c * 384:c * 384 + 128],
                                  hslice(c, 512 + w0, 512), start=(c == 0), stop=(c == 7))
                return li
            r.on("tensor", qmm_fn, inc="pe")
            rec_norm_rope(r, tiles, kit, ps_qkv, ps_n, ps_bc, QS,
                          FQ1, FQ2, w0, 512,
                          (QTw[0][0:64, :], QTw[1][0:64, :]))

            for sub in range(2):
                Kt, Qt, Vt = KTw[sub], QTw[sub], VT[sub]
                for qs in range(4):
                    def mx_fn(t, qs=qs, Kt=Kt, Qt=Qt):
                        t.matmul(ps_S[:, 0:512], Qt[0:64, qs * 128:(qs + 1) * 128],
                                 Kt[0:64, qs * 128:qs * 128 + 512], start=True, stop=True)
                        return t.matmul(ps_S[:, 512:640],
                                        Qt[0:64, qs * 128:(qs + 1) * 128],
                                        Kt[0:64, qs * 128 + 512:qs * 128 + 640],
                                        start=True, stop=True)
                    r.on("tensor", mx_fn, inc="pe")
                    r.on("vector", lambda v, qs=qs: v.reduce_max(
                        MXC[:, qs:qs + 1], ps_S[:, 0:640], AX), inc="dve")
                r.on("vector", lambda v: v.tensor_scalar_mul(MXB[:], MXC[:], -1.0),
                     inc="dve")
                r.on("tensor", lambda t: t.transpose(ps_T[:], MXB[:], IDB[:]),
                     inc="pe")
                r.on("scalar", lambda s: s.copy(MT[:], ps_T[:]), inc="act")
                for qs in range(4):
                    r.dma("scalar", Qt[64:65, qs * 128:(qs + 1) * 128],
                          MT[qs:qs + 1, :])
                blocks = []
                for m in range(8):
                    if w == 0 and m < 4:
                        msk = MW0[:, m * 512:(m + 1) * 512]
                    else:
                        msk = MKL[:, m * 512:(m + 1) * 512]
                    blocks.append((m, 4 * w + m, msk))
                rec_attn_win(r, Kt, 65, Qt, Vt, PT, ps_S, ps_O, 512, blocks)
                hh = 2 * hp + sub
                dst = OALL[hh // 2][(hh % 2) * 64:(hh % 2) * 64 + 64, w0:w0 + 512]
                rec_o_norm(r, tiles, kit, ps_O, ps_bc, OSB, RCP, dst, 512)

    for f in range(8):
        r.dma("gpsimd", WO[:].rearrange("p (c w) -> p c w", w=128),
              wo_i[:, f * 128:(f + 1) * 128].rearrange("(c p) w -> p c w", p=128))
        for tt_ in range(4):
            j0 = tt_ * 512

            def wo_fn(t, f=f, j0=j0):
                for c in range(4):
                    li = t.matmul(ps_qkv[:], WO[:, c * 128:(c + 1) * 128],
                                  OALL[c][:, j0:j0 + 512], start=(c == 0), stop=(c == 3))
                return li
            r.on("tensor", wo_fn, inc="pe")
            r.on("vector", lambda v, f=f, j0=j0: v.tensor_tensor(
                U[:, f * T + j0:f * T + j0 + 512], U[:, f * T + j0:f * T + j0 + 512],
                ps_qkv[:], ALU.add), inc="dve")

    r.dma("sync", umid_o[:].rearrange("(c p) w -> p c w", p=128),
          U[:].rearrange("p (c w) -> p c w", w=T), inc="st")
    r.on("sync", lambda e: e.nop(), waits=r.snap())
    if not ext:
        sc.close()
    if ext:
        with nc.Block() as block:
            r.finalize(block)
    return nc


def ffnA_inputs(stream_fm, c, layer, W, pre_ln):
    cst = consts()
    b, t0 = core_pos(c)

    def ksl(name):
        out = np.zeros((64, 2560), np.float32)
        lo = t0 - 512
        s = max(lo, 0)
        out[:, s - lo:] = cst[name][:, s:t0 + T]
        return out

    m = {
        "src": _slice_own(stream_fm, c),
        "srch": _slice_halo(stream_fm, c),
        "wqkv": np.ascontiguousarray(W["L0_ffn_wqkv"][layer]),
        "wo": np.ascontiguousarray(W["L0_ffn_wo"][layer]),
        "qs": np.tile(np.asarray(W["L0_ffn_q_scale"][layer], np.float32).reshape(64, 1), (2, 1)),
        "ks": np.tile(np.asarray(W["L0_ffn_k_scale"][layer], np.float32).reshape(64, 1), (2, 1)),
        "mW0": cst["maskL0"] if t0 == 0 else np.ascontiguousarray(cst["maskL"][:, 0:4 * 512]),
        "ffq1": np.ascontiguousarray(cst["ffq_t1"][:, t0:t0 + T]),
        "ffq2": np.ascontiguousarray(cst["ffq_t2"][:, t0:t0 + T]),
        "ffka1": ksl("ffka_t1"),
        "ffka2": ksl("ffka_t2"),
        "ffkb1": ksl("ffkb_t1"),
        "ffkb2": ksl("ffkb_t2"),
    }
    m.update(const_in_map(["ones128", "ones1x128", "head2", "head2exp",
                           "kpre", "ident", "maskL"]))
    return m



def build_ffnB(final_x, ctx=None, pfx="", prev_fill=None):
    nc, r, sc, ext = _ctx_open(ctx, pfx)
    UW = T + 1

    u_i = _inp(ctx, nc, "u", (D, T))
    up_i = _inp(ctx, nc, "uprev", (D, 1)) if prev_fill is None else None
    lng_i = _inp(ctx, nc, "lng", (128, 8))
    lnb_i = _inp(ctx, nc, "lnb", (128, 8))
    w1_i = _inp(ctx, nc, "w1", (D, 5460))
    b1a_i = _inp(ctx, nc, "b1a", (128, 22))
    b1g_i = _inp(ctx, nc, "b1g", (128, 22))
    w2_i = _inp(ctx, nc, "w2", (2730, D))
    b2_i = _inp(ctx, nc, "b2", (128, 8))
    if final_x:
        x_i = _inp(ctx, nc, "x", (D, T))
        out_o = _outp(ctx, nc, "xout", (D, T))
    else:
        out_o = _outp(ctx, nc, "uout", (D, T))

    U = sc.sb("U", [128, 8 * UW], F32)
    Z = sc.sb("Z", [128, 8 * 512], BF16)
    HT = sc.sb("HT", [128, 22 * 512], BF16)
    W1C = sc.sb("W1C", [128, 8 * 256], BF16)
    W2C = sc.sb("W2C", [128, 22 * 128], BF16)
    GEL = sc.sb("GEL", [128, 512], BF16)
    LNG = sc.sb("LNG", [128, 8], F32)
    LNB = sc.sb("LNB", [128, 8], F32)
    B1A = sc.sb("B1A", [128, 22], F32)
    B1G = sc.sb("B1G", [128, 22], F32)
    B2 = sc.sb("B2s", [128, 8], F32)
    if final_x:
        X = sc.sb("X", [128, 8 * T], F32)

    ps_a = sc.ps("ps_a", [128, 512], F32)
    ps_g = sc.ps("ps_g", [128, 512], F32)
    ps_w2 = sc.ps("ps_w2", [128, 512], F32)

    kit = ctx["kit"] if ctx else LNKit(nc)
    tiles, caps = (ctx["tiles"], ctx["caps"]) if ctx else load_consts(r, nc, ["ones128", "ones1x128"])

    r.dma("sync", U[:].rearrange("p (c w) -> p c w", w=UW)[:, :, 1:UW],
          u_i[:].rearrange("(c p) w -> p c w", p=128))
    if prev_fill is None:
        r.dma("sync", U[:].rearrange("p (c w) -> p c w", w=UW)[:, :, 0:1],
              up_i[:].rearrange("(c p) w -> p c w", p=128), slow=True)
    else:
        prev_fill(U)
    r.dma("sync", LNG[:], lng_i[:])
    r.dma("sync", LNB[:], lnb_i[:])
    r.dma("sync", B1A[:], b1a_i[:])
    r.dma("sync", B1G[:], b1g_i[:])
    r.dma("sync", B2[:], b2_i[:])
    if final_x:
        r.dma("sync", X[:].rearrange("p (c w) -> p c w", w=T),
              x_i[:].rearrange("(c p) w -> p c w", p=128))

    for tt_ in reversed(range(4)):
        tb = tt_ * 512

        def sl_in(c, j0, w, tb=tb):
            off = 1 if c < 4 else 0
            return U[:, c * UW + off + tb + j0: c * UW + off + tb + j0 + w]

        def sl_out(c, j0, w):
            return Z[:, c * 512 + j0: c * 512 + j0 + w]

        rec_ln(r, kit, tiles, sl_in, sl_out, 512, gb=(LNG, LNB))

        for jc in range(22):
            acw = 128 if jc < 21 else 42
            r.dma("gpsimd", W1C[:].rearrange("p (c w) -> p c w", w=256)[:, :, 0:acw],
                  w1_i[:, jc * 128:jc * 128 + acw].rearrange("(c p) w -> p c w", p=128))
            r.dma("gpsimd", W1C[:].rearrange("p (c w) -> p c w", w=256)[:, :, 128:128 + acw],
                  w1_i[:, 2730 + jc * 128:2730 + jc * 128 + acw].rearrange("(c p) w -> p c w", p=128))

            def ag_fn(t, jc=jc, acw=acw):
                for c in range(8):
                    t.matmul(ps_a[0:acw, :], W1C[:, c * 256:c * 256 + acw],
                             Z[:, c * 512:(c + 1) * 512],
                             start=(c == 0), stop=(c == 7))
                for c in range(8):
                    li = t.matmul(ps_g[0:acw, :], W1C[:, c * 256 + 128:c * 256 + 128 + acw],
                                  Z[:, c * 512:(c + 1) * 512],
                                  start=(c == 0), stop=(c == 7))
                return li
            r.on("tensor", ag_fn, inc="pe")
            r.on("scalar", lambda s, jc=jc, acw=acw: s.activation(
                GEL[0:acw, :], ps_g[0:acw, :], AF.Gelu, bias=B1G[0:acw, jc:jc + 1]),
                inc="act")
            r.on("vector", lambda v, jc=jc, acw=acw: v.scalar_tensor_tensor(
                HT[0:acw, jc * 512:(jc + 1) * 512], ps_a[0:acw, :],
                B1A[0:acw, jc:jc + 1], GEL[0:acw, :], ALU.add, ALU.mult),
                inc="dve")

        for f in range(8):
            r.dma("gpsimd", W2C[:, 0:21 * 128],
                  w2_i[0:21 * 128, f * 128:(f + 1) * 128].rearrange("(c p) w -> p c w", p=128))
            r.dma("gpsimd", W2C[0:42, 21 * 128:22 * 128],
                  w2_i[21 * 128:2730, f * 128:(f + 1) * 128])

            def w2_fn(t, f=f):
                for c in range(22):
                    kc = 128 if c < 21 else 42
                    li = t.matmul(ps_w2[:], W2C[0:kc, c * 128:(c + 1) * 128],
                                  HT[0:kc, c * 512:(c + 1) * 512],
                                  start=(c == 0), stop=(c == 21))
                return li
            r.on("tensor", w2_fn, inc="pe")
            r.on("vector", lambda v, f=f, tb=tb: v.scalar_tensor_tensor(
                U[:, f * UW + 1 + tb:f * UW + 1 + tb + 512], ps_w2[:],
                B2[:, f:f + 1], U[:, f * UW + 1 + tb:f * UW + 1 + tb + 512],
                ALU.add, ALU.add), inc="dve")

    if final_x:
        def xres_fn(v):
            for f in range(8):
                li = v.scalar_tensor_tensor(
                    X[:, f * T:(f + 1) * T], U[:, f * UW + 1:f * UW + 1 + T],
                    0.5, X[:, f * T:(f + 1) * T], ALU.mult, ALU.add)
            return li
        r.on("vector", xres_fn, inc="dve")
        r.dma("sync", out_o[:].rearrange("(c p) w -> p c w", p=128),
              X[:].rearrange("p (c w) -> p c w", w=T), inc="st")
    else:
        r.dma("sync", out_o[:].rearrange("(c p) w -> p c w", p=128),
              U[:].rearrange("p (c w) -> p c w", w=UW)[:, :, 1:UW], inc="st")
    r.on("sync", lambda e: e.nop(), waits=r.snap())
    if not ext:
        sc.close()
    if ext:
        with nc.Block() as block:
            r.finalize(block)
    return nc


def ffnB_inputs(u_fm, c, layer, W, x_fm=None):
    m = {
        "u": _slice_own(u_fm, c),
        "uprev": _slice_halo(u_fm, c, 1),
        "lng": _pp(W["L0_ffn_ln_g"][layer]),
        "lnb": _pp(W["L0_ffn_ln_b"][layer]),
        "w1": np.ascontiguousarray(W["L0_ffn_w1"][layer]),
        "b1a": _pp_pad(W["L0_ffn_b1"][layer][:2730], 22, 2730),
        "b1g": _pp_pad(W["L0_ffn_b1"][layer][2730:], 22, 2730),
        "w2": np.ascontiguousarray(W["L0_ffn_w2"][layer]),
        "b2": _pp(W["L0_ffn_b2"][layer]),
    }
    if x_fm is not None:
        m["x"] = _slice_own(x_fm, c)
    m.update(const_in_map(["ones128", "ones1x128"]))
    return m



def prep_weights(inputs):
    W = {}
    for k, v in inputs.items():
        W[k] = np.asarray(v, np.float32)
    wq = W["L0_attn_w_qkv"].copy()
    ht = np.arange(16) % 4
    perm64 = np.concatenate([np.arange(0, 64, 2), np.arange(1, 64, 2)])
    for h in range(16):
        if (ht[h] >> 1) & 1:
            for base in (0, 1024):
                cols = base + h * 64 + np.arange(64)
                wq[:, cols] = wq[:, cols[perm64]]
    W["att_wqkv_perm"] = np.ascontiguousarray(wq)
    for k in ["L1_attn_q_w", "L1_attn_k_w", "L1_attn_v_w", "L1_attn_o_w",
              "L1_attn_fc_w", "L1_attn_proj_w", "L1_ffn_w1_w", "L1_ffn_w2_w",
              "L1_ffn_h1_w", "L1_ffn_h2_w", "L1_ffn_w3_w"]:
        W[k + "_t"] = np.ascontiguousarray(W[k].T)
    return W


_NC_CACHE = {}


def _nc_get(key, fn):
    if key not in _NC_CACHE:
        _NC_CACHE[key] = fn()
    return _NC_CACHE[key]


def _run(nc, in_maps):
    return run_bass_kernel_spmd(nc, in_maps, core_ids=list(range(NCORES)))


def _ffn_phase(x_fm, W):
    ncA0 = _nc_get("A0", lambda: build_ffnA(pre_ln=True))
    ncA1 = _nc_get("A1", lambda: build_ffnA(pre_ln=False))
    ncB0 = _nc_get("B0", lambda: build_ffnB(final_x=False))
    ncB1 = _nc_get("B1", lambda: build_ffnB(final_x=True))
    res = _run(ncA0, [ffnA_inputs(x_fm, c, 0, W, True) for c in range(NCORES)])
    m_fm = _assemble_fm([res.results[c]["umid"] for c in range(NCORES)])
    res = _run(ncB0, [ffnB_inputs(m_fm, c, 0, W) for c in range(NCORES)])
    u1_fm = _assemble_fm([res.results[c]["uout"] for c in range(NCORES)])
    res = _run(ncA1, [ffnA_inputs(u1_fm, c, 1, W, False) for c in range(NCORES)])
    m2_fm = _assemble_fm([res.results[c]["umid"] for c in range(NCORES)])
    res = _run(ncB1, [ffnB_inputs(m2_fm, c, 1, W, x_fm=x_fm) for c in range(NCORES)])
    return _assemble_fm([res.results[c]["xout"] for c in range(NCORES)])


def _attn_phase(x_fm, W):
    nc1 = _nc_get("Q", build_attn_qkv)
    nc2 = _nc_get("C", build_attn_core)
    nc3 = _nc_get("P", build_attn_proj)
    res = _run(nc1, [attn_qkv_inputs(x_fm, c, W) for c in range(NCORES)])
    qf = np.empty((B, D, SEQ), BF)
    kf = np.empty((B, D, SEQ), BF)
    vf = np.empty((B, SEQ, D), BF)
    for c in range(NCORES):
        b, t0 = core_pos(c)
        qf[b, :, t0:t0 + T] = res.results[c]["q_o"]
        kf[b, :, t0:t0 + T] = res.results[c]["k_o"]
        vo = res.results[c]["v_o"]
        for h in range(16):
            vf[b, t0:t0 + T, h * 64:(h + 1) * 64] = vo[h * T:(h + 1) * T]
    chif = np.empty((16, SEQ), BF)
    clof = np.empty((16, SEQ), BF)
    for c in range(4):
        b, t0 = core_pos(c)
        chif[:, t0:t0 + T] = res.results[c]["chi"]
        clof[:, t0:t0 + T] = res.results[c]["clo"]
    res2 = _run(nc2, [attn_core_inputs(qf, kf, vf, chif, clof, c, W)
                      for c in range(NCORES)])
    o_fm = np.empty((B, D, SEQ), BF)
    for c in range(NCORES):
        b = c // 4
        h0 = (c % 4) * 4
        o4 = res2.results[c]["o4"]
        for lc in range(4):
            for p_ in range(4):
                h = h0 + p_
                o_fm[b, h * 64:(h + 1) * 64, lc * 2048:(lc + 1) * 2048] = \
                    o4[lc * 256 + p_ * 64:lc * 256 + (p_ + 1) * 64, :]
    res3 = _run(nc3, [attn_proj_inputs(x_fm, o_fm, c, W) for c in range(NCORES)])
    return _assemble_fm([res3.results[c]["xout"] for c in range(NCORES)])


def kernel(**inputs):
    return kernel_fused(**inputs)


def kernel_multilaunch(**inputs):
    W = prep_weights(inputs)
    x_fm = _fm(inputs["L0_attn_x"])
    x_fm = _ffn_phase(x_fm, W)
    x_fm = _attn_phase(x_fm, W)
    x_fm = _ffn_phase(x_fm, W)
    ncF = _nc_get("F", build_l1ffn)
    ncL = _nc_get("L", build_l1attn)
    res = _run(ncF, [l1ffn_inputs(x_fm, c, W) for c in range(NCORES)])
    x_fm = _assemble_fm([res.results[c]["xout"] for c in range(NCORES)])
    res = _run(ncL, [l1attn_inputs(x_fm, c, W) for c in range(NCORES)])
    x_fm = _assemble_fm([res.results[c]["xout"] for c in range(NCORES)])
    res = _run(ncF, [l1ffn_inputs(x_fm, c, W) for c in range(NCORES)])
    x_fm = _assemble_fm([res.results[c]["xout"] for c in range(NCORES)])
    return np.ascontiguousarray(np.transpose(x_fm, (0, 2, 1)))



def build_attn_qkv(ctx=None, pfx=""):
    nc, r, sc, ext = _ctx_open(ctx, pfx)

    x_i = _inp(ctx, nc, "x", (D, T))
    wq_i = _inp(ctx, nc, "wqkv", (D, 3072))
    lx_i = _inp(ctx, nc, "logxi", (16, 1))
    eb_i = _inp(ctx, nc, "ebits", (16, 1))
    j16_i = _inp(ctx, nc, "j16", (16, T))
    aq1_i = _inp(ctx, nc, "aq1", (64, T))
    aq2_i = _inp(ctx, nc, "aq2", (64, T))
    ak1_i = _inp(ctx, nc, "ak1", (64, T))
    ak2_i = _inp(ctx, nc, "ak2", (64, T))
    q_o = _outp(ctx, nc, "q_o", (D, T), BF16)
    k_o = _outp(ctx, nc, "k_o", (D, T), BF16)
    v_o = _outp(ctx, nc, "v_o", (16 * T, 64), BF16)
    chi_o = _outp(ctx, nc, "chi", (16, T), BF16)
    clo_o = _outp(ctx, nc, "clo", (16, T), BF16)

    X = sc.sb("X", [128, 8 * T], F32)
    Z = sc.sb("Z", [128, 8 * T], BF16)
    WC = sc.sb("WC", [128, 8 * 512], BF16)
    STG = sc.sb("STG", [128, 512], BF16)
    AQ1 = sc.sb("AQ1", [64, T], BF16)
    AQ2 = sc.sb("AQ2", [64, T], BF16)
    AK1 = sc.sb("AK1", [64, T], BF16)
    AK2 = sc.sb("AK2", [64, T], BF16)
    LX = sc.sb("LX", [16, 1], F32)
    EB = sc.sb("EB", [16, 1], F32)
    J16 = sc.sb("J16", [16, T], F32)
    C32 = sc.sb("C32", [16, 2 * T], F32)
    CBF = sc.sb("CBF", [16, 2 * T], BF16)

    ht = np.arange(16) % 4
    PI = [(int(h) >> 1) & 1 for h in ht]

    kit = ctx["kit"] if ctx else LNKit(nc)
    tiles, caps = (ctx["tiles"], ctx["caps"]) if ctx else load_consts(r, nc, ["ones128", "ones1x128"])
    ps_mm = kit.ps_b0
    ps_n = kit.ps_s0

    r.dma("sync", X[:].rearrange("p (c w) -> p c w", w=T),
          x_i[:].rearrange("(c p) w -> p c w", p=128))
    r.dma("gpsimd", AQ1[:], aq1_i[:])
    r.dma("gpsimd", AQ2[:], aq2_i[:])
    r.dma("gpsimd", AK1[:], ak1_i[:])
    r.dma("gpsimd", AK2[:], ak2_i[:])
    r.dma("sync", LX[:], lx_i[:])
    r.dma("sync", EB[:], eb_i[:])
    r.dma("sync", J16[:], j16_i[:])

    rec_ln_stream(r, kit, tiles, X, T, T, out=Z, out_chnk=T)

    r.on("scalar", lambda s: s.activation(C32[0:16, 0:1], LX[:], AF.Exp,
                                          scale=-1.0), inc="act")
    r.on("vector", lambda v: v.tensor_tensor(C32[0:16, 1:2], C32[0:16, 0:1],
                                             EB[:], ALU.mult), inc="dve")

    def crow_fn(v):
        v.tensor_scalar_mul(C32[0:16, T:2 * T], J16[:], C32[0:16, 1:2])
        v.tensor_scalar_mul(CBF[0:16, 0:T], C32[0:16, T:2 * T], 1.0)
        v.tensor_scalar_mul(C32[0:16, 0:T], CBF[0:16, 0:T], 1.0)
        v.tensor_tensor(C32[0:16, 0:T], C32[0:16, T:2 * T], C32[0:16, 0:T],
                        ALU.subtract)
        return v.tensor_scalar_mul(CBF[0:16, T:2 * T], C32[0:16, 0:T], 1.0)
    r.on("vector", crow_fn, inc="dve")
    r.dma("sync", chi_o[:], CBF[0:16, 0:T], inc="st")
    r.dma("sync", clo_o[:], CBF[0:16, T:2 * T], inc="st")

    for role in range(2):
        base = role * 1024
        t1 = AQ1 if role == 0 else AK1
        t2 = AQ2 if role == 0 else AK2
        out_ap = q_o if role == 0 else k_o
        for oc in range(8):
            r.dma("gpsimd", WC[:, 0:8 * 128].rearrange("p (c w) -> p c w", w=128),
                  wq_i[:, base + oc * 128:base + (oc + 1) * 128]
                  .rearrange("(c p) w -> p c w", p=128))
            for it in range(4):
                j0 = it * 512

                def mm_fn(t, j0=j0):
                    for c in range(8):
                        li = t.matmul(ps_mm[:], WC[:, c * 128:(c + 1) * 128],
                                      Z[:, c * T + j0:c * T + j0 + 512],
                                      start=(c == 0), stop=(c == 7))
                    return li
                r.on("tensor", mm_fn, inc="pe")
                h0, h1 = 2 * oc, 2 * oc + 1
                scl = 0.125 if role == 0 else 1.0
                if PI[h0]:
                    rec_rope_plain(r, kit, ps_mm, t1, t2, j0, 512,
                                   (STG[0:64, :], STG[64:128, :]), True, scl)
                else:
                    rec_rope_plain(r, kit, ps_mm, t1, t2, j0, 512,
                                   (STG[0:64, :], STG[64:128, :]), False, scl)
                r.dma("sync", out_ap[oc * 128:(oc + 1) * 128, j0:j0 + 512],
                      STG[:], inc="st")


    for vh in range(2):
        r.dma("gpsimd", WC[:].rearrange("p (c w) -> p c w", w=512),
              wq_i[:, 2048 + vh * 512:2048 + (vh + 1) * 512]
              .rearrange("(c p) w -> p c w", p=128))
        for tc in range(16):
            def vmm_fn(t, tc=tc):
                for c in range(8):
                    li = t.matmul(ps_mm[:], Z[:, c * T + tc * 128:c * T + (tc + 1) * 128],
                                  WC[:, c * 512:(c + 1) * 512],
                                  start=(c == 0), stop=(c == 7))
                return li
            r.on("tensor", vmm_fn, inc="pe")
            r.on("scalar", lambda s: s.copy(STG[:], ps_mm[:]), inc="act")
            for hh in range(8):
                h = vh * 8 + hh
                r.dma("sync", v_o[h * T + tc * 128:h * T + (tc + 1) * 128, :],
                      STG[:, hh * 64:(hh + 1) * 64], inc="st")

    r.on("sync", lambda e: e.nop(), waits=r.snap())
    if not ext:
        sc.close()
    if ext:
        with nc.Block() as block:
            r.finalize(block)
    return nc


def attn_qkv_inputs(x_fm, c, W):
    cst = consts()
    b, t0 = core_pos(c)
    m = {
        "x": _slice_own(x_fm, c),
        "wqkv": W["att_wqkv_perm"],
        "logxi": np.asarray(W["L0_attn_log_xi"], np.float32).reshape(16, 1),
        "ebits": cst["ebits"],
        "j16": np.ascontiguousarray(cst["j16"][:, t0:t0 + T]),
        "aq1": np.ascontiguousarray(cst["att_t1q"][:, t0:t0 + T]),
        "aq2": np.ascontiguousarray(cst["att_t2q"][:, t0:t0 + T]),
        "ak1": np.ascontiguousarray(cst["att_t1"][:, t0:t0 + T]),
        "ak2": np.ascontiguousarray(cst["att_t2"][:, t0:t0 + T]),
    }
    m.update(const_in_map(["ones128", "ones1x128"]))
    return m



def build_attn_core(ctx=None, pfx="", fills=None):
    nc, r, sc, ext = _ctx_open(ctx, pfx)

    if fills is None:
        q4_i = _inp(ctx, nc, "q4", (256, SEQ), BF16)
        k4_i = _inp(ctx, nc, "k4", (256, SEQ), BF16)
        v4_i = _inp(ctx, nc, "v4", (4 * SEQ, 64), BF16)
        chi_i = _inp(ctx, nc, "chi4", (4, SEQ), BF16)
        clo_i = _inp(ctx, nc, "clo4", (4, SEQ), BF16)
    lxs_i = _inp(ctx, nc, "logxi4", (1, 4))
    ebs_i = _inp(ctx, nc, "ebits4", (1, 4))
    jr_i = _inp(ctx, nc, "jrow", (1, SEQ))
    o_o = _outp(ctx, nc, "o4", (1024, T), BF16)

    KT = sc.sb("KT", [68, SEQ], BF16)
    QT = sc.sb("QT", [68, SEQ], BF16)
    VT = sc.sb("VT", [128, 64 * 65], BF16)
    PTB = [sc.sb(f"PTs{i}", [128, 1024], BF16) for i in range(2)]
    PT = PTB[0]
    MC = sc.sb("MC", [128, 5 * 512], F32)
    JR = sc.sb("JR", [1, SEQ], F32)
    EXI = sc.sb("EXI", [1, 8], F32)
    SC = sc.sb("SCs", [64, 512], F32)
    ST = sc.sb("STs", [1, 4 * 512], F32)
    KMX = sc.sb("KMX", [1, 64], F32)
    OSB = sc.sb("OSB", [64, 1024], F32)
    RCP = sc.sb("RCPs", [1, 1024], F32)
    OST = sc.sb("OST", [64, 1024], BF16)

    ps_S = sc.ps("ps_S", [128, 1024], F32)
    ps_O = sc.ps("ps_O", [65, 1024], F32)

    cst = consts()

    tiles, caps = (ctx["tiles"], ctx["caps"]) if ctx else load_consts(r, nc, ["ones128", "ones1x128"])
    kit = ctx["kit"] if ctx else LNKit(nc)
    ps_n = kit.ps_s0
    ps_bc = kit.ps_b0
    ps_n2 = (kit.ps_b0, kit.ps_b1)
    kpre_ap = _inp(ctx, nc, "c_kpre", cst["kpre"].shape, F32)
    qpre_ap = _inp(ctx, nc, "c_qpre", cst["qpre"].shape, F32)
    maskC_ap = _inp(ctx, nc, "c_maskC", cst["maskC"].shape, F32)

    r.dma("sync", MC[:], maskC_ap[:])
    r.dma("sync", JR[:], jr_i[:])
    EXL = sc.sb("EXL", [1, 8], F32)
    r.dma("sync", EXL[0:1, 0:4], lxs_i[:])
    r.dma("sync", EXL[0:1, 4:8], ebs_i[:])
    r.on("scalar", lambda s: s.activation(EXI[0:1, 0:4], EXL[0:1, 0:4], AF.Exp,
                                          scale=-1.0), inc="act")
    r.on("vector", lambda v: v.tensor_tensor(EXI[0:1, 4:8], EXI[0:1, 0:4],
                                             EXL[0:1, 4:8], ALU.mult), inc="dve")
    r.on("gpsimd", lambda g: g.memset(
        VT[:].rearrange("p (b c) -> p b c", c=65)[:, :, 64:65], 1.0), inc="pool")

    for p_ in range(4):
        r.dma("gpsimd", KT[64:65, :], kpre_ap[0:1, :])
        r.dma("gpsimd", KT[67:68, :], kpre_ap[3:4, :])
        r.dma("gpsimd", QT[65:67, :], qpre_ap[0:2, :])
        r.dma("gpsimd", QT[67:68, :], qpre_ap[2:3, :])
        if fills is None:
            r.dma("sync", KT[0:64, :], k4_i[p_ * 64:(p_ + 1) * 64, :])
            r.dma("sync", KT[65:66, :], chi_i[p_:p_ + 1, :])
            r.dma("sync", KT[66:67, :], clo_i[p_:p_ + 1, :])
            r.dma("sync", QT[0:64, :], q4_i[p_ * 64:(p_ + 1) * 64, :])
            for kb in range(64):
                r.dma("sync", VT[:, kb * 65:kb * 65 + 64],
                      v4_i[p_ * SEQ + kb * 128:p_ * SEQ + (kb + 1) * 128, :])
        else:
            fills(p_, KT, QT, VT, JR, EXI)

        for it in range(16):
            r.on("scalar", lambda s, it=it: s.activation(
                SC[:, 0:512], KT[0:64, it * 512:(it + 1) * 512], AF.Square),
                inc="act")
            r.on("tensor", lambda t: t.matmul(
                ps_n[0:1, 0:512], tiles["ones128"][0:64, :], SC[:, 0:512],
                start=True, stop=True), inc="pe")
            r.on("vector", lambda v, it=it: v.reduce_max(
                KMX[0:1, it:it + 1], ps_n[0:1, 0:512], AX), inc="dve")
        r.on("vector", lambda v: v.reduce_max(KMX[0:1, 32:33], KMX[0:1, 0:16], AX),
             inc="dve")
        r.on("scalar", lambda s: s.activation(KMX[0:1, 33:34], KMX[0:1, 32:33],
                                              AF.Sqrt), inc="act")

        for it in range(16):
            r.on("scalar", lambda s, it=it: s.activation(
                SC[:, 0:512], QT[0:64, it * 512:(it + 1) * 512], AF.Square),
                inc="act")
            r.on("tensor", lambda t: t.matmul(
                ps_n[0:1, 0:512], tiles["ones128"][0:64, :], SC[:, 0:512],
                start=True, stop=True), inc="pe")
            r.on("scalar", lambda s: s.activation(ST[0:1, 0:512],
                                                  ps_n[0:1, 0:512], AF.Sqrt),
                 inc="act")

            def brow_fn(v, it=it, p_=p_):
                v.tensor_scalar_mul(ST[0:1, 512:1024], ST[0:1, 0:512],
                                    KMX[0:1, 33:34])
                v.scalar_tensor_tensor(ST[0:1, 1024:1536],
                                       JR[0:1, it * 512:(it + 1) * 512],
                                       EXI[0:1, 4 + p_:5 + p_],
                                       ST[0:1, 512:1024], ALU.mult, ALU.add)
                return v.tensor_scalar_mul(QT[64:65, it * 512:(it + 1) * 512],
                                           ST[0:1, 1024:1536], -1.0)
            r.on("vector", brow_fn, inc="dve")

        for qt2 in range(8):
            q0 = qt2 * 1024
            nb = 8 * qt2 + 8
            base = r.snap()
            exp_c = {}
            s_c = {}
            mask_c = {}
            sdsts = {}

            def rec_S(kb, q0=q0, qt2=qt2):
                odd = kb % 2
                if odd:
                    sdst = (ps_n2[0][:, 0:512], ps_n2[1][:, 0:512])
                else:
                    sdst = (ps_S[:, 0:512], ps_S[:, 512:1024])
                wv = dict(base)
                if kb >= 2:
                    wv[r.salt + "act"] = exp_c[kb - 2]

                def s_fn(t, kb=kb, q0=q0, sdst=sdst):
                    t.matmul(sdst[0], KT[0:68, kb * 128:(kb + 1) * 128],
                             QT[0:68, q0:q0 + 512], start=True, stop=True)
                    return t.matmul(sdst[1],
                                    KT[0:68, kb * 128:(kb + 1) * 128],
                                    QT[0:68, q0 + 512:q0 + 1024],
                                    start=True, stop=True)
                r.on("tensor", s_fn, inc="pe", waits=wv)
                s_c[kb] = r.v("pe")
                rel = kb - 8 * qt2
                if rel >= 0:
                    def msk_fn(v, rel=rel, sdst=sdst):
                        li = None
                        if rel < 4:
                            li = v.tensor_tensor(sdst[0], sdst[0],
                                                 MC[:, rel * 512:(rel + 1) * 512],
                                                 ALU.add)
                        else:
                            v.tensor_tensor(sdst[0], sdst[0],
                                            MC[:, 4 * 512:5 * 512], ALU.add)
                            li = v.tensor_tensor(sdst[1], sdst[1],
                                                 MC[:, (rel - 4) * 512:(rel - 3) * 512],
                                                 ALU.add)
                        return li
                    wvm = dict(base)
                    wvm[r.salt + "pe"] = s_c[kb]
                    r.on("vector", msk_fn, inc="dve", waits=wvm)
                    mask_c[kb] = r.v("dve")
                sdsts[kb] = sdst

            rec_S(0)
            if nb > 1:
                rec_S(1)
            for kb in range(nb):
                odd = kb % 2
                sdst = sdsts.pop(kb)
                pt = PTB[odd]
                wv = dict(base)
                wv[r.salt + "pe"] = s_c[kb]
                if kb in mask_c:
                    wv[r.salt + "dve"] = mask_c[kb]

                def exp_fn(s, sdst=sdst, pt=pt, odd=odd):
                    if odd:
                        s.activation(pt[:, 0:512], sdst[0], AF.Exp)
                        return s.activation(pt[:, 512:1024], sdst[1], AF.Exp)
                    return s.activation(pt[:], ps_S[:], AF.Exp)
                r.on("scalar", exp_fn, inc="act", waits=wv)
                exp_c[kb] = r.v("act")

                def pv_fn(t, kb=kb, pt=pt, first=(kb == 0), last=(kb == nb - 1)):
                    t.matmul(ps_O[:, 0:512], VT[:, kb * 65:kb * 65 + 65],
                             pt[:, 0:512], start=first, stop=last)
                    return t.matmul(ps_O[:, 512:1024], VT[:, kb * 65:kb * 65 + 65],
                                    pt[:, 512:1024], start=first, stop=last)
                wvp = dict(base)
                wvp[r.salt + "act"] = exp_c[kb]
                r.on("tensor", pv_fn, inc="pe", waits=wvp)
                if kb + 2 < nb:
                    rec_S(kb + 2)

            r.on("vector", lambda v: v.reciprocal(RCP[0:1, :], ps_O[64:65, :]),
                 inc="dve")
            r.on("scalar", lambda s: s.copy(OSB[:], ps_O[0:64, :]), inc="act")
            for hf in range(2):
                r.on("tensor", lambda t, hf=hf: t.matmul(
                    ps_bc[0:64, 0:512], tiles["ones1x128"][0:1, 0:64],
                    RCP[0:1, hf * 512:(hf + 1) * 512], start=True, stop=True), inc="pe")
                r.on("vector", lambda v, hf=hf: v.tensor_tensor(
                    OST[:, hf * 512:(hf + 1) * 512], OSB[:, hf * 512:(hf + 1) * 512],
                    ps_bc[0:64, 0:512], ALU.mult), inc="dve")
            lc, cs = qt2 // 2, qt2 % 2
            r.dma("sync", o_o[lc * 256 + p_ * 64:lc * 256 + (p_ + 1) * 64,
                              cs * 1024:(cs + 1) * 1024], OST[:], inc="st")

    r.on("sync", lambda e: e.nop(), waits=r.snap())
    if not ext:
        sc.close()
    if ext:
        with nc.Block() as block:
            r.finalize(block)
    return nc


def attn_core_inputs(qf, kf, vf, chif, clof, c, W):
    cst = consts()
    b = c // 4
    h0 = (c % 4) * 4
    hs = list(range(h0, h0 + 4))
    q4 = np.concatenate([qf[b, h * 64:(h + 1) * 64, :] for h in hs], 0)
    k4 = np.concatenate([kf[b, h * 64:(h + 1) * 64, :] for h in hs], 0)
    v4 = np.concatenate([vf[b, :, h * 64:(h + 1) * 64] for h in hs], 0)
    ht = np.arange(16) % 4
    m = {
        "q4": np.ascontiguousarray(q4),
        "k4": np.ascontiguousarray(k4),
        "v4": np.ascontiguousarray(v4),
        "chi4": np.ascontiguousarray(chif[hs]),
        "clo4": np.ascontiguousarray(clof[hs]),
        "logxi4": np.asarray(W["L0_attn_log_xi"], np.float32).reshape(1, 16)[:, hs],
        "ebits4": np.ascontiguousarray(cst["ebits"].reshape(1, 16)[:, hs]),
        "jrow": cst["jrow"],
    }
    m.update(const_in_map(["ones128", "ones1x128", "kpre", "qpre", "maskC"]))
    return m



def build_attn_proj(ctx=None, pfx="", o_fill=None):
    nc, r, sc, ext = _ctx_open(ctx, pfx)

    x_i = _inp(ctx, nc, "x", (D, T))
    o_i = _inp(ctx, nc, "o", (D, T), BF16) if o_fill is None else None
    wo_i = _inp(ctx, nc, "wo", (D, D))
    xo = _outp(ctx, nc, "xout", (D, T))

    X = sc.sb("X", [128, 8 * T], F32)
    O = sc.sb("O", [128, 8 * T], BF16)
    WC = sc.sb("WC", [128, 8 * 128], BF16)
    ps = sc.ps("ps", [128, 512], F32)

    r.dma("sync", X[:].rearrange("p (c w) -> p c w", w=T),
          x_i[:].rearrange("(c p) w -> p c w", p=128))
    if o_fill is None:
        r.dma("sync", O[:].rearrange("p (c w) -> p c w", w=T),
              o_i[:].rearrange("(c p) w -> p c w", p=128))
    else:
        o_fill(O)
    for f in range(8):
        r.dma("gpsimd", WC[:].rearrange("p (c w) -> p c w", w=128),
              wo_i[:, f * 128:(f + 1) * 128].rearrange("(c p) w -> p c w", p=128))
        for it in range(4):
            j0 = it * 512

            def mm_fn(t, f=f, j0=j0):
                for c in range(8):
                    li = t.matmul(ps[:], WC[:, c * 128:(c + 1) * 128],
                                  O[:, c * T + j0:c * T + j0 + 512],
                                  start=(c == 0), stop=(c == 7))
                return li
            r.on("tensor", mm_fn, inc="pe")
            r.on("vector", lambda v, f=f, j0=j0: v.tensor_tensor(
                X[:, f * T + j0:f * T + j0 + 512],
                X[:, f * T + j0:f * T + j0 + 512], ps[:], ALU.add), inc="dve")

    r.dma("sync", xo[:].rearrange("(c p) w -> p c w", p=128),
          X[:].rearrange("p (c w) -> p c w", w=T), inc="st")
    r.on("sync", lambda e: e.nop(), waits=r.snap())
    if not ext:
        sc.close()
    if ext:
        with nc.Block() as block:
            r.finalize(block)
    return nc


def attn_proj_inputs(x_fm, o_fm, c, W):
    return {
        "x": _slice_own(x_fm, c),
        "o": _slice_own_bf(o_fm, c),
        "wo": W["L0_attn_w_o"],
    }


def _slice_own_bf(x_fm, c):
    b, t0 = core_pos(c)
    return np.ascontiguousarray(x_fm[b, :, t0:t0 + T])



def build_l1ffn(ctx=None, pfx=""):
    nc, r, sc, ext = _ctx_open(ctx, pfx)

    x_i = _inp(ctx, nc, "x", (D, T))
    w1_i = _inp(ctx, nc, "w1", (D, 4096))
    w2_i = _inp(ctx, nc, "w2", (D, 4096))
    h1_i = _inp(ctx, nc, "h1", (4096, 4096))
    h2_i = _inp(ctx, nc, "h2", (4096, 4096))
    w3_i = _inp(ctx, nc, "w3", (4096, D))
    s1_i = _inp(ctx, nc, "s1", (128, 32))
    s2_i = _inp(ctx, nc, "s2", (128, 32))
    sh1_i = _inp(ctx, nc, "sh1", (128, 32))
    sh2_i = _inp(ctx, nc, "sh2", (128, 32))
    s3_i = _inp(ctx, nc, "s3", (128, 8))
    xo = _outp(ctx, nc, "xout", (D, T))

    X = sc.sb("X", [128, 8 * T], F32)
    Z = sc.sb("Z", [128, 8 * 512], BF16)
    HA = sc.sb("HA", [128, 32 * 512], BF16)
    HBB = sc.sb("HB2", [128, 32 * 512], BF16)
    WCA = sc.sb("WCA", [128, 8 * 128], BF16)
    WCB = sc.sb("WCB", [128, 8 * 128], BF16)
    WH = sc.sb("WH", [128, 32 * 128], BF16)
    SA = sc.sb("SA", [128, 512], BF16)
    S1 = sc.sb("S1", [128, 32], F32)
    S2 = sc.sb("S2", [128, 32], F32)
    SH1 = sc.sb("SH1", [128, 32], F32)
    SH2 = sc.sb("SH2", [128, 32], F32)
    S3H = sc.sb("S3H", [128, 8], F32)

    ps_a = sc.ps("ps_a", [128, 512], F32)
    ps_g = sc.ps("ps_g", [128, 512], F32)

    kit = ctx["kit"] if ctx else LNKit(nc)
    tiles, caps = (ctx["tiles"], ctx["caps"]) if ctx else load_consts(r, nc, ["ones128", "ones1x128"])

    r.dma("sync", X[:].rearrange("p (c w) -> p c w", w=T),
          x_i[:].rearrange("(c p) w -> p c w", p=128))
    r.dma("sync", S1[:], s1_i[:])
    r.dma("sync", S2[:], s2_i[:])
    r.dma("sync", SH1[:], sh1_i[:])
    r.dma("sync", SH2[:], sh2_i[:])
    r.dma("sync", S3H[:], s3_i[:])
    r.on("vector", lambda v: v.tensor_scalar_mul(S3H[:], S3H[:], 0.5), inc="dve")

    for tt_ in range(4):
        tb = tt_ * 512

        def sl_in(c, j0, w, tb=tb):
            return X[:, c * T + tb + j0: c * T + tb + j0 + w]

        def sl_out(c, j0, w):
            return Z[:, c * 512 + j0: c * 512 + j0 + w]
        rec_ln(r, kit, tiles, sl_in, sl_out, 512)

        for oc in range(32):
            r.dma("gpsimd", WCA[:].rearrange("p (c w) -> p c w", w=128),
                  w1_i[:, oc * 128:(oc + 1) * 128].rearrange("(c p) w -> p c w", p=128))
            r.dma("gpsimd", WCB[:].rearrange("p (c w) -> p c w", w=128),
                  w2_i[:, oc * 128:(oc + 1) * 128].rearrange("(c p) w -> p c w", p=128))

            def mm1_fn(t):
                for c in range(8):
                    t.matmul(ps_a[:], WCA[:, c * 128:(c + 1) * 128],
                             Z[:, c * 512:(c + 1) * 512], start=(c == 0), stop=(c == 7))
                for c in range(8):
                    li = t.matmul(ps_g[:], WCB[:, c * 128:(c + 1) * 128],
                                  Z[:, c * 512:(c + 1) * 512], start=(c == 0), stop=(c == 7))
                return li
            r.on("tensor", mm1_fn, inc="pe")
            r.on("scalar", lambda s, oc=oc: s.activation(
                SA[:], ps_a[:], AF.Silu, scale=S1[:, oc:oc + 1]), inc="act")
            r.on("vector", lambda v, oc=oc: v.scalar_tensor_tensor(
                HA[:, oc * 512:(oc + 1) * 512], ps_g[:], S2[:, oc:oc + 1],
                SA[:], ALU.mult, ALU.mult), inc="dve")

        for oc in range(32):
            r.dma("gpsimd", WH[:].rearrange("p (c w) -> p c w", w=128),
                  h1_i[:, oc * 128:(oc + 1) * 128].rearrange("(c p) w -> p c w", p=128))

            def mm2_fn(t):
                for c in range(32):
                    li = t.matmul(ps_a[:], WH[:, c * 128:(c + 1) * 128],
                                  HA[:, c * 512:(c + 1) * 512],
                                  start=(c == 0), stop=(c == 31))
                return li
            r.on("tensor", mm2_fn, inc="pe")
            r.on("scalar", lambda s, oc=oc: s.activation(
                HBB[:, oc * 512:(oc + 1) * 512], ps_a[:], AF.Gelu,
                scale=SH1[:, oc:oc + 1]), inc="act")

        for oc in range(32):
            r.dma("gpsimd", WH[:].rearrange("p (c w) -> p c w", w=128),
                  h2_i[:, oc * 128:(oc + 1) * 128].rearrange("(c p) w -> p c w", p=128))

            def mm3_fn(t):
                for c in range(32):
                    li = t.matmul(ps_a[:], WH[:, c * 128:(c + 1) * 128],
                                  HBB[:, c * 512:(c + 1) * 512],
                                  start=(c == 0), stop=(c == 31))
                return li
            r.on("tensor", mm3_fn, inc="pe")
            r.on("scalar", lambda s, oc=oc: s.activation(
                HA[:, oc * 512:(oc + 1) * 512], ps_a[:], AF.Gelu,
                scale=SH2[:, oc:oc + 1]), inc="act")

        for f in range(8):
            r.dma("gpsimd", WH[:, 0:32 * 128].rearrange("p (c w) -> p c w", w=128),
                  w3_i[:, f * 128:(f + 1) * 128].rearrange("(c p) w -> p c w", p=128))

            def mm4_fn(t):
                for c in range(32):
                    li = t.matmul(ps_a[:], WH[:, c * 128:(c + 1) * 128],
                                  HA[:, c * 512:(c + 1) * 512],
                                  start=(c == 0), stop=(c == 31))
                return li
            r.on("tensor", mm4_fn, inc="pe")
            r.on("vector", lambda v, f=f, tb=tb: v.scalar_tensor_tensor(
                X[:, f * T + tb:f * T + tb + 512], ps_a[:], S3H[:, f:f + 1],
                X[:, f * T + tb:f * T + tb + 512], ALU.mult, ALU.add), inc="dve")

    r.dma("sync", xo[:].rearrange("(c p) w -> p c w", p=128),
          X[:].rearrange("p (c w) -> p c w", w=T), inc="st")
    r.on("sync", lambda e: e.nop(), waits=r.snap())
    if not ext:
        sc.close()
    if ext:
        with nc.Block() as block:
            r.finalize(block)
    return nc


def l1ffn_inputs(x_fm, c, W):
    m = {
        "x": _slice_own(x_fm, c),
        "w1": W["L1_ffn_w1_w_t"],
        "w2": W["L1_ffn_w2_w_t"],
        "h1": W["L1_ffn_h1_w_t"],
        "h2": W["L1_ffn_h2_w_t"],
        "w3": W["L1_ffn_w3_w_t"],
        "s1": _pp(W["L1_ffn_w1_s"], 32),
        "s2": _pp(W["L1_ffn_w2_s"], 32),
        "sh1": _pp(W["L1_ffn_h1_s"], 32),
        "sh2": _pp(W["L1_ffn_h2_s"], 32),
        "s3": _pp(W["L1_ffn_w3_s"], 8),
    }
    m.update(const_in_map(["ones128", "ones1x128"]))
    return m



def build_l1attn(ctx=None, pfx=""):
    nc, r, sc, ext = _ctx_open(ctx, pfx)

    x_i = _inp(ctx, nc, "x", (D, T))
    ln1g_i = _inp(ctx, nc, "ln1g", (128, 8))
    ln1b_i = _inp(ctx, nc, "ln1b", (128, 8))
    ln2g_i = _inp(ctx, nc, "ln2g", (128, 8))
    ln2b_i = _inp(ctx, nc, "ln2b", (128, 8))
    wq_i = _inp(ctx, nc, "wq", (D, D))
    wk_i = _inp(ctx, nc, "wk", (D, D))
    wv_i = _inp(ctx, nc, "wv", (D, D))
    wo_i = _inp(ctx, nc, "wo", (D, D))
    qb_i = _inp(ctx, nc, "qb", (1, D))
    kb_i = _inp(ctx, nc, "kb", (1, D))
    vb_i = _inp(ctx, nc, "vb", (1, D))
    ob_i = _inp(ctx, nc, "ob", (128, 8))
    fc_i = _inp(ctx, nc, "fc", (D, 4096))
    fcb_i = _inp(ctx, nc, "fcb", (128, 32))
    pj_i = _inp(ctx, nc, "pj", (4096, D))
    pjb_i = _inp(ctx, nc, "pjb", (128, 8))
    id_i = _inp(ctx, nc, "c_ident", (128, 128))
    xo = _outp(ctx, nc, "xout", (D, T))

    X = sc.sb("X", [128, 8 * T], F32)
    ZARENA = sc.sb("ZARENA", [128, 20 * 1024], BF16)
    Z = ZARENA[:, 0:8 * T]
    ZT = ZARENA[:, 0:8 * 512]
    GT = ZARENA[:, 4096:4096 + 32 * 512]
    WAR = sc.sb("WAR", [128, 24 * 1024], BF16)
    OW = sc.sb("OW", [128, 8 * 128], BF16)
    LN1G = sc.sb("LN1G", [128, 8], F32)
    LN1B = sc.sb("LN1B", [128, 8], F32)
    LN2G = sc.sb("LN2G", [128, 8], F32)
    LN2B = sc.sb("LN2B", [128, 8], F32)
    OBP = sc.sb("OBP", [128, 8], F32)
    FCB = sc.sb("FCB", [128, 32], F32)
    PJB = sc.sb("PJB", [128, 8], F32)
    BB = sc.sb("BB", [128, 3 * D], F32)
    QTok = sc.sb("QTok", [128, D], BF16)
    KTok = sc.sb("KTok", [128, D], BF16)
    VTok = sc.sb("VTok", [128, D], BF16)
    PRD = sc.sb("PRD", [128, D], F32)
    AA = sc.sb("AA", [128, 256], F32)
    DN = sc.sb("DN", [128, 32], F32)
    OACC = sc.sb("OACC", [128, D], F32)
    ATok = sc.sb("ATok", [128, D], BF16)
    AFM = sc.sb("AFM", [128, D], BF16)
    IDB = sc.sb("IDB", [128, 128], BF16)

    ps_a = sc.ps("ps_a", [128, 512], F32)
    ps_tr = sc.ps("ps_tr", [128, 128], BF16)

    kit = ctx["kit"] if ctx else LNKit(nc)
    tiles, caps = (ctx["tiles"], ctx["caps"]) if ctx else load_consts(r, nc, ["ones128", "ones1x128"])

    r.dma("sync", X[:].rearrange("p (c w) -> p c w", w=T),
          x_i[:].rearrange("(c p) w -> p c w", p=128))
    for tin, tsb in ((ln1g_i, LN1G), (ln1b_i, LN1B), (ln2g_i, LN2G),
                     (ln2b_i, LN2B), (ob_i, OBP), (fcb_i, FCB), (pjb_i, PJB)):
        r.dma("sync", tsb[:], tin[:])
    r.dma("sync", BB[:, 0:D], qb_i[:].broadcast_to([128, D]))
    r.dma("sync", BB[:, D:2 * D], kb_i[:].broadcast_to([128, D]))
    r.dma("sync", BB[:, 2 * D:3 * D], vb_i[:].broadcast_to([128, D]))
    r.dma("gpsimd", IDB[:], id_i[:])
    SCQ = kit.sq[:, 0:512]
    SCR = kit.sq[:, 512:1024]
    SCM = kit.sq[:, 1024:1536]

    rec_ln_stream(r, kit, tiles, X, T, T)

    def zaff_fn(v):
        for c in range(8):
            li = v.tensor_scalar(Z[:, c * T:(c + 1) * T], X[:, c * T:(c + 1) * T],
                                 LN1G[:, c:c + 1], LN1B[:, c:c + 1],
                                 ALU.mult, ALU.add)
        return li
    r.on("vector", zaff_fn, inc="dve")

    for wi, base in ((wq_i, 0), (wk_i, 8), (wv_i, 16)):
        r.dma("gpsimd", WAR[:, base * 1024:(base + 8) * 1024]
              .rearrange("p (c w) -> p c w", w=1024),
              wi[:].rearrange("(c p) w -> p c w", p=128))

    for tc in range(16):
        tb = tc * 128
        for pi_, (wbase, bseg, dst, do_elu) in enumerate(
                ((0, 0, QTok, True), (8, 1, KTok, True), (16, 2, VTok, False))):
            for half in range(2):
                def mm_fn(t, wbase=wbase, half=half, tb=tb):
                    for c in range(8):
                        li = t.matmul(ps_a[:],
                                      Z[:, c * T + tb:c * T + tb + 128],
                                      WAR[:, (wbase + c) * 1024 + half * 512:
                                          (wbase + c) * 1024 + (half + 1) * 512],
                                      start=(c == 0), stop=(c == 7))
                    return li
                r.on("tensor", mm_fn, inc="pe")

                def bias_fn(v, bseg=bseg, half=half):
                    return v.tensor_tensor(
                        SCQ[:], ps_a[:],
                        BB[:, bseg * 1024 + half * 512:bseg * 1024 + (half + 1) * 512],
                        ALU.add)
                r.on("vector", bias_fn, inc="dve")
                if do_elu:
                    def elu1_fn(v):
                        v.tensor_scalar_max(SCR[:], SCQ[:], 0.0)
                        return v.tensor_tensor(SCM[:], SCQ[:], SCR[:], ALU.subtract)
                    r.on("vector", elu1_fn, inc="dve")
                    r.on("scalar", lambda s: s.activation(SCM[:], SCM[:], AF.Exp),
                         inc="act")
                    r.on("vector", lambda v, dst=dst, half=half: v.tensor_tensor(
                        dst[:, half * 512:(half + 1) * 512], SCR[:], SCM[:],
                        ALU.add), inc="dve")
                else:
                    r.on("vector", lambda v, dst=dst, half=half: v.tensor_scalar_mul(
                        dst[:, half * 512:(half + 1) * 512], SCQ[:], 1.0), inc="dve")

        Q3 = QTok[:].rearrange("p (h d) -> p h d", d=64)
        K3 = KTok[:].rearrange("p (h d) -> p h d", d=64)
        V3 = VTok[:].rearrange("p (h d) -> p h d", d=64)
        P3 = PRD[:].rearrange("p (h d) -> p h d", d=64)
        A3 = AA[:].rearrange("p (h g) -> p h g", g=16)
        for h in range(16):
            def amul_fn(v, h=h):
                v.tensor_tensor(P3, K3, Q3[:, h:h + 1, :].broadcast_to([128, 16, 64]),
                                ALU.mult)
                return v.tensor_reduce(A3[:, h:h + 1, :].transpose([0, 2, 1])
                                       if False else
                                       AA[:, h * 16:(h + 1) * 16]
                                       .rearrange("p (g o) -> p g o", o=1),
                                       P3, AX, ALU.add)
            r.on("vector", amul_fn, inc="dve")
        r.on("vector", lambda v: v.tensor_reduce(
            DN[:, 0:16].rearrange("p (h o) -> p h o", o=1),
            AA[:].rearrange("p (h g) -> p h g", g=16), AX, ALU.add), inc="dve")
        r.on("vector", lambda v: v.reciprocal(DN[:, 16:32], DN[:, 0:16]), inc="dve")
        for g in range(16):
            def oacc_fn(v, g=g):
                src = AA[:].rearrange("p (h g) -> p h g", g=16)[:, :, g:g + 1] \
                    .broadcast_to([128, 16, 64])
                vv = V3[:, g:g + 1, :].broadcast_to([128, 16, 64])
                o3 = OACC[:].rearrange("p (h d) -> p h d", d=64)
                if g == 0:
                    return v.tensor_tensor(o3, src, vv, ALU.mult)
                v.tensor_tensor(P3, src, vv, ALU.mult)
                return v.tensor_tensor(OACC[:], OACC[:], PRD[:], ALU.add)
            r.on("vector", oacc_fn, inc="dve")
        r.on("vector", lambda v: v.tensor_tensor(
            ATok[:].rearrange("p (h d) -> p h d", d=64),
            OACC[:].rearrange("p (h d) -> p h d", d=64),
            DN[:, 16:32].rearrange("p (h o) -> p h o", o=1)
            .broadcast_to([128, 16, 64]), ALU.mult), inc="dve")

        for c8 in range(8):
            r.on("tensor", lambda t, c8=c8: t.transpose(
                ps_tr[:], ATok[:, c8 * 128:(c8 + 1) * 128], IDB[:]), inc="pe")
            r.on("scalar", lambda s, c8=c8: s.copy(
                AFM[:, c8 * 128:(c8 + 1) * 128], ps_tr[:]), inc="act")
        for f in range(8):
            r.dma("gpsimd", OW[:].rearrange("p (c w) -> p c w", w=128),
                  wo_i[:, f * 128:(f + 1) * 128].rearrange("(c p) w -> p c w", p=128))

            def opj_fn(t, f=f):
                for c in range(8):
                    li = t.matmul(ps_a[0:128, 0:128],
                                  OW[:, c * 128:(c + 1) * 128],
                                  AFM[:, c * 128:(c + 1) * 128],
                                  start=(c == 0), stop=(c == 7))
                return li
            r.on("tensor", opj_fn, inc="pe")
            r.on("vector", lambda v, f=f, tb=tb: v.scalar_tensor_tensor(
                X[:, f * T + tb:f * T + tb + 128], ps_a[0:128, 0:128],
                OBP[:, f:f + 1], X[:, f * T + tb:f * T + tb + 128],
                ALU.add, ALU.add), inc="dve")

    for tt_ in range(4):
        tb = tt_ * 512

        def sl_in(c, j0, w, tb=tb):
            return X[:, c * T + tb + j0: c * T + tb + j0 + w]

        def sl_out(c, j0, w):
            return ZT[:, c * 512 + j0: c * 512 + j0 + w]
        rec_ln(r, kit, tiles, sl_in, sl_out, 512, gb=(LN2G, LN2B))

        for oc in range(32):
            r.dma("gpsimd", WAR[:, 0:8 * 128].rearrange("p (c w) -> p c w", w=128),
                  fc_i[:, oc * 128:(oc + 1) * 128].rearrange("(c p) w -> p c w", p=128))

            def fc_fn(t):
                for c in range(8):
                    li = t.matmul(ps_a[:], WAR[:, c * 128:(c + 1) * 128],
                                  ZT[:, c * 512:(c + 1) * 512],
                                  start=(c == 0), stop=(c == 7))
                return li
            r.on("tensor", fc_fn, inc="pe")
            r.on("scalar", lambda s, oc=oc: s.activation(
                GT[:, oc * 512:(oc + 1) * 512], ps_a[:], AF.Gelu,
                bias=FCB[:, oc:oc + 1]), inc="act")

        for f in range(8):
            r.dma("gpsimd", WAR[:, 0:32 * 128].rearrange("p (c w) -> p c w", w=128),
                  pj_i[:, f * 128:(f + 1) * 128].rearrange("(c p) w -> p c w", p=128))

            def pj_fn(t):
                for c in range(32):
                    li = t.matmul(ps_a[:], WAR[:, c * 128:(c + 1) * 128],
                                  GT[:, c * 512:(c + 1) * 512],
                                  start=(c == 0), stop=(c == 31))
                return li
            r.on("tensor", pj_fn, inc="pe")
            r.on("vector", lambda v, f=f, tb=tb: v.scalar_tensor_tensor(
                X[:, f * T + tb:f * T + tb + 512], ps_a[:], PJB[:, f:f + 1],
                X[:, f * T + tb:f * T + tb + 512], ALU.add, ALU.add), inc="dve")

    r.on("gpsimd", lambda g: g.dma_start(
        out=X[:].rearrange("p (c w) -> p c w", w=T),
        in_=x_i[:].rearrange("(c p) w -> p c w", p=128),
        accum_op=ALU.add), inc="wdma", inc_by=16)
    r.dma("sync", xo[:].rearrange("(c p) w -> p c w", p=128),
          X[:].rearrange("p (c w) -> p c w", w=T), inc="st")
    r.on("sync", lambda e: e.nop(), waits=r.snap())
    if not ext:
        sc.close()
    if ext:
        with nc.Block() as block:
            r.finalize(block)
    return nc


def l1attn_inputs(x_fm, c, W):
    m = {
        "x": _slice_own(x_fm, c),
        "ln1g": _pp(W["L1_attn_ln1_w"]),
        "ln1b": _pp(W["L1_attn_ln1_b"]),
        "ln2g": _pp(W["L1_attn_ln2_w"]),
        "ln2b": _pp(W["L1_attn_ln2_b"]),
        "wq": W["L1_attn_q_w_t"],
        "wk": W["L1_attn_k_w_t"],
        "wv": W["L1_attn_v_w_t"],
        "wo": W["L1_attn_o_w_t"],
        "qb": np.asarray(W["L1_attn_q_b"], np.float32).reshape(1, D),
        "kb": np.asarray(W["L1_attn_k_b"], np.float32).reshape(1, D),
        "vb": np.asarray(W["L1_attn_v_b"], np.float32).reshape(1, D),
        "ob": _pp(W["L1_attn_o_b"]),
        "fc": W["L1_attn_fc_w_t"],
        "fcb": _pp(W["L1_attn_fc_b"], 32),
        "pj": W["L1_attn_proj_w_t"],
        "pjb": _pp(W["L1_attn_proj_b"]),
        "c_ident": consts()["ident"],
    }
    m.update(const_in_map(["ones128", "ones1x128"]))
    return m



def build_fused(upto=99):
    nc = _new_nc()
    r = Rec(nc)
    cst = consts()
    kit = LNKit(nc)

    r.salt = "ld_"
    tiles, _unused = load_consts(r, nc, ["ones128", "ones1x128", "head2",
                                         "head2exp"])
    capd = {}
    for n_ in ["kpre", "qpre", "ident", "maskL", "maskC"]:
        capd["c_" + n_] = _din(nc, "c_" + n_, cst[n_].shape, F32)

    x_i = _din(nc, "x", (D, T))
    srch1_i = _din(nc, "srch1", (D, HALO))
    gate_i = _din(nc, "gate", (128, 1))
    ipv_i = _din(nc, "ipv", (128, 8), I32)
    qki_i = _din(nc, "qki", (64, 16), I32)
    vdi_i = _din(nc, "vdi", (128, 256), I32)
    odi_i = _din(nc, "odi", (128, 8), I32)
    xo = _dout(nc, "xout", (D, T))

    GATE = _sb(nc, "GATEf", [128, 1], F32)
    IPV = _sb(nc, "IPVf", [128, 8], I32)
    QKI = _sb(nc, "QKIf", [64, 16], I32)
    VDI = _sb(nc, "VDIf", [128, 256], I32)
    ODI = _sb(nc, "ODIf", [128, 8], I32)
    for tsb, tin in ((GATE, gate_i), (IPV, ipv_i), (QKI, qki_i), (VDI, vdi_i),
                     (ODI, odi_i)):
        r.dma("sync", tsb[:], tin[:])

    WA = {}
    for l in range(2):
        WA[f"wqkv{l}"] = _din(nc, f"wqkv{l}", (D, 1536))
        WA[f"wo{l}"] = _din(nc, f"wo{l}", (512, D))
        WA[f"qs{l}"] = _din(nc, f"qs{l}", (128, 1))
        WA[f"ks{l}"] = _din(nc, f"ks{l}", (128, 1))
        WA[f"lng{l}"] = _din(nc, f"lng{l}", (128, 8))
        WA[f"lnb{l}"] = _din(nc, f"lnb{l}", (128, 8))
        WA[f"w1{l}"] = _din(nc, f"w1{l}", (D, 5460))
        WA[f"b1a{l}"] = _din(nc, f"b1a{l}", (128, 22))
        WA[f"b1g{l}"] = _din(nc, f"b1g{l}", (128, 22))
        WA[f"w2{l}"] = _din(nc, f"w2{l}", (2730, D))
        WA[f"b2{l}"] = _din(nc, f"b2{l}", (128, 8))
    WA["awqkv"] = _din(nc, "awqkv", (D, 3072))
    WA["awo"] = _din(nc, "awo", (D, D))
    WA["logxi"] = _din(nc, "logxi", (16, 1))
    WA["ebits"] = _din(nc, "ebits", (16, 1))
    WA["j16"] = _din(nc, "j16", (16, T))
    WA["jrow"] = _din(nc, "jrow", (1, SEQ))
    WA["logxi4"] = _din(nc, "logxi4", (1, 4))
    WA["ebits4"] = _din(nc, "ebits4", (1, 4))
    for n_ in ["aq1", "aq2", "ak1", "ak2", "ffq1", "ffq2"]:
        WA[n_] = _din(nc, n_, (64, T))
    for n_ in ["ffka1", "ffka2", "ffkb1", "ffkb2"]:
        WA[n_] = _din(nc, n_, (64, 2560))
    WA["mW0"] = _din(nc, "mW0", (128, 4 * 512))
    for n_ in ["f_w1", "f_w2"]:
        WA[n_] = _din(nc, n_, (D, 4096))
    for n_ in ["f_h1", "f_h2"]:
        WA[n_] = _din(nc, n_, (4096, 4096))
    WA["f_w3"] = _din(nc, "f_w3", (4096, D))
    for n_ in ["f_s1", "f_s2", "f_sh1", "f_sh2"]:
        WA[n_] = _din(nc, n_, (128, 32))
    WA["f_s3"] = _din(nc, "f_s3", (128, 8))
    for n_ in ["ln1g", "ln1b", "ln2g", "ln2b", "ob", "pjb"]:
        WA["a_" + n_] = _din(nc, "a_" + n_, (128, 8))
    WA["a_fcb"] = _din(nc, "a_fcb", (128, 32))
    for n_ in ["wq", "wk", "wv", "wo"]:
        WA["a_" + n_] = _din(nc, "a_" + n_, (D, D))
    for n_ in ["qb", "kb", "vb"]:
        WA["a_" + n_] = _din(nc, "a_" + n_, (1, D))
    WA["a_fc"] = _din(nc, "a_fc", (D, 4096))
    WA["a_pj"] = _din(nc, "a_pj", (4096, D))

    xs = nc.dram_tensor("xs", [D, T], F32)
    ud = nc.dram_tensor("ud", [D, T], F32)
    ccA = {}
    for i_ in range(3):
        ccA[i_] = (nc.dram_tensor(f"ccA_in{i_}", [D, 513], F32),
                   nc.dram_tensor(f"ccA_out{i_}", [8 * D, 513], F32))
    ccB = {}
    for i_ in range(4):
        ccB[i_] = (nc.dram_tensor(f"ccB_in{i_}", [D, 1], F32),
                   nc.dram_tensor(f"ccB_out{i_}", [8 * D, 1], F32))
    ccQ_in = nc.dram_tensor("ccQ_in", [D, T], BF16)
    ccQ_out = nc.dram_tensor("ccQ_out", [8 * D, T], BF16)
    ccK_in = nc.dram_tensor("ccK_in", [D, T], BF16)
    ccK_out = nc.dram_tensor("ccK_out", [8 * D, T], BF16)
    ccV_in = nc.dram_tensor("ccV_in", [16 * T, 64], BF16)
    ccV_out = nc.dram_tensor("ccV_out", [8 * 16 * T, 64], BF16)
    ccC_in = nc.dram_tensor("ccC_in", [32, T], BF16)
    ccO_in = nc.dram_tensor("ccO_in", [1024, T], BF16)
    ccO_out = nc.dram_tensor("ccO_out", [8 * 1024, T], BF16)

    ctx = {"nc": nc, "r": r, "kit": kit, "tiles": tiles, "caps": capd,
           "aps": None}

    def ag(cin, cout):
        r.on("gpsimd", lambda g: g.collective_compute(
            "AllGather", ALU.bypass, replica_groups=[list(range(8))],
            ins=[cin.ap().opt()], outs=[cout.ap().opt()]), inc="pool")

    def tails513(stream_ap, buf):
        cin, cout = ccA[buf]
        r.dma("gpsimd", cin.ap(), stream_ap[:, T - 513:T])
        ag(cin, cout)

    def mk_halo_fill(buf):
        cout = ccA[buf][1]

        def halo_fill(UH):
            for c8 in range(8):
                r.on("gpsimd", lambda g, c8=c8, UH=UH: g.indirect_dma_start(
                    UH[:, c8 * 513:(c8 + 1) * 513], None, cout.ap(),
                    IndirectOffsetOnAxis(IPV[:, c8:c8 + 1], 0)),
                    inc="wdma", inc_by=16)
            r.on("vector", lambda v, UH=UH: v.tensor_scalar_mul(
                UH[:], UH[:], GATE[:, 0:1]), inc="dve")
        return halo_fill

    def tails1(stream_ap, buf):
        cin, cout = ccB[buf]
        r.dma("gpsimd", cin.ap(), stream_ap[:, T - 1:T], slow=True)
        ag(cin, cout)

    def mk_prev_fill(buf):
        cout = ccB[buf][1]

        def prev_fill(U):
            UW = T + 1
            for c8 in range(8):
                r.on("gpsimd", lambda g, c8=c8, U=U: g.indirect_dma_start(
                    U[:, c8 * UW:c8 * UW + 1], None, cout.ap(),
                    IndirectOffsetOnAxis(IPV[:, c8:c8 + 1], 0)),
                    inc="wdma", inc_by=16)
            r.on("vector", lambda v, U=U: v.tensor_scalar_mul(
                U[:].rearrange("p (c w) -> p c w", w=UW)[:, :, 0:1],
                U[:].rearrange("p (c w) -> p c w", w=UW)[:, :, 0:1],
                GATE[:, 0:1]), inc="dve")
        return prev_fill

    def l0_aps(l, src_ap, umid_ap):
        return {"src": src_ap, "wqkv": WA[f"wqkv{l}"], "wo": WA[f"wo{l}"],
                "qs": WA[f"qs{l}"], "ks": WA[f"ks{l}"], "mW0": WA["mW0"],
                "ffq1": WA["ffq1"], "ffq2": WA["ffq2"],
                "ffka1": WA["ffka1"], "ffka2": WA["ffka2"],
                "ffkb1": WA["ffkb1"], "ffkb2": WA["ffkb2"],
                "c_kpre": capd["c_kpre"], "c_ident": capd["c_ident"],
                "c_maskL": capd["c_maskL"], "umid": umid_ap}

    def b_aps(l, u_ap, out_key, out_ap, x_ap=None):
        m = {"u": u_ap, "lng": WA[f"lng{l}"], "lnb": WA[f"lnb{l}"],
             "w1": WA[f"w1{l}"], "b1a": WA[f"b1a{l}"], "b1g": WA[f"b1g{l}"],
             "w2": WA[f"w2{l}"], "b2": WA[f"b2{l}"], out_key: out_ap}
        if x_ap is not None:
            m["x"] = x_ap
        return m

    def ffn_phase(tag, x_ap, xout_ap, first_srch, bufbase):
        r.salt = tag + "a0_"
        ap0 = l0_aps(0, x_ap, ud.ap())
        if first_srch is not None:
            ap0["srch"] = first_srch
            ctx["aps"] = ap0
            build_ffnA(True, ctx=ctx, pfx=tag + "a0")
        else:
            tails513(x_ap, 2)
            ctx["aps"] = ap0
            build_ffnA(True, ctx=ctx, pfx=tag + "a0", halo_fill=mk_halo_fill(2))
        r.salt = tag + "b0_"
        tails1(ud.ap(), bufbase)
        ctx["aps"] = b_aps(0, ud.ap(), "uout", ud.ap())
        build_ffnB(False, ctx=ctx, pfx=tag + "b0", prev_fill=mk_prev_fill(bufbase))
        r.salt = tag + "a1_"
        tails513(ud.ap(), bufbase // 2)
        ctx["aps"] = l0_aps(1, ud.ap(), ud.ap())
        build_ffnA(False, ctx=ctx, pfx=tag + "a1", halo_fill=mk_halo_fill(bufbase // 2))
        r.salt = tag + "b1_"
        tails1(ud.ap(), bufbase + 1)
        ctx["aps"] = b_aps(1, ud.ap(), "xout", xout_ap, x_ap=x_ap)
        build_ffnB(True, ctx=ctx, pfx=tag + "b1", prev_fill=mk_prev_fill(bufbase + 1))

    def finish_early():
        r.salt = "zz_"
        r.dma("sync", xo[:], xs.ap()[:], inc="st")
        r.on("sync", lambda e: e.nop(), waits=r.snap())
        with nc.Block() as block:
            r.finalize(block)
        return nc

    ffn_phase("p1", x_i, xs.ap(), srch1_i, 0)
    if upto <= 1:
        return finish_early()

    r.salt = "aq_"
    ctx["aps"] = {"x": xs.ap(), "wqkv": WA["awqkv"], "logxi": WA["logxi"],
                  "ebits": WA["ebits"], "j16": WA["j16"],
                  "aq1": WA["aq1"], "aq2": WA["aq2"], "ak1": WA["ak1"],
                  "ak2": WA["ak2"], "q_o": ccQ_in.ap(), "k_o": ccK_in.ap(),
                  "v_o": ccV_in.ap(), "chi": ccC_in.ap()[0:16, :],
                  "clo": ccC_in.ap()[16:32, :]}
    build_attn_qkv(ctx=ctx, pfx="aq")
    if upto <= 12:
        return finish_early()
    ag(ccQ_in, ccQ_out)
    ag(ccK_in, ccK_out)
    ag(ccV_in, ccV_out)
    if upto <= 14:
        return finish_early()

    asc = Scope(nc, "at_")
    FCSa = asc.sb("FCSaf", [1, T], F32)
    FCSb = asc.sb("FCSbf", [1, T], F32)
    CSTa = asc.sb("CSTaf", [1, T], BF16)
    CSTb = asc.sb("CSTbf", [1, T], BF16)

    def core_fills(p_, KT, QT, VT, JRc, EXIc):
        for s in range(4):
            idx = QKI[0:64, p_ * 4 + s:p_ * 4 + s + 1]
            r.on("gpsimd", lambda g, s=s, idx=idx, KT=KT: g.indirect_dma_start(
                KT[0:64, s * T:(s + 1) * T], None, ccK_out.ap(),
                IndirectOffsetOnAxis(idx, 0)), inc="wdma", inc_by=16)
            r.on("gpsimd", lambda g, s=s, idx=idx, QT=QT: g.indirect_dma_start(
                QT[0:64, s * T:(s + 1) * T], None, ccQ_out.ap(),
                IndirectOffsetOnAxis(idx, 0)), inc="wdma", inc_by=16)
            def crow_fn(v, s=s, p_=p_, JRc=JRc, EXIc=EXIc):
                v.tensor_scalar_mul(FCSa[0:1, :], JRc[0:1, s * T:(s + 1) * T],
                                    EXIc[0:1, 4 + p_:5 + p_])
                v.tensor_scalar_mul(CSTa[0:1, :], FCSa[0:1, :], 1.0)
                v.tensor_scalar_mul(FCSb[0:1, :], CSTa[0:1, :], 1.0)
                v.tensor_tensor(FCSa[0:1, :], FCSa[0:1, :], FCSb[0:1, :],
                                ALU.subtract)
                return v.tensor_scalar_mul(CSTb[0:1, :], FCSa[0:1, :], 1.0)
            r.on("vector", crow_fn, inc="dve")
            r.dma("sync", KT[65:66, s * T:(s + 1) * T], CSTa[0:1, :])
            r.dma("sync", KT[66:67, s * T:(s + 1) * T], CSTb[0:1, :])
        for kb in range(64):
            r.on("gpsimd", lambda g, kb=kb, p_=p_, VT=VT: g.indirect_dma_start(
                VT[:, kb * 65:kb * 65 + 64], None, ccV_out.ap(),
                IndirectOffsetOnAxis(VDI[:, p_ * 64 + kb:p_ * 64 + kb + 1], 0)),
                inc="wdma", inc_by=16)

    r.salt = "ac_"
    ctx["aps"] = {"logxi4": WA["logxi4"], "ebits4": WA["ebits4"],
                  "jrow": WA["jrow"], "o4": ccO_in.ap(),
                  "c_kpre": capd["c_kpre"], "c_qpre": capd["c_qpre"],
                  "c_maskC": capd["c_maskC"]}
    build_attn_core(ctx=ctx, pfx="ac", fills=core_fills)
    if upto <= 16:
        return finish_early()
    ag(ccO_in, ccO_out)
    if upto == 17:
        dbg = _dout(nc, "dbg", (8 * 1024, T), BF16)
        r.dma("sync", dbg[:], ccO_out.ap()[:], inc="st")
        return finish_early()
    if upto == 18:
        dbg = _dout(nc, "dbg", (D, T), BF16)
        OD = _sb(nc, "ODbg", [128, 8 * T], BF16)
        o_fill_dbg = None
        def o_fill18(O):
            for j in range(8):
                idx = ODI[:, j:j + 1]
                r.on("gpsimd", lambda g, j=j, idx=idx, O=O: g.indirect_dma_start(
                    O[:, j * T:(j + 1) * T], None, ccO_out.ap(),
                    IndirectOffsetOnAxis(idx, 0)), inc="wdma", inc_by=16)
        o_fill18(OD)
        r.dma("sync", dbg[:].rearrange("(c p) w -> p c w", p=128),
              OD[:].rearrange("p (c w) -> p c w", w=T), inc="st")
        return finish_early()

    def o_fill(O):
        for j in range(8):
            idx = ODI[:, j:j + 1]
            r.on("gpsimd", lambda g, j=j, idx=idx, O=O: g.indirect_dma_start(
                O[:, j * T:(j + 1) * T], None, ccO_out.ap(),
                IndirectOffsetOnAxis(idx, 0)), inc="wdma", inc_by=16)

    r.salt = "ap_"
    od_dram = nc.dram_tensor("od_dram", [D, T], BF16)
    osc = Scope(nc, "og_")
    OG = osc.sb("OG", [128, 8 * T], BF16)
    o_fill(OG)
    r.dma("sync", od_dram.ap()[:].rearrange("(c p) w -> p c w", p=128),
          OG[:].rearrange("p (c w) -> p c w", w=T), inc="st")
    osc.close()
    if upto == 22:
        dsc = Scope(nc, "dj_")
        OD2 = dsc.sb("OD2", [128, 8 * T], BF16)
        WCD = dsc.sb("WCD", [128, 8 * 128], BF16)
        PSD = dsc.ps("PSD", [128, 512], F32)
        RES = dsc.sb("RES", [128, 512], F32)
        dbg = _dout(nc, "dbg", (128, 512))
        r.dma("sync", OD2[:].rearrange("p (c w) -> p c w", w=T),
              od_dram.ap()[:].rearrange("(c p) w -> p c w", p=128))
        r.dma("gpsimd", WCD[:].rearrange("p (c w) -> p c w", w=128),
              WA["awo"][:, 0:128].rearrange("(c p) w -> p c w", p=128))

        def dmm_fn(t):
            for c in range(8):
                li = t.matmul(PSD[:], WCD[:, c * 128:(c + 1) * 128],
                              OD2[:, c * T:c * T + 512], start=(c == 0), stop=(c == 7))
            return li
        r.on("tensor", dmm_fn, inc="pe")
        r.on("scalar", lambda s: s.copy(RES[:], PSD[:]), inc="act")
        r.dma("sync", dbg[:], RES[:], inc="st")
        r.salt = "zz_"
        r.on("sync", lambda e: e.nop(), waits=r.snap())
        with nc.Block() as block:
            r.finalize(block)
        return nc
    ctx["aps"] = {"x": xs.ap(), "wo": WA["awo"], "xout": xs.ap(), "o": od_dram.ap()}
    build_attn_proj(ctx=ctx, pfx="ap", o_fill=None)
    asc.close()
    if upto <= 20:
        return finish_early()

    ffn_phase("p3", xs.ap(), xs.ap(), None, 2)
    if upto <= 30:
        return finish_early()

    r.salt = "f1_"
    def f_aps(xout_ap):
        return {"x": xs.ap(), "w1": WA["f_w1"], "w2": WA["f_w2"],
                "h1": WA["f_h1"], "h2": WA["f_h2"], "w3": WA["f_w3"],
                "s1": WA["f_s1"], "s2": WA["f_s2"], "sh1": WA["f_sh1"],
                "sh2": WA["f_sh2"], "s3": WA["f_s3"], "xout": xout_ap}
    ctx["aps"] = f_aps(xs.ap())
    build_l1ffn(ctx=ctx, pfx="f1")
    if upto <= 40:
        return finish_early()

    r.salt = "la_"
    ctx["aps"] = {"x": xs.ap(), "ln1g": WA["a_ln1g"], "ln1b": WA["a_ln1b"],
                  "ln2g": WA["a_ln2g"], "ln2b": WA["a_ln2b"],
                  "wq": WA["a_wq"], "wk": WA["a_wk"], "wv": WA["a_wv"],
                  "wo": WA["a_wo"], "qb": WA["a_qb"], "kb": WA["a_kb"],
                  "vb": WA["a_vb"], "ob": WA["a_ob"], "fc": WA["a_fc"],
                  "fcb": WA["a_fcb"], "pj": WA["a_pj"], "pjb": WA["a_pjb"],
                  "c_ident": capd["c_ident"], "xout": xs.ap()}
    build_l1attn(ctx=ctx, pfx="la")
    if upto <= 50:
        return finish_early()

    r.salt = "f2_"
    ctx["aps"] = f_aps(xo)
    build_l1ffn(ctx=ctx, pfx="f2")

    r.salt = "zz_"
    r.on("sync", lambda e: e.nop(), waits=r.snap())
    with nc.Block() as block:
        r.finalize(block)
    return nc


def fused_inputs(inputs, W, c):
    cst = consts()
    b, t0 = core_pos(c)
    x_fm = W["_x_fm"]
    m = {
        "x": _slice_own(x_fm, c),
        "srch1": _slice_halo(x_fm, c),
        "gate": np.full((128, 1), 0.0 if t0 == 0 else 1.0, np.float32),
        "ipv": np.tile((((c - 1) % 8) * 1024 + np.arange(128, dtype=np.int32))[:, None],
                       (1, 8)) + (np.arange(8, dtype=np.int32) * 128)[None, :],
        "xout": None,
    }
    del m["xout"]
    h0 = (c % 4) * 4
    qki = np.zeros((64, 16), np.int32)
    for p_ in range(4):
        h = h0 + p_
        for s in range(4):
            rk = b * 4 + s
            qki[:, p_ * 4 + s] = rk * 1024 + h * 64 + np.arange(64)
    m["qki"] = qki
    vdi = np.zeros((128, 256), np.int32)
    for p_ in range(4):
        h = h0 + p_
        for kb in range(64):
            gtok = kb * 128 + np.arange(128)
            rk = b * 4 + gtok // T
            vdi[:, p_ * 64 + kb] = rk * (16 * T) + h * T + (gtok % T)
    m["vdi"] = vdi
    odi = np.zeros((128, 8), np.int32)
    lc = c % 4
    for j in range(8):
        for half in range(2):
            h = 2 * j + half
            rk = b * 4 + h // 4
            p_ = h % 4
            odi[half * 64:(half + 1) * 64, j] = \
                rk * 1024 + lc * 256 + p_ * 64 + np.arange(64)
    m["odi"] = odi

    for l in range(2):
        m[f"wqkv{l}"] = np.ascontiguousarray(W["L0_ffn_wqkv"][l])
        m[f"wo{l}"] = np.ascontiguousarray(W["L0_ffn_wo"][l])
        m[f"qs{l}"] = np.tile(np.asarray(W["L0_ffn_q_scale"][l], np.float32).reshape(64, 1), (2, 1))
        m[f"ks{l}"] = np.tile(np.asarray(W["L0_ffn_k_scale"][l], np.float32).reshape(64, 1), (2, 1))
        m[f"lng{l}"] = _pp(W["L0_ffn_ln_g"][l])
        m[f"lnb{l}"] = _pp(W["L0_ffn_ln_b"][l])
        m[f"w1{l}"] = np.ascontiguousarray(W["L0_ffn_w1"][l])
        m[f"b1a{l}"] = _pp_pad(W["L0_ffn_b1"][l][:2730], 22, 2730)
        m[f"b1g{l}"] = _pp_pad(W["L0_ffn_b1"][l][2730:], 22, 2730)
        m[f"w2{l}"] = np.ascontiguousarray(W["L0_ffn_w2"][l])
        m[f"b2{l}"] = _pp(W["L0_ffn_b2"][l])
    m["awqkv"] = W["att_wqkv_perm"]
    m["awo"] = W["L0_attn_w_o"]
    m["logxi"] = np.asarray(W["L0_attn_log_xi"], np.float32).reshape(16, 1)
    m["ebits"] = cst["ebits"]
    m["j16"] = np.ascontiguousarray(cst["j16"][:, t0:t0 + T])
    m["jrow"] = cst["jrow"]
    m["logxi4"] = np.asarray(W["L0_attn_log_xi"], np.float32).reshape(1, 16)[:, h0:h0 + 4]
    m["ebits4"] = np.ascontiguousarray(cst["ebits"].reshape(1, 16)[:, h0:h0 + 4])
    m["aq1"] = np.ascontiguousarray(cst["att_t1q"][:, t0:t0 + T])
    m["aq2"] = np.ascontiguousarray(cst["att_t2q"][:, t0:t0 + T])
    m["ak1"] = np.ascontiguousarray(cst["att_t1"][:, t0:t0 + T])
    m["ak2"] = np.ascontiguousarray(cst["att_t2"][:, t0:t0 + T])
    m["ffq1"] = np.ascontiguousarray(cst["ffq_t1"][:, t0:t0 + T])
    m["ffq2"] = np.ascontiguousarray(cst["ffq_t2"][:, t0:t0 + T])

    def ksl(name):
        out = np.zeros((64, 2560), np.float32)
        lo = t0 - 512
        s = max(lo, 0)
        out[:, s - lo:] = cst[name][:, s:t0 + T]
        return out
    m["ffka1"] = ksl("ffka_t1")
    m["ffka2"] = ksl("ffka_t2")
    m["ffkb1"] = ksl("ffkb_t1")
    m["ffkb2"] = ksl("ffkb_t2")
    m["mW0"] = cst["maskL0"] if t0 == 0 else np.ascontiguousarray(cst["maskL"][:, 0:4 * 512])
    m["f_w1"] = W["L1_ffn_w1_w_t"]
    m["f_w2"] = W["L1_ffn_w2_w_t"]
    m["f_h1"] = W["L1_ffn_h1_w_t"]
    m["f_h2"] = W["L1_ffn_h2_w_t"]
    m["f_w3"] = W["L1_ffn_w3_w_t"]
    m["f_s1"] = _pp(W["L1_ffn_w1_s"], 32)
    m["f_s2"] = _pp(W["L1_ffn_w2_s"], 32)
    m["f_sh1"] = _pp(W["L1_ffn_h1_s"], 32)
    m["f_sh2"] = _pp(W["L1_ffn_h2_s"], 32)
    m["f_s3"] = _pp(W["L1_ffn_w3_s"], 8)
    m["a_ln1g"] = _pp(W["L1_attn_ln1_w"])
    m["a_ln1b"] = _pp(W["L1_attn_ln1_b"])
    m["a_ln2g"] = _pp(W["L1_attn_ln2_w"])
    m["a_ln2b"] = _pp(W["L1_attn_ln2_b"])
    m["a_wq"] = W["L1_attn_q_w_t"]
    m["a_wk"] = W["L1_attn_k_w_t"]
    m["a_wv"] = W["L1_attn_v_w_t"]
    m["a_wo"] = W["L1_attn_o_w_t"]
    m["a_qb"] = np.asarray(W["L1_attn_q_b"], np.float32).reshape(1, D)
    m["a_kb"] = np.asarray(W["L1_attn_k_b"], np.float32).reshape(1, D)
    m["a_vb"] = np.asarray(W["L1_attn_v_b"], np.float32).reshape(1, D)
    m["a_ob"] = _pp(W["L1_attn_o_b"])
    m["a_fc"] = W["L1_attn_fc_w_t"]
    m["a_fcb"] = _pp(W["L1_attn_fc_b"], 32)
    m["a_pj"] = W["L1_attn_proj_w_t"]
    m["a_pjb"] = _pp(W["L1_attn_proj_b"])
    m.update(const_in_map(["ones128", "ones1x128", "head2", "head2exp",
                           "kpre", "qpre", "ident", "maskL", "maskC"]))
    return m


def kernel_fused(**inputs):
    W = prep_weights(inputs)
    W["_x_fm"] = _fm(inputs["L0_attn_x"])
    nc = _nc_get("FUSED", build_fused)
    in_maps = [fused_inputs(inputs, W, c) for c in range(NCORES)]
    res = run_bass_kernel_spmd(nc, in_maps, core_ids=list(range(NCORES)))
    x_fm = _assemble_fm([res.results[c]["xout"] for c in range(NCORES)])
    return np.ascontiguousarray(np.transpose(x_fm, (0, 2, 1)))
